# version 13
# baseline (speedup 1.0000x reference)
import dataclasses
from contextlib import ExitStack
import numpy as np
import ml_dtypes
import concourse.bass as bass
import concourse.mybir as mybir
from concourse.bass_utils import run_bass_kernel_spmd

F32 = mybir.dt.float32
BF16 = mybir.dt.bfloat16
AX = mybir.AluOpType
AF = mybir.ActivationFunctionType
AL = mybir.AxisListType

B, T, D = 4, 4096, 2048
NTPB, TT = 8, 512
C, NCH, CPT = 128, 32, 4
NC8 = 8
E, HPB_R, HPB_S = 8, 11, 22
DPB, KPB, NB = 16, 16, 4
NGI = E * HPB_R + HPB_S
NEG = 10


def s0(ap, n):
    return dataclasses.replace(ap, ap=list(ap.ap) + [[0, n]])


def mid0(ap, n):
    a = list(ap.ap)
    return dataclasses.replace(ap, ap=[a[0], [0, n]] + a[1:])


def dup2(ap):
    a = list(ap.ap)
    return dataclasses.replace(ap, ap=[a[0], a[1], [0, 2], a[2]])


class Prog:
    ENGS = ("pe", "dve", "act", "pool", "sp")

    def __init__(self, nc, ctx):
        self.strict = True
        self.nc = nc
        self.q = {e: [] for e in self.ENGS}
        self.ctx = ctx
        self.sem, self.cnt = {}, {}
        for s in self.ENGS + ("cc",):
            self.sem[s] = ctx.enter_context(nc.semaphore("sem_" + s))
            self.cnt[s] = 0
        self.seen = {e: {} for e in self.ENGS}
        self.bufw, self.bufr = {}, {}

    def _need(self, eng, dep, waits, seen):
        sem, val = dep
        if sem == eng and (eng == "pe" or (eng != "act" and not self.strict)):
            return
        if seen.get(sem, 0) >= val:
            return
        waits.append((sem, val))
        seen[sem] = val

    def op(self, eng, fn, reads=(), writes=(), dma=None, cc=False, selfinc=False):
        waits, seen = [], self.seen[eng]
        for b in reads:
            w = self.bufw.get(b)
            if w:
                self._need(eng, w, waits, seen)
        for b in writes:
            w = self.bufw.get(b)
            if w:
                self._need(eng, w, waits, seen)
            for r in self.bufr.get(b, ()):
                self._need(eng, r, waits, seen)
        if dma:
            sname, step = dma, 16
            if sname not in self.sem:
                self.sem[sname] = self.ctx.enter_context(
                    self.nc.semaphore("sem_" + sname))
                self.cnt[sname] = 0
        elif cc:
            sname, step = "cc", 1
        else:
            sname, step = eng, 1
        self.cnt[sname] += step
        val = self.cnt[sname]
        wl = [(self.sem[s], v) for s, v in waits]
        sem_obj = self.sem[sname]

        inc = 16 if dma else step
        if selfinc:
            def run(engine, fn=fn, wl=wl, sem_obj=sem_obj, inc=inc):
                for s, v in wl:
                    engine.wait_ge(s, v)
                fn(engine, sem_obj, inc)
        else:
            def run(engine, fn=fn, wl=wl, sem_obj=sem_obj, inc=inc):
                for s, v in wl:
                    engine.wait_ge(s, v)
                inst = fn(engine)
                if inst is not None:
                    inst.then_inc(sem_obj, inc)

        self.q[eng].append(run)
        for b in writes:
            self.bufw[b] = (sname, val)
            self.bufr[b] = []
        for b in reads:
            self.bufr.setdefault(b, []).append((sname, val))

    def barrier(self):
        snap = dict(self.cnt)
        for e in self.ENGS:
            todo = []
            for s, v in snap.items():
                if (s == e and not self.strict) or v == 0 or \
                        self.seen[e].get(s, 0) >= v:
                    continue
                todo.append((self.sem[s], v))
                self.seen[e][s] = v

            def run(engine, todo=todo):
                for s, v in todo:
                    engine.wait_ge(s, v)
            self.q[e].append(run)
        self.bufw, self.bufr = {}, {}

    def emit(self, blk):
        qq = self.q
        blk.tensor(lambda e: [f(e) for f in qq["pe"]] and None)
        blk.vector(lambda e: [f(e) for f in qq["dve"]] and None)
        blk.scalar(lambda e: [f(e) for f in qq["act"]] and None)
        blk.gpsimd(lambda e: [f(e) for f in qq["pool"]] and None)
        blk.sync(lambda e: [f(e) for f in qq["sp"]] and None)


def build_nc():
    nc = bass.Bass("TRN2", target_bir_lowering=False)
    dt = nc.dram_tensor
    xT = dt("xT", [D, B * T], F32, kind="ExternalInput").ap()
    wqkvz = dt("wqkvz", [12, KPB, 128, 128], F32, kind="ExternalInput").ap()
    wba = dt("wba", [KPB, 128, 8], F32, kind="ExternalInput").ap()
    convw = dt("convw", [8, 128, 4], F32, kind="ExternalInput").ap()
    dtb8 = dt("dtb8", [8], F32, kind="ExternalInput").ap()
    alog8 = dt("alog8", [8], F32, kind="ExternalInput").ap()
    nwin = dt("nwin", [128], F32, kind="ExternalInput").ap()
    wout = dt("wout", [DPB, 32, 128, 128], F32, kind="ExternalInput").ap()
    gatew = dt("gatew", [KPB, 128, 8], F32, kind="ExternalInput").ap()
    wg = dt("wg", [E, HPB_R, KPB, 128, 128], F32, kind="ExternalInput").ap()
    wu = dt("wu", [E, HPB_R, KPB, 128, 128], F32, kind="ExternalInput").ap()
    wdn = dt("wdn", [E * HPB_R, DPB, 128, 128], F32, kind="ExternalInput").ap()
    swg = dt("swg", [HPB_S, KPB, 128, 128], F32, kind="ExternalInput").ap()
    swu = dt("swu", [HPB_S, KPB, 128, 128], F32, kind="ExternalInput").ap()
    swd = dt("swd", [HPB_S, DPB, 128, 128], F32, kind="ExternalInput").ap()
    out_fm = dt("out_fm", [D, 2048], F32, kind="ExternalOutput").ap()
    myq = dt("myq", [1, 1], F32, kind="ExternalOutput").ap()
    ident_in = dt("ident_in", [128, 128], BF16, kind="ExternalInput").ap()
    masku_in = dt("masku_in", [128, 128], F32, kind="ExternalInput").ap()
    maskus_in = dt("maskus_in", [128, 128], F32, kind="ExternalInput").ap()
    pramp_in = dt("pramp_in", [128], F32, kind="ExternalInput").ap()
    og_my = [dt(f"og_my_{b}", [512, T], BF16) for b in range(B)]
    og_ag = [dt(f"og_ag_{b}", [NC8 * 512, T], BF16) for b in range(B)]
    wd_bf = dt("wd_bf", [NGI * DPB, 128, 128], BF16).ap()
    wo_bf = dt("wo_bf", [DPB, 32, 128, 128], BF16).ap()
    g_d = dt("g_d", [NGI, 128, 2048], BF16).ap()

    with ExitStack() as ctx:
        p = Prog(nc, ctx)
        sb = lambda nm, shape, dty=F32: ctx.enter_context(nc.sbuf_tensor(nm, shape, dty))
        psum = lambda nm, shape, dty=F32: ctx.enter_context(nc.psum_tensor(nm, shape, dty))

        identb = sb("identb", [128, 128], BF16)
        maskUi = sb("maskUi", [128, 128], F32)
        maskUs = sb("maskUs", [128, 128], F32)
        ones_cf = sb("ones_cf", [128, 1], F32)
        ones_cb = sb("ones_cb", [128, 1], BF16)
        ones_rf = sb("ones_rf", [1, 128], F32)
        pramp = sb("pramp", [128, 1], F32)
        nw_bc = sb("nw_bc", [128, 128], F32)
        row128 = sb("row128", [1, 128], F32)
        dtb_bc = sb("dtb_bc", [128, 8], F32)
        nea_bc = sb("nea_bc", [128, 8], F32)
        vec8 = sb("vec8", [1, 8], F32)
        gatew_b = sb("gatew_b", [128, KPB, 8], BF16)
        tiny = sb("tiny", [1, 1], F32)
        stat_m = sb("stat_m", [1, 512], F32)
        stat_q = sb("stat_q", [1, 512], F32)
        stat_v = sb("stat_v", [1, 512], F32)
        epsA = sb("epsA", [1, 1], F32)
        epsB = sb("epsB", [1, 1], F32)
        r_s = sb("r_s", [128, 512], F32)
        scr2 = sb("scr2", [128, 512], F32)

        P = [psum(f"PS{i}", [128, 512], F32) for i in range(7)]
        P6 = psum("PSB", [128, 1024], BF16)
        PV = [P[i][:].rearrange("p (a c) -> p a c", a=4) for i in range(7)]
        PK = ["PS0", "PS1", "PS2", "PS3", "PS4", "PS5", "PSX"]

        blk = ctx.enter_context(nc.Block())
        evt = [0]

        def ev():
            evt[0] ^= 1
            return "dve" if evt[0] else "act"

        def cop(eng, dst, src, rd, wr):
            if eng == "act":
                p.op("act", lambda e, d=dst, s=src: e.copy(d, s), reads=rd, writes=wr)
            else:
                p.op(eng, lambda e, d=dst, s=src: e.tensor_copy(d, s), reads=rd, writes=wr)

        p.op("sp", lambda e: e.dma_start(out=identb[:], in_=ident_in),
             writes=["masks"], dma="d_ini0")
        p.op("sp", lambda e: e.dma_start(out=maskUi[:], in_=masku_in),
             writes=["masks"], dma="d_ini1")
        p.op("sp", lambda e: e.dma_start(out=maskUs[:], in_=maskus_in),
             writes=["masks"], dma="d_ini2")
        p.op("sp", lambda e: e.dma_start(
            out=pramp[:, 0], in_=pramp_in),
            writes=["pramp"], dma="d_ini3")
        def f_ones(e):
            e.memset(ones_cf[:], 1.0)
            e.memset(ones_cb[:], 1.0)
            e.memset(epsA[:], 1e-6)
            e.memset(epsB[:], 1e-5)
            return e.memset(ones_rf[:], 1.0)
        p.op("dve", f_ones, writes=["masks"])

        p.op("sp", lambda e: e.dma_start(out=row128[0:1, :], in_=nwin.rearrange("(a b) -> a b", a=1)), writes=["row128"], dma="d_row")
        p.op("pe", lambda e: e.matmul(P[4][:, 0:128], ones_rf[:], row128[:], start=True, stop=True),
             reads=["masks", "row128"], writes=["PS4"])
        cop("dve", nw_bc[:], P[4][:, 0:128], ["PS4"], ["nwbc"])
        p.op("sp", lambda e: e.dma_start(out=vec8[0:1, :], in_=dtb8.rearrange("(a b) -> a b", a=1)), writes=["vec8"], dma="d_vec8")
        p.op("pe", lambda e: e.matmul(P[4][:, 0:8], ones_rf[:], vec8[:], start=True, stop=True),
             reads=["masks", "vec8", "nwbc"], writes=["PS4"])
        cop("dve", dtb_bc[:], P[4][:, 0:8], ["PS4"], ["dtbbc"])
        p.op("sp", lambda e: e.dma_start(out=vec8[0:1, :], in_=alog8.rearrange("(a b) -> a b", a=1)),
             reads=["dtbbc", "vec8"], writes=["vec8"], dma="d_vec8")
        p.op("act", lambda e: e.activation(vec8[0:1, :], vec8[0:1, :], AF.Exp),
             reads=["vec8"], writes=["vec8"])
        p.op("dve", lambda e: e.tensor_scalar(vec8[0:1, :], vec8[0:1, :], -1.0, None, AX.mult),
             reads=["vec8"], writes=["vec8"])
        p.op("pe", lambda e: e.matmul(P[4][:, 0:8], ones_rf[:], vec8[:], start=True, stop=True),
             reads=["vec8", "dtbbc"], writes=["PS4"])
        cop("dve", nea_bc[:], P[4][:, 0:8], ["PS4"], ["neabc"])

        def f_q(e, sem, inc):
            for arm in e.switch_core_id(n=128):
                e.memset(tiny[0:1, 0:1], float(int(arm) % 8)).then_inc(sem, inc)
        p.op("dve", f_q, writes=["tiny"], selfinc=True)
        p.op("sp", lambda e: e.dma_start(out=myq, in_=tiny[0:1, 0:1]),
             reads=["tiny"], writes=["myqd"], dma="d_out")

        with ExitStack() as p1:
            sb1 = lambda nm, shape, dty=F32: p1.enter_context(nc.sbuf_tensor(nm, shape, dty))
            wq_b = sb1("wq_b", [128, 12, KPB, 128], BF16)
            wba_b = sb1("wba_b", [128, KPB, 8], BF16)
            convw_s = sb1("convw_s", [128, 8, 4], F32)
            diag_b = sb1("diag_b", [128, 8, 4, 128], BF16)
            winit = sb1("winit", [128, KPB, 128], F32)
            xstg = sb1("xstg", [128, 2, TT], F32)
            x_bb = sb1("x_bb", [128, KPB, TT], BF16)
            raw = sb1("raw", [128, 8, TT + 4], BF16)
            qkc = sb1("qkc", [128, 4, TT], BF16)
            vc_b = sb1("vc_b", [128, 4, TT], BF16)
            z_b = sb1("z_b", [128, 4, TT], BF16)
            gbuf = sb1("gbuf", [128, CPT, 8], F32)
            bbuf = [sb1(f"bbuf{i}", [128, CPT, 8], F32) for i in range(2)]
            sp_t = sb1("sp_t", [128, 24], F32)
            W4 = sb1("W4", [128, CPT, 4, 128], BF16)
            At44 = sb1("At44", [128, CPT, 4, 128], BF16)
            Kt44 = sb1("Kt44", [128, CPT, 4, 128], BF16)
            V44 = sb1("V44", [128, CPT, 4, 128], BF16)
            z44 = sb1("z44", [128, CPT, 4, 128], BF16)
            kh4 = sb1("kh4", [128, CPT, 2, 128], BF16)
            qh4 = sb1("qh4", [128, CPT, 2, 128], BF16)
            lam44 = sb1("lam44", [128, CPT, 4], F32)
            lamn44 = sb1("lamn44", [128, CPT, 4], F32)
            dec44 = sb1("dec44", [128, CPT, 4], F32)
            LamB44 = sb1("LamB44", [128, CPT, 4], F32)
            bcol4 = sb1("bcol4", [128, CPT, 4], F32)
            brow4 = sb1("brow4", [1, CPT, 512], F32)
            TS = []
            for s_ in range(2):
                TS.append(dict(
                    Dt4=sb1(f"Dt4_{s_}", [128, 4, 128], BF16),
                    scr4=sb1(f"scr4_{s_}", [128, 4, 128], BF16),
                    sq4=sb1(f"sq4_{s_}", [128, 4, 128], F32),
                    Mt4=sb1(f"Mt4_{s_}", [128, 4, 128], BF16),
                    Tb=[sb1(f"Tb{s_}{j}", [128, 4, 128], BF16) for j in range(2)],
                    Nb=[sb1(f"Nb{s_}{j}", [128, 4, 128], BF16) for j in range(2)],
                    Vp=[sb1(f"Vp{s_}{j}", [128, 4, 128], BF16) for j in range(2)],
                    rn=sb1(f"rn_{s_}", [1, 512], F32),
                    bfm=sb1(f"bfm_{s_}", [4, 128], F32),
                ))
            Y0f = sb1("Y0f", [128, 4, 128], F32)
            Y0b = sb1("Y0b", [128, 4, 128], BF16)
            U4 = sb1("U4", [128, CPT, 4, 128], BF16)
            QS4 = sb1("QS4", [128, CPT, 4, 128], BF16)
            S_f = sb1("S_f", [128, 4, 128], F32)
            S_b = sb1("S_b", [128, 4, 128], BF16)
            o4 = [sb1(f"o4_{i}", [128, 4, 128], F32) for i in range(2)]
            scro = [sb1(f"scro_{i}", [128, 4, 128], BF16) for i in range(2)]
            ssum4 = [sb1(f"ssum4_{i}", [128, 4], F32) for i in range(2)]
            rstd4 = [sb1(f"rstd4_{i}", [128, 4], F32) for i in range(2)]
            ogb = [sb1(f"ogb_{i}", [128, 4, 128], BF16) for i in range(2)]
            ogt = [sb1(f"ogt_{i}", [128, 4, 128], BF16) for i in range(2)]

            p.op("sp", lambda e: e.dma_start(out=convw_s[:], in_=convw.rearrange("p d j -> d p j")),
                 writes=["convws"], dma="d_convw")
            for pb in range(8):
                for j in range(4):
                    p.op("dve", lambda e, pb=pb, j=j: e.tensor_scalar(
                        diag_b[:, pb, j, :], identb[:], convw_s[:, pb, j:j + 1], None, AX.mult),
                        reads=["masks", "convws"], writes=["diag"])
            for pb in range(12):
                p.op("sp", lambda e, pb=pb: e.dma_start(
                    out=winit[:], in_=wqkvz[pb].rearrange("k d c -> d k c")),
                    writes=["winit"], dma="d_winit")
                cop(ev(), wq_b[:, pb], winit[:], ["winit"], ["wqb"])
            p.op("sp", lambda e: e.dma_start(
                out=winit[:, :, 0:8], in_=wba.rearrange("k d c -> d k c")),
                writes=["winit"], dma="d_winit")
            cop("dve", wba_b[:], winit[:, :, 0:8], ["winit"], ["wbab"])
            p.op("sp", lambda e: e.dma_start(
                out=winit[:, :, 0:8], in_=gatew.rearrange("k d c -> d k c")),
                writes=["winit"], dma="d_winit")
            cop("dve", gatew_b[:], winit[:, :, 0:8], ["winit"], ["gatewb"])
            p.op("dve", lambda e: e.memset(raw[:, :, 0:4], 0.0),
                 writes=[f"raw{pb}" for pb in range(8)])

            wstbC = sb1("wstbC", [128, 16, 128], BF16)
            cast_q = []
            for cpb in range(DPB):
                for hf in range(2):
                    cast_q.append(lambda cpb=cpb, hf=hf: p.op(
                        "sp", lambda e: e.dma_start(
                            out=winit[:],
                            in_=wout[cpb, hf * 16:(hf + 1) * 16].rearrange("k d c -> d k c")),
                        writes=["winit"], dma="d_winit"))
                    cast_q.append(lambda: cop(ev(), wstbC[:], winit[:],
                                              ["winit"], ["wstbC"]))
                    cast_q.append(lambda cpb=cpb, hf=hf: p.op(
                        "pool", lambda e: e.dma_start(
                            out=wo_bf[cpb, hf * 16:(hf + 1) * 16].rearrange("k d c -> d k c"),
                            in_=wstbC[:]),
                        reads=["wstbC"], writes=[f"wobf{cpb}"], dma="d_wbf"))
            for gi in range(NGI):
                srcap = wdn[gi] if gi < E * HPB_R else swd[gi - E * HPB_R]
                eg, gl = gi // 11, gi % 11
                cast_q.append(lambda srcap=srcap: p.op(
                    "sp", lambda e: e.dma_start(
                        out=winit[:], in_=srcap.rearrange("k d c -> d k c")),
                    writes=["winit"], dma="d_winit"))
                cast_q.append(lambda: cop(ev(), wstbC[:], winit[:],
                                          ["winit"], ["wstbC"]))
                for hf in range(2):
                    r0 = (eg * 2 + hf) * 88 + gl * 8
                    cast_q.append(lambda r0=r0, hf=hf: p.op(
                        "pool", lambda e: e.dma_start(
                            out=wd_bf[r0:r0 + 8].rearrange("k d c -> d k c"),
                            in_=wstbC[:, hf * 8:(hf + 1) * 8, :]),
                        reads=["wstbC"], writes=[f"wdbf{r0}"], dma="d_wbf"))

            def drain_cast(n):
                for _ in range(min(n, len(cast_q))):
                    cast_q.pop(0)()

            def proj_ttile(b, tt):
                t0 = b * T + tt * TT
                pt = tt % 2
                pj = [0]

                def bank():
                    pj[0] ^= 1
                    return (P[5], "PS5") if pj[0] else (P[6], "PSX")
                for kk in range(KPB):
                    sl = kk % 2
                    p.op("sp", lambda e, kk=kk, sl=sl: e.dma_start(
                        out=xstg[:, sl, :], in_=xT[kk * 128:(kk + 1) * 128, t0:t0 + TT]),
                        writes=[f"xstg{sl}"], dma=f"d_x{sl}")
                    cop(ev(), x_bb[:, kk, :], xstg[:, sl, :], [f"xstg{sl}"], ["xbb"])
                for pb in range(12):
                    ps, pk = bank()
                    def fmm(e, pb=pb, ps=ps):
                        r = None
                        for k in range(KPB):
                            r = e.matmul(ps[:], wq_b[:, pb, k, :], x_bb[:, k, :],
                                         start=(k == 0), stop=(k == KPB - 1))
                        return r
                    p.op("pe", fmm, reads=["wqb", "xbb"], writes=[pk])
                    if pb < 8:
                        cop(ev(), raw[:, pb, 4:TT + 4], ps[:], [pk], [f"raw{pb}"])
                    else:
                        p.op("act", lambda e, pb=pb, ps=ps: e.activation(
                            z_b[:, pb - 8, :], ps[:], AF.Silu), reads=[pk], writes=["zb"])
                for ci in range(CPT):
                    ps, pk = bank()
                    def fba(e, ci=ci, ps=ps):
                        r = None
                        for k in range(KPB):
                            r = e.matmul(ps[:, 0:8], x_bb[:, k, ci * 128:(ci + 1) * 128],
                                         wba_b[:, k, :], start=(k == 0), stop=(k == KPB - 1))
                        return r
                    p.op("pe", fba, reads=["wbab", "xbb"], writes=[pk])
                    p.op("act", lambda e, ci=ci, ps=ps: e.activation(
                        bbuf[pt][:, ci, :], ps[:, 0:8], AF.Sigmoid),
                        reads=[pk], writes=[f"bbuf{pt}{ci}"])
                    p.op("dve", lambda e, ps=ps: e.tensor_tensor(
                        sp_t[:, 0:8], ps[:, 0:8], dtb_bc[:], AX.add),
                        reads=[pk, "dtbbc"], writes=["sp0"])
                    p.op("act", lambda e: e.activation(sp_t[:, 8:16], sp_t[:, 0:8], AF.Abs),
                         reads=["sp0"], writes=["sp1"])
                    p.op("act", lambda e: e.activation(sp_t[:, 16:24], sp_t[:, 8:16], AF.Exp, scale=-1.0),
                         reads=["sp1"], writes=["sp2"])
                    p.op("act", lambda e: e.activation(sp_t[:, 8:16], sp_t[:, 16:24], AF.Ln, bias=1.0),
                         reads=["sp2"], writes=["sp1"])
                    p.op("dve", lambda e: e.tensor_scalar_max(sp_t[:, 16:24], sp_t[:, 0:8], 0.0),
                         reads=["sp0"], writes=["sp2"])
                    p.op("dve", lambda e: e.tensor_tensor(sp_t[:, 0:8], sp_t[:, 8:16],
                                                          sp_t[:, 16:24], AX.add),
                         reads=["sp1", "sp2"], writes=["sp0"])
                    p.op("dve", lambda e, ci=ci: e.tensor_tensor(
                        gbuf[:, ci, :], sp_t[:, 0:8], nea_bc[:], AX.mult),
                        reads=["sp0", "neabc"], writes=[f"gbuf{ci}"])
                for pb in range(8):
                    rk = f"raw{pb}"
                    ps, pk = bank()
                    def fcv(e, pb=pb, ps=ps):
                        r = None
                        for j in range(4):
                            r = e.matmul(ps[:], diag_b[:, pb, j, :], raw[:, pb, 1 + j:1 + j + TT],
                                         start=(j == 0), stop=(j == 3))
                        return r
                    p.op("pe", fcv, reads=["diag", rk], writes=[pk])
                    if pb < 4:
                        p.op("act", lambda e, pb=pb, ps=ps: e.activation(
                            qkc[:, pb, :], ps[:], AF.Silu), reads=[pk], writes=["qkc"])
                    else:
                        p.op("act", lambda e, pb=pb, ps=ps: e.activation(
                            vc_b[:, pb - 4, :], ps[:], AF.Silu), reads=[pk], writes=["vcb"])
                    if tt < NTPB - 1:
                        p.op("dve", lambda e, pb=pb: e.tensor_copy(
                            raw[:, pb, 0:4], raw[:, pb, TT:TT + 4]), reads=[rk], writes=[rk])
                    else:
                        p.op("dve", lambda e, pb=pb: e.memset(raw[:, pb, 0:4], 0.0),
                             reads=[rk], writes=[rk])

            PV2, PV3, PV4 = PV[2], PV[3], PV[4]

            def prep_steps(b, tt, ci):
                st = []
                A = st.append
                sx = ci % 2
                S = TS[sx]
                pt = tt % 2
                cs = ci * 128
                kG, kB = f"gbuf{ci}", f"bbuf{pt}{ci}"
                kbc, kdec, klam, klamn, kLB = (f"bcol{ci}", f"dec{ci}", f"lam{ci}",
                                               f"lamn{ci}", f"LamB{ci}")
                kkh, kqh = f"kh{ci}", f"qh{ci}"
                kDt, kscr, ksq, kMt = f"Dt{sx}", f"scr{sx}", f"sq4{sx}", f"Mt{sx}"
                krn, kbfm = f"rn{sx}", f"bfm{sx}"
                G2 = dup2(P[3][:, 0:256].rearrange("p (a c) -> p a c", a=2))
                QK2 = dup2(P[4][:, 0:256].rearrange("p (a c) -> p a c", a=2))
                P6a2 = dup2(P6[:, 0:256].rearrange("p (a c) -> p a c", a=2))
                P6a4 = P6[:, 0:512].rearrange("p (a c) -> p a c", a=4)

                def u_bcol():
                    p.op("pe", lambda e: e.matmul(
                        P[2][:, 0:4], maskUi[:], gbuf[:, ci, 4:8], start=True, stop=True),
                        reads=[kG, "masks"], writes=["PS2"])
                    cop(ev(), bcol4[:, ci, :], P[2][:, 0:4], ["PS2"], [kbc])
                A(u_bcol)

                def u_bfm():
                    p.op("pe", lambda e: e.matmul(
                        P[3][0:4, 0:128], gbuf[:, ci, 4:8], maskUi[:], start=True, stop=True),
                        reads=[kG, "masks"], writes=["PS3"])
                    cop(ev(), S["bfm"][:], P[3][0:4, 0:128], ["PS3"], [kbfm])
                A(u_bfm)
                brow_dst = brow4[0:1, ci, :]
                brow_v = dataclasses.replace(
                    brow_dst, ap=[brow_dst.ap[0], [128, 4], [1, 128]])
                A(lambda: p.op("sp", lambda e: e.dma_start(out=brow_v, in_=S["bfm"][:]),
                               reads=[kbfm], writes=[f"brow{ci}"], dma=f"d_brow{sx}"))

                def u_fbb():
                    def fbb(e):
                        r = None
                        for vh in range(4):
                            r = e.matmul(PV2[:, vh, :], ones_rf[:],
                                         brow4[0:1, ci, vh * 128:(vh + 1) * 128],
                                         start=True, stop=True)
                        return r
                    p.op("pe", fbb, reads=[f"brow{ci}", "masks"], writes=["PS2"])
                    p.op("dve", lambda e: e.tensor_tensor(
                        dec44[:, ci, :], PV2[:, :, 127], bcol4[:, ci, :], AX.subtract),
                        reads=["PS2", kbc], writes=[kdec])
                    p.op("act", lambda e: e.activation(
                        LamB44[:, ci, :], PV2[:, :, 127], AF.Exp), reads=["PS2"], writes=[kLB])
                    p.op("dve", lambda e: e.tensor_tensor(
                        S["sq4"][:], PV2[:], s0(bcol4[:, ci, :], 128), AX.subtract),
                        reads=["PS2", kbc], writes=[ksq])
                A(u_fbb)

                def u_dec():
                    p.op("act", lambda e: e.activation(
                        dec44[:, ci, :], dec44[:, ci, :], AF.Exp), reads=[kdec], writes=[kdec])
                    p.op("act", lambda e: e.activation(
                        lam44[:, ci, :], bcol4[:, ci, :], AF.Exp), reads=[kbc], writes=[klam])
                    p.op("dve", lambda e: e.tensor_scalar(
                        lamn44[:, ci, :], lam44[:, ci, :], -1.0, None, AX.mult),
                        reads=[klam], writes=[klamn])
                    p.op("dve", lambda e: e.tensor_scalar_min(
                        S["sq4"][:], S["sq4"][:], 0.0), reads=[ksq], writes=[ksq])
                    p.op("act", lambda e: e.activation(
                        S["Dt4"][:], S["sq4"][:], AF.Exp), reads=[ksq], writes=[kDt])
                A(u_dec)

                def u_l2a():
                    p.op("act", lambda e: e.activation(
                        S["sq4"][:], qkc[:, :, cs:cs + 128], AF.Square),
                        reads=["qkc"], writes=[ksq])
                    p.op("pe", lambda e: e.matmul(
                        P[3][0:1, 0:512], ones_cf[:],
                        S["sq4"][:].rearrange("p a c -> p (a c)"), start=True, stop=True),
                        reads=[ksq, "masks"], writes=["PS3"])
                    p.op("act", lambda e: e.activation(
                        S["rn"][0:1, :], P[3][0:1, 0:512], AF.Sqrt, bias=epsA[0:1, 0:1]),
                        reads=["PS3", "masks"], writes=[krn])
                A(u_l2a)

                def u_l2b():
                    p.op("dve", lambda e: e.reciprocal(
                        S["rn"][0:1, :], S["rn"][0:1, :]), reads=[krn], writes=[krn])
                    p.op("dve", lambda e: e.tensor_scalar(
                        S["rn"][0:1, 0:256], S["rn"][0:1, 0:256], 0.08838834764831845,
                        None, AX.mult), reads=[krn], writes=[krn])
                A(u_l2b)

                def u_l2c():
                    p.op("pe", lambda e: e.matmul(
                        P[2][:, 0:512], ones_rf[:], S["rn"][0:1, :], start=True, stop=True),
                        reads=[krn, "masks"], writes=["PS2"])
                    p.op("dve", lambda e: e.tensor_tensor(
                        qh4[:, ci], qkc[:, 0:2, cs:cs + 128], PV2[:, 0:2, :], AX.mult),
                        reads=["PS2", "qkc"], writes=[kqh])
                    p.op("dve", lambda e: e.tensor_tensor(
                        kh4[:, ci], qkc[:, 2:4, cs:cs + 128], PV2[:, 2:4, :], AX.mult),
                        reads=["PS2", "qkc"], writes=[kkh])
                A(u_l2c)

                def u_gramG():
                    def fgrG(e):
                        r = None
                        for kh in range(2):
                            r = e.matmul(P[3][:, kh * 128:(kh + 1) * 128], kh4[:, ci, kh, :],
                                         kh4[:, ci, kh, :], start=True, stop=True)
                        return r
                    p.op("pe", fgrG, reads=[kkh], writes=["PS3"])
                    p.op("dve", lambda e: e.tensor_tensor(
                        S["scr4"][:], S["Dt4"][:], G2, AX.mult),
                        reads=["PS3", kDt], writes=[kscr])
                A(u_gramG)

                def u_gramQ():
                    def fgrQ(e):
                        r = None
                        for kh in range(2):
                            r = e.matmul(P[4][:, kh * 128:(kh + 1) * 128], kh4[:, ci, kh, :],
                                         qh4[:, ci, kh, :], start=True, stop=True)
                        return r
                    p.op("pe", fgrQ, reads=[kkh, kqh], writes=["PS4"])
                    p.op("dve", lambda e: e.tensor_tensor(
                        S["sq4"][:], S["Dt4"][:], QK2, AX.mult),
                        reads=["PS4", kDt], writes=[ksq])
                A(u_gramQ)

                def u_mat():
                    p.op("pool", lambda e: e.tensor_tensor(
                        S["scr4"][:], S["scr4"][:], mid0(maskUs[:], 4), AX.mult),
                        reads=[kscr, "masks"], writes=[kscr])
                    p.op("pool", lambda e: e.tensor_tensor(
                        S["Mt4"][:], S["scr4"][:], s0(bbuf[pt][:, ci, 0:4], 128), AX.mult),
                        reads=[kscr, kB], writes=[kMt])
                    p.op("pool", lambda e: e.tensor_tensor(
                        At44[:, ci], S["sq4"][:], mid0(maskUi[:], 4), AX.mult),
                        reads=[ksq, "masks"], writes=[f"At{ci}"])
                A(u_mat)

                def u_kT():
                    def fkT(e):
                        r = None
                        for kh in range(2):
                            r = e.transpose(P6[:, kh * 128:(kh + 1) * 128], kh4[:, ci, kh, :],
                                            identb[:])
                        return r
                    p.op("pe", fkT, reads=[kkh, "masks"], writes=["PSBa"])
                    p.op("dve", lambda e: e.tensor_tensor(
                        Kt44[:, ci], P6a2, s0(dec44[:, ci, :], 128), AX.mult),
                        reads=["PSBa", kdec], writes=[f"Kt{ci}"])
                A(u_kT)

                def u_vT():
                    def fvT(e):
                        r = None
                        for vh in range(4):
                            r = e.transpose(P6[:, vh * 128:(vh + 1) * 128],
                                            vc_b[:, vh, cs:cs + 128], identb[:])
                        return r
                    p.op("pe", fvT, reads=["vcb", "masks"], writes=["PSBa"])
                    cop(ev(), V44[:, ci], P6a4, ["PSBa"], [f"V4{ci}"])
                A(u_vT)

                def u_zT():
                    def fzT(e):
                        r = None
                        for vh in range(4):
                            r = e.transpose(P6[:, vh * 128:(vh + 1) * 128],
                                            z_b[:, vh, cs:cs + 128], identb[:])
                        return r
                    p.op("pe", fzT, reads=["zb", "masks"], writes=["PSBa"])
                    cop(ev(), z44[:, ci], P6a4, ["PSBa"], [f"z4{ci}"])
                A(u_zT)

                def u_V0():
                    p.op("dve", lambda e: e.tensor_tensor(
                        S["Vp"][0][:], mid0(identb[:], 4), S["Mt4"][:], AX.subtract),
                        reads=["masks", kMt], writes=[f"Vp{sx}0"])
                    def fMT(e):
                        r = None
                        for vh in range(4):
                            r = e.transpose(P6[:, vh * 128:(vh + 1) * 128], S["Mt4"][:, vh, :],
                                            identb[:])
                        return r
                    p.op("pe", fMT, reads=[kMt, "masks"], writes=["PSBa"])
                    p.op("act", lambda e: e.copy(S["Nb"][0][:], P6a4),
                         reads=["PSBa"], writes=[f"Nb{sx}0"])
                A(u_V0)

                for j in range(1, 7):
                    tb_p, tkp = ((S["Mt4"], kMt) if j == 1
                                 else (S["Tb"][(j - 1) % 2], f"Tb{sx}{(j - 1) % 2}"))
                    nb_p, nkp = S["Nb"][(j - 1) % 2], f"Nb{sx}{(j - 1) % 2}"
                    vp_p, vkp = S["Vp"][(j - 1) % 2], f"Vp{sx}{(j - 1) % 2}"
                    ji = j % 2
                    if j < 6:
                        def u_sqT(nb_p=nb_p, tb_p=tb_p, nkp=nkp, tkp=tkp, ji=ji):
                            def fsqT(e):
                                r = None
                                for vh in range(4):
                                    r = e.matmul(PV2[:, vh, :], nb_p[:, vh, :], tb_p[:, vh, :],
                                                 start=True, stop=True)
                                return r
                            p.op("pe", fsqT, reads=[nkp, tkp], writes=["PS2"])
                            cop(ev(), S["Tb"][ji][:], PV2[:], ["PS2"], [f"Tb{sx}{ji}"])
                        A(u_sqT)

                    def u_sqN(nb_p=nb_p, tb_p=tb_p, nkp=nkp, tkp=tkp, ji=ji):
                        def fsqN(e):
                            r = None
                            for vh in range(4):
                                r = e.matmul(PV3[:, vh, :], tb_p[:, vh, :], nb_p[:, vh, :],
                                             start=True, stop=True)
                            return r
                        p.op("pe", fsqN, reads=[nkp, tkp], writes=["PS3"])
                        p.op("act", lambda e: e.copy(S["Nb"][ji][:], PV3[:]),
                             reads=["PS3"], writes=[f"Nb{sx}{ji}"])
                    A(u_sqN)

                    def u_Vm(vp_p=vp_p, vkp=vkp, ji=ji, j=j):
                        def fVm(e):
                            r = None
                            for vh in range(4):
                                r = e.matmul(PV4[:, vh, :], S["Nb"][ji][:, vh, :],
                                             vp_p[:, vh, :], start=True, stop=True)
                            return r
                        p.op("pe", fVm, reads=[f"Nb{sx}{ji}", vkp], writes=["PS4"])
                        if j < 6:
                            p.op("dve", lambda e: e.tensor_tensor(
                                S["Vp"][ji][:], vp_p[:], PV4[:], AX.add),
                                reads=["PS4", vkp], writes=[f"Vp{sx}{ji}"])
                        else:
                            p.op("dve", lambda e: e.tensor_tensor(
                                W4[:, ci], vp_p[:], PV4[:], AX.add),
                                reads=["PS4", vkp], writes=[f"W4{ci}"])
                    A(u_Vm)
                return st

            PV0, PV1 = PV[0], PV[1]

            def chain_emit(b, tt, ci):
                pt = tt % 2
                ch = tt * CPT + ci
                op = ch % 2
                first = (ch == 0)
                if not first:
                    def fks(e):
                        r = None
                        for vh in range(4):
                            r = e.matmul(PV0[:, vh, :], kh4[:, ci, vh // 2, :], S_b[:, vh, :],
                                         start=True, stop=True)
                        return r
                    p.op("pe", fks, reads=[f"kh{ci}", "Sb"], writes=["PS0"])
                    def fqs(e):
                        r = None
                        for vh in range(4):
                            r = e.matmul(PV1[:, vh, :], qh4[:, ci, vh // 2, :], S_b[:, vh, :],
                                         start=True, stop=True)
                        return r
                    p.op("pe", fqs, reads=[f"qh{ci}", "Sb"], writes=["PS1"])
                    p.op("act", lambda e: e.copy(QS4[:, ci], PV1[:]),
                         reads=["PS1"], writes=[f"QS{ci}"])
                    p.op("dve", lambda e: e.tensor_tensor(
                        Y0f[:], PV0[:], s0(lamn44[:, ci, :], 128), AX.mult),
                        reads=["PS0", f"lamn{ci}"], writes=["Y0f"])
                    p.op("pool", lambda e: e.tensor_tensor(
                        Y0b[:], Y0f[:], V44[:, ci], AX.add),
                        reads=["Y0f", f"V4{ci}"], writes=["Y0b"])
                else:
                    cop("dve", Y0b[:], V44[:, ci], [f"V4{ci}"], ["Y0b"])
                def fyW(e):
                    r = None
                    for vh in range(4):
                        r = e.matmul(PV0[:, vh, :], W4[:, ci, vh, :], Y0b[:, vh, :],
                                     start=True, stop=True)
                    return r
                p.op("pe", fyW, reads=[f"W4{ci}", "Y0b"], writes=["PS0"])
                p.op("dve", lambda e: e.tensor_tensor(
                    U4[:, ci], PV0[:], s0(bbuf[pt][:, ci, 0:4], 128), AX.mult),
                    reads=["PS0", f"bbuf{pt}{ci}"], writes=[f"U4{ci}"])
                def fsd(e):
                    r = None
                    for vh in range(4):
                        r = e.matmul(PV0[:, vh, :], Kt44[:, ci, vh, :], U4[:, ci, vh, :],
                                     start=True, stop=True)
                    return r
                p.op("pe", fsd, reads=[f"Kt{ci}", f"U4{ci}"], writes=["PS0"])
                if first:
                    p.op("dve", lambda e: e.tensor_copy(S_f[:], PV0[:]),
                         reads=["PS0"], writes=["Sf"])
                else:
                    p.op("dve", lambda e: e.tensor_tensor(
                        S_f[:], S_f[:], s0(LamB44[:, ci, :], 128), AX.mult),
                        reads=["Sf", f"LamB{ci}"], writes=["Sf"])
                    p.op("dve", lambda e: e.tensor_tensor(S_f[:], S_f[:], PV0[:], AX.add),
                         reads=["Sf", "PS0"], writes=["Sf"])
                cop("act", S_b[:], S_f[:], ["Sf"], ["Sb"])
                def fo(e):
                    r = None
                    for vh in range(4):
                        r = e.matmul(PV1[:, vh, :], At44[:, ci, vh, :], U4[:, ci, vh, :],
                                     start=True, stop=True)
                    return r
                p.op("pe", fo, reads=[f"At{ci}", f"U4{ci}", f"QS{ci}"], writes=["PS1"])
                ko = f"o4{op}"
                if first:
                    p.op("dve", lambda e: e.tensor_copy(o4[op][:], PV1[:]),
                         reads=["PS1"], writes=[ko])
                else:
                    p.op("dve", lambda e: e.tensor_tensor(
                        o4[op][:], QS4[:, ci], s0(lam44[:, ci, :], 128), AX.mult),
                        reads=[f"QS{ci}", f"lam{ci}"], writes=[ko])
                    p.op("dve", lambda e: e.tensor_tensor(o4[op][:], o4[op][:], PV1[:], AX.add),
                         reads=[ko, "PS1"], writes=[ko])
                p.op("pool", lambda e: e.tensor_tensor(o4[op][:], o4[op][:], z44[:, ci], AX.mult),
                     reads=[ko, f"z4{ci}"], writes=[ko])
                p.op("pool", lambda e: e.tensor_tensor(scro[op][:], o4[op][:], o4[op][:], AX.mult),
                     reads=[ko], writes=[f"scro{op}"])
                r6 = dataclasses.replace(ssum4[op][:], ap=list(ssum4[op][:].ap) + [[1, 1]])
                p.op("dve", lambda e: e.reduce_sum(r6, scro[op][:], AL.X),
                     reads=[f"scro{op}"], writes=[f"ssum{op}"])
                p.op("dve", lambda e: e.tensor_scalar(rstd4[op][:], ssum4[op][:], 1.0 / 128,
                                                      1e-6, AX.mult, AX.add),
                     reads=[f"ssum{op}"], writes=[f"rstd{op}"])
                p.op("act", lambda e: e.activation(rstd4[op][:], rstd4[op][:], AF.Sqrt),
                     reads=[f"rstd{op}"], writes=[f"rstd{op}"])
                p.op("dve", lambda e: e.reciprocal(rstd4[op][:], rstd4[op][:]),
                     reads=[f"rstd{op}"], writes=[f"rstd{op}"])
                p.op("pool", lambda e: e.tensor_tensor(
                    o4[op][:], o4[op][:], s0(rstd4[op][:], 128), AX.mult),
                    reads=[ko, f"rstd{op}"], writes=[ko])
                p.op("pool", lambda e: e.tensor_tensor(
                    ogb[op][:], o4[op][:], mid0(nw_bc[:], 4), AX.mult),
                    reads=[ko, "nwbc"], writes=[f"ogb{op}"])
                def ftro(e):
                    r = None
                    for vh in range(4):
                        r = e.transpose(P6[:, 512 + vh * 128:512 + (vh + 1) * 128],
                                        ogb[op][:, vh, :], identb[:])
                    return r
                p.op("pe", ftro, reads=[f"ogb{op}", "masks"], writes=["PSBb"])
                p.op("dve", lambda e: e.tensor_copy(
                    ogt[op][:], P6[:, 512:1024].rearrange("p (a c) -> p a c", a=4)),
                    reads=["PSBb"], writes=[f"ogt{op}"])
                dst = og_my[b].ap().rearrange("(v d) t -> d v t", v=4)[:, :, ch * 128:(ch + 1) * 128]
                p.op("pool", lambda e, dst=dst: e.dma_start(out=dst, in_=ogt[op][:]),
                     reads=[f"ogt{op}"], writes=[f"ogd{b}_{ch}"], dma=f"d_og{b}_{op}")

            for b in range(B):
                for tt in range(NTPB):
                    proj_ttile(b, tt)
                    streams = [prep_steps(b, tt, ci) for ci in range(CPT)]
                    for pair in ((0, 1), (2, 3)):
                        ix = {c2: 0 for c2 in pair}
                        alive = True
                        while alive:
                            alive = False
                            for c2 in pair:
                                if ix[c2] < len(streams[c2]):
                                    streams[c2][ix[c2]]()
                                    ix[c2] += 1
                                    alive = True
                    for ci in range(CPT):
                        chain_emit(b, tt, ci)
                        drain_cast(5)
                def fag(e, b=b):
                    return e.collective_compute(
                        "AllGather", AX.bypass, replica_groups=[list(range(NC8))],
                        ins=[og_my[b].ap().opt()], outs=[og_ag[b].ap().opt()])
                p.op("pool", fag, reads=[f"ogd{b}_{ch}" for ch in range(NCH)],
                     writes=[f"ogag{b}"], cc=True)
            drain_cast(len(cast_q) + 1)

        p.barrier()
        with ExitStack() as p2:
            sb2 = lambda nm, shape, dty=F32: p2.enter_context(nc.sbuf_tensor(nm, shape, dty))
            x1f = sb2("x1f", [128, KPB, 2048], BF16)
            yf = sb2("yf", [128, DPB, 512], F32)
            vb2 = sb2("vb2", [128, 512], BF16)
            wstb = sb2("wstb", [128, 32, 128], BF16)
            cmb_t = sb2("cmb_t", [128, 8], F32)
            cmb_b = sb2("cmb_b", [128, 8], BF16)
            cmbfm2 = sb2("cmbfm2", [8, 2048], BF16)
            sel8 = sb2("sel8", [8, 8 * 128], BF16)
            m8 = sb2("m8", [128, 2], F32)

            def fsel(e):
                r = None
                for ei in range(8):
                    r = e.tensor_scalar(sel8[:, ei * 128:(ei + 1) * 128],
                                        s0(pramp[0:8, 0:1], 128), float(ei), None, AX.is_equal)
                return r
            p.op("dve", fsel, reads=["pramp"], writes=["sel8"])

            def ln_fm(vget, vkeys, emit_out):
                for cpb in range(DPB):
                    cop("dve", vb2[:], vget(cpb), vkeys, ["vb2"])
                    p.op("pe", lambda e, cpb=cpb: e.matmul(
                        P[5][0:1, 0:512], ones_cb[:], vb2[:],
                        start=(cpb == 0), stop=(cpb == DPB - 1)), reads=["vb2"], writes=["PS5"])
                    p.op("act", lambda e: e.activation(vb2[:], vb2[:], AF.Square),
                         reads=["vb2"], writes=["vb2"])
                    p.op("pe", lambda e, cpb=cpb: e.matmul(
                        P[4][0:1, 0:512], ones_cb[:], vb2[:],
                        start=(cpb == 0), stop=(cpb == DPB - 1)), reads=["vb2"], writes=["PS4"])
                p.op("dve", lambda e: e.tensor_scalar(stat_m[0:1, :], P[5][0:1, 0:512],
                                                      1.0 / 2048, None, AX.mult),
                     reads=["PS5"], writes=["statm"])
                p.op("dve", lambda e: e.tensor_scalar(stat_q[0:1, :], P[4][0:1, 0:512],
                                                      1.0 / 2048, None, AX.mult),
                     reads=["PS4"], writes=["statq"])
                p.op("dve", lambda e: e.tensor_tensor(stat_v[0:1, :], stat_m[0:1, :],
                                                      stat_m[0:1, :], AX.mult),
                     reads=["statm"], writes=["statv"])
                p.op("dve", lambda e: e.tensor_tensor(stat_v[0:1, :], stat_q[0:1, :],
                                                      stat_v[0:1, :], AX.subtract),
                     reads=["statq", "statv"], writes=["statv"])
                p.op("act", lambda e: e.activation(stat_v[0:1, :], stat_v[0:1, :],
                                                   AF.Sqrt, bias=epsB[0:1, 0:1]),
                     reads=["statv", "masks"], writes=["statv"])
                p.op("dve", lambda e: e.reciprocal(stat_v[0:1, :], stat_v[0:1, :]),
                     reads=["statv"], writes=["statv"])
                p.op("pe", lambda e: e.matmul(P[5][:, 0:512], ones_rf[:], stat_m[:],
                                              start=True, stop=True),
                     reads=["statm"], writes=["PS5"])
                p.op("pe", lambda e: e.matmul(P[4][:, 0:512], ones_rf[:], stat_v[:],
                                              start=True, stop=True),
                     reads=["statv"], writes=["PS4"])
                cop("dve", r_s[:], P[4][:, 0:512], ["PS4"], ["rs"])
                for cpb in range(DPB):
                    p.op("dve", lambda e, cpb=cpb: e.tensor_tensor(
                        scr2[:], vget(cpb), P[5][:, 0:512], AX.subtract),
                        reads=list(vkeys) + ["PS5"], writes=["scr2"])
                    emit_out(cpb)

            p2a = p2.enter_context(ExitStack())
            ogsb = p2a.enter_context(nc.sbuf_tensor("ogsb", [128, 32, 512], BF16))
            for nb in range(NB):
                def fldog(e, sem, inc, nb=nb):
                    for arm in e.switch_core_id(n=128):
                        qv = int(arm) % 8
                        bq, soff = qv // 2, (qv % 2) * 2048
                        col = soff + nb * 512
                        e.dma_start(out=ogsb[:], in_=og_ag[bq].ap().rearrange(
                            "(k d) t -> d k t", d=128)[:, :, col:col + 512]).then_inc(sem, inc)
                p.op("sp", fldog, reads=[f"ogag{bq}" for bq in range(B)],
                     writes=["ogsb"], dma="d_ogsb", selfinc=True)
                def fldxr(e, sem, inc, nb=nb):
                    for arm in e.switch_core_id(n=128):
                        qv = int(arm) % 8
                        xcol = (qv // 2) * T + (qv % 2) * 2048 + nb * 512
                        e.dma_start(out=yf[:], in_=xT.rearrange("(k d) t -> d k t", d=128)
                                    [:, :, xcol:xcol + 512]).then_inc(sem, inc)
                p.op("sp", fldxr, reads=["yfx"], writes=["yfx"], dma="d_yfres", selfinc=True)
                for cpb in range(DPB):
                    p.op("sp", lambda e, cpb=cpb: e.dma_start(
                        out=wstb[:], in_=wo_bf[cpb].rearrange("k d c -> d k c")),
                        reads=[f"wobf{cpb}"], writes=["wstb"], dma="d_wstb")
                    ps, pk = P[cpb % 2], PK[cpb % 2]
                    def fwo(e, ps=ps):
                        r = None
                        for k in range(32):
                            r = e.matmul(ps[:], wstb[:, k, :], ogsb[:, k, :],
                                         start=(k == 0), stop=(k == 31))
                        return r
                    p.op("pe", fwo, reads=["wstb", "ogsb"], writes=[pk])
                    p.op("dve", lambda e, cpb=cpb, ps=ps: e.tensor_tensor(
                        yf[:, cpb, :], yf[:, cpb, :], ps[:], AX.add),
                        reads=[pk, "yfx"], writes=["yfx"])

                def emit_x1(cpb, nb=nb):
                    p.op("dve", lambda e, cpb=cpb: e.tensor_tensor(
                        x1f[:, cpb, nb * 512:(nb + 1) * 512], scr2[:], r_s[:], AX.mult),
                        reads=["scr2", "rs"], writes=["x1f"])
                ln_fm(lambda cpb: yf[:, cpb, :], ["yfx"], emit_x1)

            for ttk in range(16):
                def flg(e, ttk=ttk):
                    r = None
                    for k in range(KPB):
                        r = e.matmul(P[5][:, 0:8], x1f[:, k, ttk * 128:(ttk + 1) * 128],
                                     gatew_b[:, k, :], start=(k == 0), stop=(k == KPB - 1))
                    return r
                p.op("pe", flg, reads=["x1f", "gatewb"], writes=["PS5"])
                p.op("dve", lambda e: e.reduce_max(m8[:, 0:1], P[5][:, 0:8], AL.X),
                     reads=["PS5"], writes=["m8"])
                p.op("dve", lambda e: e.tensor_scalar(m8[:, 0:1], m8[:, 0:1], -1.0, None, AX.mult),
                     reads=["m8"], writes=["m8"])
                p.op("act", lambda e: e.activation(cmb_t[:], P[5][:, 0:8], AF.Exp,
                                                   bias=m8[:, 0:1]),
                     reads=["PS5", "m8"], writes=["cmbt"])
                p.op("dve", lambda e: e.reduce_sum(m8[:, 1:2], cmb_t[:], AL.X),
                     reads=["cmbt"], writes=["m8b"])
                p.op("dve", lambda e: e.reciprocal(m8[:, 1:2], m8[:, 1:2]),
                     reads=["m8b"], writes=["m8b"])
                p.op("dve", lambda e: e.tensor_scalar(cmb_t[:], cmb_t[:], m8[:, 1:2], None, AX.mult),
                     reads=["cmbt", "m8b"], writes=["cmbt"])
                p.op("dve", lambda e: e.reduce_max(m8[:, 0:1], cmb_t[:], AL.X),
                     reads=["cmbt"], writes=["m8"])
                p.op("dve", lambda e: e.tensor_scalar(scr2[:, 0:8], cmb_t[:], m8[:, 0:1],
                                                      None, AX.is_ge),
                     reads=["cmbt", "m8"], writes=["scr2"])
                p.op("dve", lambda e: e.scalar_tensor_tensor(scr2[:, 0:8], scr2[:, 0:8], -1e30,
                                                             cmb_t[:], AX.mult, AX.add),
                     reads=["scr2", "cmbt"], writes=["scr2"])
                p.op("dve", lambda e: e.reduce_max(m8[:, 1:2], scr2[:, 0:8], AL.X),
                     reads=["scr2"], writes=["m8b"])
                p.op("dve", lambda e: e.tensor_scalar(scr2[:, 0:8], cmb_t[:], m8[:, 1:2],
                                                      None, AX.is_ge),
                     reads=["cmbt", "m8b"], writes=["scr2"])
                p.op("dve", lambda e: e.tensor_tensor(cmb_b[:], cmb_t[:], scr2[:, 0:8], AX.mult),
                     reads=["cmbt", "scr2"], writes=["cmbb"])
                p.op("pe", lambda e, ttk=ttk: e.transpose(P6[0:8, 0:128],
                                                          cmb_b[:], identb[:]),
                     reads=["cmbb", "masks"], writes=["PS6"])
                cop("dve", cmbfm2[:, ttk * 128:(ttk + 1) * 128], P6[0:8, 0:128],
                    ["PS6"], ["cmbfm"])
            p2a.close()

            p2b = p2.enter_context(ExitStack())
            sb2b = lambda nm, shape, dty=F32: p2b.enter_context(nc.sbuf_tensor(nm, shape, dty))
            cb_e = sb2b("cb_e", [128, 2048], BF16)
            sgb = sb2b("sgb", [128, NB, 512], BF16)
            ggb = sb2b("ggb", [128, NB, 512], BF16)
            wsg = [sb2b(f"wsg{i}", [128, 16, 128], F32) for i in range(2)]
            wsb = [sb2b(f"wsb{i}", [128, 16, 128], BF16) for i in range(2)]

            ucnt = [0]
            for gi in range(NGI):
                is_sh = gi >= E * HPB_R
                e_i = gi // HPB_R if not is_sh else None
                hpb = gi % HPB_R if not is_sh else gi - E * HPB_R
                if (not is_sh) and hpb == 0:
                    for nb2 in range(NB):
                        p.op("pe", lambda e, e_i=e_i, nb2=nb2: e.matmul(
                            P[4][:, 0:512], sel8[:, e_i * 128:(e_i + 1) * 128],
                            cmbfm2[:, nb2 * 512:(nb2 + 1) * 512], start=True, stop=True),
                            reads=["cmbfm", "sel8", "cbe"], writes=["PS4"])
                        cop("dve", cb_e[:, nb2 * 512:(nb2 + 1) * 512], P[4][:, 0:512],
                            ["PS4"], ["cbe"] + (["ogsb", "wstb"] if gi == 0 else []))
                wsrc_g = wg[e_i, hpb] if not is_sh else swg[hpb]
                wsrc_u = wu[e_i, hpb] if not is_sh else swu[hpb]
                for phase, wsrc in (("g", wsrc_g), ("u", wsrc_u)):
                    par = ucnt[0] % 2
                    guard = ["ogsb", "wstb"] if ucnt[0] < 2 else []
                    ucnt[0] += 1
                    wsgX, wsgK = wsg[par], f"wsg{par}"
                    wsbX, wsbK = wsb[par], f"wsb{par}"
                    p.op("sp", lambda e, wsrc=wsrc, wsgX=wsgX: e.dma_start(
                        out=wsgX[:], in_=wsrc.rearrange("k d c -> d k c")),
                        writes=[wsgK] + guard, dma=f"d_wstg{par}")
                    cop(ev(), wsbX[:], wsgX[:], [wsgK], [wsbK] + guard)
                    def fgu(e, wsbX=wsbX):
                        r = None
                        for k in range(KPB):
                            for nb2 in range(NB):
                                r = e.matmul(P[nb2][:],
                                             wsbX[:, k, :], x1f[:, k, nb2 * 512:(nb2 + 1) * 512],
                                             start=(k == 0), stop=(k == KPB - 1))
                        return r
                    p.op("pe", fgu, reads=[wsbK, "x1f"],
                         writes=["PS0", "PS1", "PS2", "PS3"])
                    if phase == "g":
                        for nb2 in range(NB):
                            p.op("act", lambda e, nb2=nb2: e.activation(
                                sgb[:, nb2, :], P[nb2][:], AF.Silu),
                                reads=[PK[nb2]], writes=["sgb"] + guard)
                    else:
                        for nb2 in range(NB):
                            p.op("dve", lambda e, nb2=nb2: e.tensor_tensor(
                                ggb[:, nb2, :], P[nb2][:], sgb[:, nb2, :], AX.mult),
                                reads=[PK[nb2], "sgb"], writes=["ggb"] + guard)
                        if not is_sh:
                            p.op("pool", lambda e: e.tensor_tensor(
                                ggb[:].rearrange("p a c -> p (a c)"),
                                ggb[:].rearrange("p a c -> p (a c)"), cb_e[:], AX.mult),
                                reads=["ggb", "cbe"], writes=["ggb"])
                p.op("pool", lambda e, gi=gi: e.dma_start(
                    out=g_d[gi], in_=ggb[:].rearrange("p a c -> p (a c)")),
                    reads=["ggb"], writes=[f"gd{gi}"], dma="d_gd")

            p2b.close()
            p2c = p2.enter_context(ExitStack())
            sb2c = lambda nm, shape, dty=F32: p2c.enter_context(nc.sbuf_tensor(nm, shape, dty))
            g_eb = [sb2c(f"g_eb{i}", [128, 11, 512], BF16) for i in range(2)]
            wd_eb = [sb2c(f"wd_eb{i}", [128, 88, 128], BF16) for i in range(2)]
            gA = ["wsg0", "wsg1", "wsb0", "wsb1", "cbe", "sgb", "ggb"]
            for nb in range(NB):
                for eg in range(NEG):
                    gpar = (nb * NEG + eg) % 2
                    gX, gK = g_eb[gpar], f"geb{gpar}"
                    guard = gA if (nb, eg) in ((0, 0), (0, 1)) else []
                    p.op("sp", lambda e, eg=eg, nb=nb, gX=gX: e.dma_start(
                        out=gX[:], in_=g_d[eg * 11:(eg + 1) * 11, :, nb * 512:(nb + 1) * 512]
                        .rearrange("a d c -> d a c")),
                        reads=[f"gd{gg}" for gg in range(eg * 11, (eg + 1) * 11)],
                        writes=[gK] + guard, dma=f"d_geb{gpar}")
                    for hf in range(2):
                        wpar = (nb * NEG * 2 + eg * 2 + hf) % 2
                        wX, wK = wd_eb[wpar], f"wdeb{wpar}"
                        r0 = (eg * 2 + hf) * 88
                        wdkeys = [f"wdbf{(eg * 2 + hf) * 88 + gl * 8}" for gl in range(11)]
                        p.op("sp", lambda e, r0=r0, wX=wX: e.dma_start(
                            out=wX[:], in_=wd_bf[r0:r0 + 88].rearrange("r d c -> d r c")),
                            reads=wdkeys, writes=[wK] + guard, dma=f"d_wdeb{wpar}")
                        for dj in range(8):
                            dpb = hf * 8 + dj
                            ps, pk = P[4 + dpb % 2], PK[4 + dpb % 2]
                            def fdn(e, dj=dj, ps=ps, wX=wX, gX=gX):
                                r = None
                                for gl in range(11):
                                    r = e.matmul(ps[:], wX[:, gl * 8 + dj, :], gX[:, gl, :],
                                                 start=(gl == 0), stop=(gl == 10))
                                return r
                            p.op("pe", fdn, reads=[wK, gK], writes=[pk])
                            if eg == 0:
                                p.op("dve", lambda e, dpb=dpb, ps=ps: e.tensor_copy(
                                    yf[:, dpb, :], ps[:]), reads=[pk], writes=["yfx"])
                            else:
                                p.op("dve", lambda e, dpb=dpb, ps=ps: e.tensor_tensor(
                                    yf[:, dpb, :], yf[:, dpb, :], ps[:], AX.add),
                                    reads=[pk, "yfx"], writes=["yfx"])
                for dpb in range(DPB):
                    p.op("dve", lambda e, dpb=dpb, nb=nb: e.tensor_tensor(
                        yf[:, dpb, :], yf[:, dpb, :],
                        x1f[:, dpb, nb * 512:(nb + 1) * 512], AX.add),
                        reads=["yfx", "x1f"], writes=["yfx"])

                def emit_out(cpb, nb=nb):
                    p.op("dve", lambda e, cpb=cpb: e.tensor_tensor(
                        yf[:, cpb, :], scr2[:], r_s[:], AX.mult),
                        reads=["scr2", "rs"], writes=["yfx"])
                    p.op("pool", lambda e, cpb=cpb, nb=nb: e.dma_start(
                        out=out_fm[cpb * 128:(cpb + 1) * 128, nb * 512:(nb + 1) * 512],
                        in_=yf[:, cpb, :]), reads=["yfx"], writes=[f"outd{nb}_{cpb}"],
                        dma="d_out")
                ln_fm(lambda cpb: yf[:, cpb, :], ["yfx"], emit_out)

            p.barrier()

        p.emit(blk)
    return nc


_NC_CACHE = {}


def _get_nc():
    if "nc" not in _NC_CACHE:
        _NC_CACHE["nc"] = build_nc()
    return _NC_CACHE["nc"]


def _prep_host(inputs):
    f32 = np.float32
    x = np.asarray(inputs["L0_attn_hidden_states"], f32)
    xT = np.ascontiguousarray(x.reshape(B * T, D).T)
    Wq = np.asarray(inputs["L0_attn_W_qkvz"], f32)
    Wba = np.asarray(inputs["L0_attn_W_ba"], f32)
    cw = np.asarray(inputs["L0_attn_conv_w"], f32)
    dtb = np.asarray(inputs["L0_attn_dt_bias"], f32)
    alog = np.asarray(inputs["L0_attn_A_log"], f32)
    nw = np.asarray(inputs["L0_attn_norm_weight"], f32)
    Wo = np.asarray(inputs["L0_attn_W_out"], f32)
    gw = np.asarray(inputs["L0_ffn_gate_w"], f32)
    wg_ = np.asarray(inputs["L0_ffn_w_gate"], f32)
    wu_ = np.asarray(inputs["L0_ffn_w_up"], f32)
    wd_ = np.asarray(inputs["L0_ffn_w_down"], f32)
    swg_ = np.asarray(inputs["L0_ffn_sw_gate"], f32)
    swu_ = np.asarray(inputs["L0_ffn_sw_up"], f32)
    swd_ = np.asarray(inputs["L0_ffn_sw_down"], f32)

    ident_c = np.eye(128, dtype=ml_dtypes.bfloat16)
    rr_, cc_ = np.arange(128)[:, None], np.arange(128)[None, :]
    masku_c = (cc_ >= rr_).astype(f32)
    maskus_c = (cc_ > rr_).astype(f32)
    pramp_c = np.arange(128, dtype=f32)
    wout_t = np.ascontiguousarray(
        Wo.reshape(32, 128, DPB, 128).transpose(2, 0, 1, 3))
    gatew_t = np.ascontiguousarray(gw.reshape(KPB, 128, 8))
    wg_t = np.ascontiguousarray(
        wg_.reshape(E, KPB, 128, HPB_R, 128).transpose(0, 3, 1, 2, 4))
    wu_t = np.ascontiguousarray(
        wu_.reshape(E, KPB, 128, HPB_R, 128).transpose(0, 3, 1, 2, 4))
    wd_t = np.ascontiguousarray(
        wd_.reshape(E, HPB_R, 128, DPB, 128).transpose(0, 1, 3, 2, 4)
        .reshape(E * HPB_R, DPB, 128, 128))
    swg_t = np.ascontiguousarray(swg_.reshape(KPB, 128, HPB_S, 128).transpose(2, 0, 1, 3))
    swu_t = np.ascontiguousarray(swu_.reshape(KPB, 128, HPB_S, 128).transpose(2, 0, 1, 3))
    swd_t = np.ascontiguousarray(swd_.reshape(HPB_S, 128, DPB, 128).transpose(0, 2, 1, 3))

    in_maps = []
    for c in range(NC8):
        khs = [2 * c, 2 * c + 1]
        vhs = [4 * c + i for i in range(4)]
        cols = []
        for kh in khs:
            cols.append(np.arange(kh * 768, kh * 768 + 128))
        for kh in khs:
            cols.append(np.arange(kh * 768 + 128, kh * 768 + 256))
        for vh in vhs:
            cols.append(np.arange((vh // 2) * 768 + 256 + (vh % 2) * 128,
                                  (vh // 2) * 768 + 256 + (vh % 2) * 128 + 128))
        for vh in vhs:
            cols.append(np.arange((vh // 2) * 768 + 512 + (vh % 2) * 128,
                                  (vh // 2) * 768 + 512 + (vh % 2) * 128 + 128))
        wq_c = np.ascontiguousarray(
            np.stack([Wq[:, ci] for ci in cols], 0).reshape(12, KPB, 128, 128))
        bacols = [4 * (vh // 2) + (vh % 2) for vh in vhs] + \
                 [4 * (vh // 2) + 2 + (vh % 2) for vh in vhs]
        wba_c = np.ascontiguousarray(Wba[:, bacols].reshape(KPB, 128, 8))
        convrows = np.concatenate(
            [np.arange(kh * 128, (kh + 1) * 128) for kh in khs] +
            [np.arange(2048 + kh * 128, 2048 + (kh + 1) * 128) for kh in khs] +
            [np.arange(4096 + vh * 128, 4096 + (vh + 1) * 128) for vh in vhs])
        convw_c = np.ascontiguousarray(cw[convrows].reshape(8, 128, 4))
        dtb8 = np.concatenate([np.zeros(4, f32), dtb[vhs]])
        alog8 = np.concatenate([np.zeros(4, f32), alog[vhs]])
        in_maps.append({
            "xT": xT, "wqkvz": wq_c, "wba": wba_c, "convw": convw_c,
            "dtb8": dtb8, "alog8": alog8, "nwin": nw, "wout": wout_t,
            "gatew": gatew_t, "wg": wg_t, "wu": wu_t, "wdn": wd_t,
            "swg": swg_t, "swu": swu_t, "swd": swd_t,
            "ident_in": ident_c, "masku_in": masku_c, "maskus_in": maskus_c,
            "pramp_in": pramp_c,
        })
    return in_maps


def kernel(**inputs):
    nc = _get_nc()
    in_maps = _prep_host(inputs)
    res = run_bass_kernel_spmd(nc, in_maps, core_ids=list(range(NC8)))
    out = np.zeros((B * T, D), np.float32)
    for c in range(NC8):
        q = int(round(float(res.results[c]["myq"][0, 0])))
        out[q * 2048:(q + 1) * 2048, :] = res.results[c]["out_fm"].T
    return out.reshape(B, T, D)


# version 14
# speedup vs baseline: 1.0175x; 1.0175x over previous
import dataclasses
from contextlib import ExitStack
import numpy as np
import ml_dtypes
import concourse.bass as bass
import concourse.mybir as mybir
from concourse.bass_utils import run_bass_kernel_spmd

F32 = mybir.dt.float32
BF16 = mybir.dt.bfloat16
AX = mybir.AluOpType
AF = mybir.ActivationFunctionType
AL = mybir.AxisListType

B, T, D = 4, 4096, 2048
NTPB, TT = 8, 512
C, NCH, CPT = 128, 32, 4
NC8 = 8
E, HPB_R, HPB_S = 8, 11, 22
DPB, KPB, NB = 16, 16, 4
NGI = E * HPB_R + HPB_S
NEG = 10


def s0(ap, n):
    return dataclasses.replace(ap, ap=list(ap.ap) + [[0, n]])


def mid0(ap, n):
    a = list(ap.ap)
    return dataclasses.replace(ap, ap=[a[0], [0, n]] + a[1:])


def dup2(ap):
    a = list(ap.ap)
    return dataclasses.replace(ap, ap=[a[0], a[1], [0, 2], a[2]])


class Prog:
    ENGS = ("pe", "dve", "act", "pool", "sp")

    def __init__(self, nc, ctx):
        self.strict = True
        self.nc = nc
        self.q = {e: [] for e in self.ENGS}
        self.ctx = ctx
        self.sem, self.cnt = {}, {}
        for s in self.ENGS + ("cc",):
            self.sem[s] = ctx.enter_context(nc.semaphore("sem_" + s))
            self.cnt[s] = 0
        self.seen = {e: {} for e in self.ENGS}
        self.bufw, self.bufr = {}, {}

    def _need(self, eng, dep, waits, seen):
        sem, val = dep
        if sem == eng and (eng == "pe" or (eng != "act" and not self.strict)):
            return
        if seen.get(sem, 0) >= val:
            return
        waits.append((sem, val))
        seen[sem] = val

    def op(self, eng, fn, reads=(), writes=(), dma=None, cc=False, selfinc=False):
        waits, seen = [], self.seen[eng]
        for b in reads:
            w = self.bufw.get(b)
            if w:
                self._need(eng, w, waits, seen)
        for b in writes:
            w = self.bufw.get(b)
            if w:
                self._need(eng, w, waits, seen)
            for r in self.bufr.get(b, ()):
                self._need(eng, r, waits, seen)
        if dma:
            sname, step = dma, 16
            if sname not in self.sem:
                self.sem[sname] = self.ctx.enter_context(
                    self.nc.semaphore("sem_" + sname))
                self.cnt[sname] = 0
        elif cc:
            sname, step = "cc", 1
        else:
            sname, step = eng, 1
        self.cnt[sname] += step
        val = self.cnt[sname]
        wl = [(self.sem[s], v) for s, v in waits]
        sem_obj = self.sem[sname]

        inc = 16 if dma else step
        if selfinc:
            def run(engine, fn=fn, wl=wl, sem_obj=sem_obj, inc=inc):
                for s, v in wl:
                    engine.wait_ge(s, v)
                fn(engine, sem_obj, inc)
        else:
            def run(engine, fn=fn, wl=wl, sem_obj=sem_obj, inc=inc):
                for s, v in wl:
                    engine.wait_ge(s, v)
                inst = fn(engine)
                if inst is not None:
                    inst.then_inc(sem_obj, inc)

        self.q[eng].append(run)
        for b in writes:
            self.bufw[b] = (sname, val)
            self.bufr[b] = []
        for b in reads:
            self.bufr.setdefault(b, []).append((sname, val))

    def barrier(self):
        snap = dict(self.cnt)
        for e in self.ENGS:
            todo = []
            for s, v in snap.items():
                if (s == e and not self.strict) or v == 0 or \
                        self.seen[e].get(s, 0) >= v:
                    continue
                todo.append((self.sem[s], v))
                self.seen[e][s] = v

            def run(engine, todo=todo):
                for s, v in todo:
                    engine.wait_ge(s, v)
            self.q[e].append(run)
        self.bufw, self.bufr = {}, {}

    def emit(self, blk):
        qq = self.q
        blk.tensor(lambda e: [f(e) for f in qq["pe"]] and None)
        blk.vector(lambda e: [f(e) for f in qq["dve"]] and None)
        blk.scalar(lambda e: [f(e) for f in qq["act"]] and None)
        blk.gpsimd(lambda e: [f(e) for f in qq["pool"]] and None)
        blk.sync(lambda e: [f(e) for f in qq["sp"]] and None)


def build_nc():
    nc = bass.Bass("TRN2", target_bir_lowering=False)
    dt = nc.dram_tensor
    xT = dt("xT", [D, B * T], F32, kind="ExternalInput").ap()
    wqkvz = dt("wqkvz", [12, KPB, 128, 128], F32, kind="ExternalInput").ap()
    wba = dt("wba", [KPB, 128, 8], F32, kind="ExternalInput").ap()
    convw = dt("convw", [8, 128, 4], F32, kind="ExternalInput").ap()
    dtb8 = dt("dtb8", [8], F32, kind="ExternalInput").ap()
    alog8 = dt("alog8", [8], F32, kind="ExternalInput").ap()
    nwin = dt("nwin", [128], F32, kind="ExternalInput").ap()
    wout = dt("wout", [DPB, 32, 128, 128], F32, kind="ExternalInput").ap()
    gatew = dt("gatew", [KPB, 128, 8], F32, kind="ExternalInput").ap()
    wg = dt("wg", [E, HPB_R, KPB, 128, 128], F32, kind="ExternalInput").ap()
    wu = dt("wu", [E, HPB_R, KPB, 128, 128], F32, kind="ExternalInput").ap()
    wdn = dt("wdn", [E * HPB_R, DPB, 128, 128], F32, kind="ExternalInput").ap()
    swg = dt("swg", [HPB_S, KPB, 128, 128], F32, kind="ExternalInput").ap()
    swu = dt("swu", [HPB_S, KPB, 128, 128], F32, kind="ExternalInput").ap()
    swd = dt("swd", [HPB_S, DPB, 128, 128], F32, kind="ExternalInput").ap()
    out_fm = dt("out_fm", [D, 2048], F32, kind="ExternalOutput").ap()
    myq = dt("myq", [1, 1], F32, kind="ExternalOutput").ap()
    ident_in = dt("ident_in", [128, 128], BF16, kind="ExternalInput").ap()
    masku_in = dt("masku_in", [128, 128], F32, kind="ExternalInput").ap()
    maskus_in = dt("maskus_in", [128, 128], F32, kind="ExternalInput").ap()
    pramp_in = dt("pramp_in", [128], F32, kind="ExternalInput").ap()
    og_my = [dt(f"og_my_{b}", [512, T], BF16) for b in range(B)]
    og_ag = [dt(f"og_ag_{b}", [NC8 * 512, T], BF16) for b in range(B)]
    wd_bf = dt("wd_bf", [NGI * DPB, 128, 128], BF16).ap()
    wo_bf = dt("wo_bf", [DPB, 32, 128, 128], BF16).ap()
    g_d = dt("g_d", [NGI, 128, 2048], BF16).ap()

    with ExitStack() as ctx:
        p = Prog(nc, ctx)
        sb = lambda nm, shape, dty=F32: ctx.enter_context(nc.sbuf_tensor(nm, shape, dty))
        psum = lambda nm, shape, dty=F32: ctx.enter_context(nc.psum_tensor(nm, shape, dty))

        identb = sb("identb", [128, 128], BF16)
        maskUi = sb("maskUi", [128, 128], F32)
        maskUs = sb("maskUs", [128, 128], F32)
        ones_cf = sb("ones_cf", [128, 1], F32)
        ones_cb = sb("ones_cb", [128, 1], BF16)
        ones_rf = sb("ones_rf", [1, 128], F32)
        pramp = sb("pramp", [128, 1], F32)
        nw_bc = sb("nw_bc", [128, 128], F32)
        row128 = sb("row128", [1, 128], F32)
        dtb_bc = sb("dtb_bc", [128, 8], F32)
        nea_bc = sb("nea_bc", [128, 8], F32)
        vec8 = sb("vec8", [1, 8], F32)
        gatew_b = sb("gatew_b", [128, KPB, 8], BF16)
        tiny = sb("tiny", [1, 1], F32)
        stat_m = sb("stat_m", [1, 512], F32)
        stat_q = sb("stat_q", [1, 512], F32)
        stat_v = sb("stat_v", [1, 512], F32)
        epsA = sb("epsA", [1, 1], F32)
        epsB = sb("epsB", [1, 1], F32)
        r_s = sb("r_s", [128, 512], F32)
        scr2 = sb("scr2", [128, 512], F32)

        P = [psum(f"PS{i}", [128, 512], F32) for i in range(7)]
        P6 = psum("PSB", [128, 1024], BF16)
        PV = [P[i][:].rearrange("p (a c) -> p a c", a=4) for i in range(7)]
        PK = ["PS0", "PS1", "PS2", "PS3", "PS4", "PS5", "PSX"]

        blk = ctx.enter_context(nc.Block())
        evt = [0]

        def ev():
            evt[0] ^= 1
            return "dve" if evt[0] else "act"

        def cop(eng, dst, src, rd, wr):
            if eng == "act":
                p.op("act", lambda e, d=dst, s=src: e.copy(d, s), reads=rd, writes=wr)
            else:
                p.op(eng, lambda e, d=dst, s=src: e.tensor_copy(d, s), reads=rd, writes=wr)

        p.op("sp", lambda e: e.dma_start(out=identb[:], in_=ident_in),
             writes=["masks"], dma="d_ini0")
        p.op("sp", lambda e: e.dma_start(out=maskUi[:], in_=masku_in),
             writes=["masks"], dma="d_ini1")
        p.op("sp", lambda e: e.dma_start(out=maskUs[:], in_=maskus_in),
             writes=["masks"], dma="d_ini2")
        p.op("sp", lambda e: e.dma_start(
            out=pramp[:, 0], in_=pramp_in),
            writes=["pramp"], dma="d_ini3")
        def f_ones(e):
            e.memset(ones_cf[:], 1.0)
            e.memset(ones_cb[:], 1.0)
            e.memset(epsA[:], 1e-6)
            e.memset(epsB[:], 1e-5)
            return e.memset(ones_rf[:], 1.0)
        p.op("dve", f_ones, writes=["masks"])

        p.op("sp", lambda e: e.dma_start(out=row128[0:1, :], in_=nwin.rearrange("(a b) -> a b", a=1)), writes=["row128"], dma="d_row")
        p.op("pe", lambda e: e.matmul(P[4][:, 0:128], ones_rf[:], row128[:], start=True, stop=True),
             reads=["masks", "row128"], writes=["PS4"])
        cop("dve", nw_bc[:], P[4][:, 0:128], ["PS4"], ["nwbc"])
        p.op("sp", lambda e: e.dma_start(out=vec8[0:1, :], in_=dtb8.rearrange("(a b) -> a b", a=1)), writes=["vec8"], dma="d_vec8")
        p.op("pe", lambda e: e.matmul(P[4][:, 0:8], ones_rf[:], vec8[:], start=True, stop=True),
             reads=["masks", "vec8", "nwbc"], writes=["PS4"])
        cop("dve", dtb_bc[:], P[4][:, 0:8], ["PS4"], ["dtbbc"])
        p.op("sp", lambda e: e.dma_start(out=vec8[0:1, :], in_=alog8.rearrange("(a b) -> a b", a=1)),
             reads=["dtbbc", "vec8"], writes=["vec8"], dma="d_vec8")
        p.op("act", lambda e: e.activation(vec8[0:1, :], vec8[0:1, :], AF.Exp),
             reads=["vec8"], writes=["vec8"])
        p.op("dve", lambda e: e.tensor_scalar(vec8[0:1, :], vec8[0:1, :], -1.0, None, AX.mult),
             reads=["vec8"], writes=["vec8"])
        p.op("pe", lambda e: e.matmul(P[4][:, 0:8], ones_rf[:], vec8[:], start=True, stop=True),
             reads=["vec8", "dtbbc"], writes=["PS4"])
        cop("dve", nea_bc[:], P[4][:, 0:8], ["PS4"], ["neabc"])

        def f_q(e, sem, inc):
            for arm in e.switch_core_id(n=128):
                e.memset(tiny[0:1, 0:1], float(int(arm) % 8)).then_inc(sem, inc)
        p.op("dve", f_q, writes=["tiny"], selfinc=True)
        p.op("sp", lambda e: e.dma_start(out=myq, in_=tiny[0:1, 0:1]),
             reads=["tiny"], writes=["myqd"], dma="d_out")

        with ExitStack() as p1:
            sb1 = lambda nm, shape, dty=F32: p1.enter_context(nc.sbuf_tensor(nm, shape, dty))
            wq_b = sb1("wq_b", [128, 12, KPB, 128], BF16)
            wba_b = sb1("wba_b", [128, KPB, 8], BF16)
            convw_s = sb1("convw_s", [128, 8, 4], F32)
            diag_b = sb1("diag_b", [128, 8, 4, 128], BF16)
            winit = sb1("winit", [128, KPB, 128], F32)
            xstg = sb1("xstg", [128, 2, TT], F32)
            x_bb = sb1("x_bb", [128, KPB, TT], BF16)
            raw = sb1("raw", [128, 8, TT + 4], BF16)
            qkc = sb1("qkc", [128, 4, TT], BF16)
            vc_b = sb1("vc_b", [128, 4, TT], BF16)
            z_b = sb1("z_b", [128, 4, TT], BF16)
            gbuf = sb1("gbuf", [128, CPT, 8], F32)
            bbuf = [sb1(f"bbuf{i}", [128, CPT, 8], F32) for i in range(2)]
            sp_t = sb1("sp_t", [128, 24], F32)
            W4 = sb1("W4", [128, CPT, 4, 128], BF16)
            At44 = sb1("At44", [128, CPT, 4, 128], BF16)
            Kt44 = sb1("Kt44", [128, CPT, 4, 128], BF16)
            V44 = sb1("V44", [128, CPT, 4, 128], BF16)
            z44 = sb1("z44", [128, CPT, 4, 128], BF16)
            kh4 = sb1("kh4", [128, CPT, 2, 128], BF16)
            qh4 = sb1("qh4", [128, CPT, 2, 128], BF16)
            lam44 = sb1("lam44", [128, CPT, 4], F32)
            lamn44 = sb1("lamn44", [128, CPT, 4], F32)
            dec44 = sb1("dec44", [128, CPT, 4], F32)
            LamB44 = sb1("LamB44", [128, CPT, 4], F32)
            bcol4 = sb1("bcol4", [128, CPT, 4], F32)
            brow4 = sb1("brow4", [1, CPT, 512], F32)
            TS = []
            for s_ in range(2):
                TS.append(dict(
                    Dt4=sb1(f"Dt4_{s_}", [128, 4, 128], BF16),
                    scr4=sb1(f"scr4_{s_}", [128, 4, 128], BF16),
                    sq4=sb1(f"sq4_{s_}", [128, 4, 128], F32),
                    Mt4=sb1(f"Mt4_{s_}", [128, 4, 128], BF16),
                    Tb=[sb1(f"Tb{s_}{j}", [128, 4, 128], BF16) for j in range(2)],
                    Nb=[sb1(f"Nb{s_}{j}", [128, 4, 128], BF16) for j in range(2)],
                    Vp=[sb1(f"Vp{s_}{j}", [128, 4, 128], BF16) for j in range(2)],
                    rn=sb1(f"rn_{s_}", [1, 512], F32),
                    bfm=sb1(f"bfm_{s_}", [4, 128], F32),
                ))
            Y0f = sb1("Y0f", [128, 4, 128], F32)
            Y0b = sb1("Y0b", [128, 4, 128], BF16)
            U4 = sb1("U4", [128, CPT, 4, 128], BF16)
            QS4 = sb1("QS4", [128, CPT, 4, 128], BF16)
            S_f = sb1("S_f", [128, 4, 128], F32)
            S_b = sb1("S_b", [128, 4, 128], BF16)
            o4 = [sb1(f"o4_{i}", [128, 4, 128], F32) for i in range(2)]
            scro = [sb1(f"scro_{i}", [128, 4, 128], BF16) for i in range(2)]
            ssum4 = [sb1(f"ssum4_{i}", [128, 4], F32) for i in range(2)]
            rstd4 = [sb1(f"rstd4_{i}", [128, 4], F32) for i in range(2)]
            ogb = [sb1(f"ogb_{i}", [128, 4, 128], BF16) for i in range(2)]
            ogt = [sb1(f"ogt_{i}", [128, 4, 128], BF16) for i in range(2)]

            p.op("sp", lambda e: e.dma_start(out=convw_s[:], in_=convw.rearrange("p d j -> d p j")),
                 writes=["convws"], dma="d_convw")
            for pb in range(8):
                for j in range(4):
                    p.op("dve", lambda e, pb=pb, j=j: e.tensor_scalar(
                        diag_b[:, pb, j, :], identb[:], convw_s[:, pb, j:j + 1], None, AX.mult),
                        reads=["masks", "convws"], writes=["diag"])
            for pb in range(12):
                p.op("sp", lambda e, pb=pb: e.dma_start(
                    out=winit[:], in_=wqkvz[pb].rearrange("k d c -> d k c")),
                    writes=["winit"], dma="d_winit")
                cop(ev(), wq_b[:, pb], winit[:], ["winit"], ["wqb"])
            p.op("sp", lambda e: e.dma_start(
                out=winit[:, :, 0:8], in_=wba.rearrange("k d c -> d k c")),
                writes=["winit"], dma="d_winit")
            cop("dve", wba_b[:], winit[:, :, 0:8], ["winit"], ["wbab"])
            p.op("sp", lambda e: e.dma_start(
                out=winit[:, :, 0:8], in_=gatew.rearrange("k d c -> d k c")),
                writes=["winit"], dma="d_winit")
            cop("dve", gatew_b[:], winit[:, :, 0:8], ["winit"], ["gatewb"])
            p.op("dve", lambda e: e.memset(raw[:, :, 0:4], 0.0),
                 writes=[f"raw{pb}" for pb in range(8)])

            wstbC = sb1("wstbC", [128, 16, 128], BF16)
            cast_q = []
            for cpb in range(DPB):
                for hf in range(2):
                    cast_q.append(lambda cpb=cpb, hf=hf: p.op(
                        "sp", lambda e: e.dma_start(
                            out=winit[:],
                            in_=wout[cpb, hf * 16:(hf + 1) * 16].rearrange("k d c -> d k c")),
                        writes=["winit"], dma="d_winit"))
                    cast_q.append(lambda: cop(ev(), wstbC[:], winit[:],
                                              ["winit"], ["wstbC"]))
                    cast_q.append(lambda cpb=cpb, hf=hf: p.op(
                        "pool", lambda e: e.dma_start(
                            out=wo_bf[cpb, hf * 16:(hf + 1) * 16].rearrange("k d c -> d k c"),
                            in_=wstbC[:]),
                        reads=["wstbC"], writes=[f"wobf{cpb}"], dma="d_wbf"))
            for gi in range(NGI):
                srcap = wdn[gi] if gi < E * HPB_R else swd[gi - E * HPB_R]
                eg, gl = gi // 11, gi % 11
                cast_q.append(lambda srcap=srcap: p.op(
                    "sp", lambda e: e.dma_start(
                        out=winit[:], in_=srcap.rearrange("k d c -> d k c")),
                    writes=["winit"], dma="d_winit"))
                cast_q.append(lambda: cop(ev(), wstbC[:], winit[:],
                                          ["winit"], ["wstbC"]))
                for hf in range(2):
                    r0 = (eg * 2 + hf) * 88 + gl * 8
                    cast_q.append(lambda r0=r0, hf=hf: p.op(
                        "pool", lambda e: e.dma_start(
                            out=wd_bf[r0:r0 + 8].rearrange("k d c -> d k c"),
                            in_=wstbC[:, hf * 8:(hf + 1) * 8, :]),
                        reads=["wstbC"], writes=[f"wdbf{r0}"], dma="d_wbf"))

            def drain_cast(n):
                for _ in range(min(n, len(cast_q))):
                    cast_q.pop(0)()

            def proj_ttile(b, tt):
                t0 = b * T + tt * TT
                pt = tt % 2
                pj = [0]

                def bank():
                    pj[0] ^= 1
                    return (P[5], "PS5") if pj[0] else (P[6], "PSX")
                for kk in range(KPB):
                    sl = kk % 2
                    p.op("sp", lambda e, kk=kk, sl=sl: e.dma_start(
                        out=xstg[:, sl, :], in_=xT[kk * 128:(kk + 1) * 128, t0:t0 + TT]),
                        writes=[f"xstg{sl}"], dma=f"d_x{sl}")
                    cop(ev(), x_bb[:, kk, :], xstg[:, sl, :], [f"xstg{sl}"], ["xbb"])
                for pb in range(12):
                    ps, pk = bank()
                    def fmm(e, pb=pb, ps=ps):
                        r = None
                        for k in range(KPB):
                            r = e.matmul(ps[:], wq_b[:, pb, k, :], x_bb[:, k, :],
                                         start=(k == 0), stop=(k == KPB - 1))
                        return r
                    p.op("pe", fmm, reads=["wqb", "xbb"], writes=[pk])
                    if pb < 8:
                        cop(ev(), raw[:, pb, 4:TT + 4], ps[:], [pk], [f"raw{pb}"])
                    else:
                        p.op("act", lambda e, pb=pb, ps=ps: e.activation(
                            z_b[:, pb - 8, :], ps[:], AF.Silu), reads=[pk], writes=["zb"])
                for ci in range(CPT):
                    ps, pk = bank()
                    def fba(e, ci=ci, ps=ps):
                        r = None
                        for k in range(KPB):
                            r = e.matmul(ps[:, 0:8], x_bb[:, k, ci * 128:(ci + 1) * 128],
                                         wba_b[:, k, :], start=(k == 0), stop=(k == KPB - 1))
                        return r
                    p.op("pe", fba, reads=["wbab", "xbb"], writes=[pk])
                    p.op("act", lambda e, ci=ci, ps=ps: e.activation(
                        bbuf[pt][:, ci, :], ps[:, 0:8], AF.Sigmoid),
                        reads=[pk], writes=[f"bbuf{pt}{ci}"])
                    p.op("dve", lambda e, ps=ps: e.tensor_tensor(
                        sp_t[:, 0:8], ps[:, 0:8], dtb_bc[:], AX.add),
                        reads=[pk, "dtbbc"], writes=["sp0"])
                    p.op("act", lambda e: e.activation(sp_t[:, 8:16], sp_t[:, 0:8], AF.Abs),
                         reads=["sp0"], writes=["sp1"])
                    p.op("act", lambda e: e.activation(sp_t[:, 16:24], sp_t[:, 8:16], AF.Exp, scale=-1.0),
                         reads=["sp1"], writes=["sp2"])
                    p.op("act", lambda e: e.activation(sp_t[:, 8:16], sp_t[:, 16:24], AF.Ln, bias=1.0),
                         reads=["sp2"], writes=["sp1"])
                    p.op("dve", lambda e: e.tensor_scalar_max(sp_t[:, 16:24], sp_t[:, 0:8], 0.0),
                         reads=["sp0"], writes=["sp2"])
                    p.op("dve", lambda e: e.tensor_tensor(sp_t[:, 0:8], sp_t[:, 8:16],
                                                          sp_t[:, 16:24], AX.add),
                         reads=["sp1", "sp2"], writes=["sp0"])
                    p.op("dve", lambda e, ci=ci: e.tensor_tensor(
                        gbuf[:, ci, :], sp_t[:, 0:8], nea_bc[:], AX.mult),
                        reads=["sp0", "neabc"], writes=[f"gbuf{ci}"])
                for pb in range(8):
                    rk = f"raw{pb}"
                    ps, pk = bank()
                    def fcv(e, pb=pb, ps=ps):
                        r = None
                        for j in range(4):
                            r = e.matmul(ps[:], diag_b[:, pb, j, :], raw[:, pb, 1 + j:1 + j + TT],
                                         start=(j == 0), stop=(j == 3))
                        return r
                    p.op("pe", fcv, reads=["diag", rk], writes=[pk])
                    if pb < 4:
                        p.op("act", lambda e, pb=pb, ps=ps: e.activation(
                            qkc[:, pb, :], ps[:], AF.Silu), reads=[pk], writes=["qkc"])
                    else:
                        p.op("act", lambda e, pb=pb, ps=ps: e.activation(
                            vc_b[:, pb - 4, :], ps[:], AF.Silu), reads=[pk], writes=["vcb"])
                    if tt < NTPB - 1:
                        p.op("dve", lambda e, pb=pb: e.tensor_copy(
                            raw[:, pb, 0:4], raw[:, pb, TT:TT + 4]), reads=[rk], writes=[rk])
                    else:
                        p.op("dve", lambda e, pb=pb: e.memset(raw[:, pb, 0:4], 0.0),
                             reads=[rk], writes=[rk])

            PV2, PV3, PV4 = PV[2], PV[3], PV[4]

            def prep_steps(b, tt, ci):
                st = []
                A = st.append
                sx = ci % 2
                S = TS[sx]
                pt = tt % 2
                cs = ci * 128
                kG, kB = f"gbuf{ci}", f"bbuf{pt}{ci}"
                kbc, kdec, klam, klamn, kLB = (f"bcol{ci}", f"dec{ci}", f"lam{ci}",
                                               f"lamn{ci}", f"LamB{ci}")
                kkh, kqh = f"kh{ci}", f"qh{ci}"
                kDt, kscr, ksq, kMt = f"Dt{sx}", f"scr{sx}", f"sq4{sx}", f"Mt{sx}"
                krn, kbfm = f"rn{sx}", f"bfm{sx}"
                G2 = dup2(P[3][:, 0:256].rearrange("p (a c) -> p a c", a=2))
                QK2 = dup2(P[4][:, 0:256].rearrange("p (a c) -> p a c", a=2))
                P6a2 = dup2(P6[:, 0:256].rearrange("p (a c) -> p a c", a=2))
                P6a4 = P6[:, 0:512].rearrange("p (a c) -> p a c", a=4)

                def u_bcol():
                    p.op("pe", lambda e: e.matmul(
                        P[2][:, 0:4], maskUi[:], gbuf[:, ci, 4:8], start=True, stop=True),
                        reads=[kG, "masks"], writes=["PS2"])
                    p.op("dve", lambda e: e.tensor_copy(bcol4[:, ci, :], P[2][:, 0:4]),
                         reads=["PS2"], writes=[kbc])
                A(u_bcol)

                def u_bfm():
                    p.op("pe", lambda e: e.matmul(
                        P[3][0:4, 0:128], gbuf[:, ci, 4:8], maskUi[:], start=True, stop=True),
                        reads=[kG, "masks"], writes=["PS3"])
                    p.op("dve", lambda e: e.tensor_copy(S["bfm"][:], P[3][0:4, 0:128]),
                         reads=["PS3"], writes=[kbfm])
                A(u_bfm)
                brow_dst = brow4[0:1, ci, :]
                brow_v = dataclasses.replace(
                    brow_dst, ap=[brow_dst.ap[0], [128, 4], [1, 128]])
                A(lambda: p.op("sp", lambda e: e.dma_start(out=brow_v, in_=S["bfm"][:]),
                               reads=[kbfm], writes=[f"brow{ci}"], dma=f"d_brow{sx}"))

                def u_fbb():
                    def fbb(e):
                        r = None
                        for vh in range(4):
                            r = e.matmul(PV2[:, vh, :], ones_rf[:],
                                         brow4[0:1, ci, vh * 128:(vh + 1) * 128],
                                         start=True, stop=True)
                        return r
                    p.op("pe", fbb, reads=[f"brow{ci}", "masks"], writes=["PS2"])
                    p.op("dve", lambda e: e.tensor_tensor(
                        dec44[:, ci, :], PV2[:, :, 127], bcol4[:, ci, :], AX.subtract),
                        reads=["PS2", kbc], writes=[kdec])
                    p.op("act", lambda e: e.activation(
                        LamB44[:, ci, :], PV2[:, :, 127], AF.Exp), reads=["PS2"], writes=[kLB])
                    p.op("dve", lambda e: e.tensor_tensor(
                        S["sq4"][:], PV2[:], s0(bcol4[:, ci, :], 128), AX.subtract),
                        reads=["PS2", kbc], writes=[ksq])
                A(u_fbb)

                def u_dec():
                    p.op("act", lambda e: e.activation(
                        dec44[:, ci, :], dec44[:, ci, :], AF.Exp), reads=[kdec], writes=[kdec])
                    p.op("act", lambda e: e.activation(
                        lam44[:, ci, :], bcol4[:, ci, :], AF.Exp), reads=[kbc], writes=[klam])
                    p.op("dve", lambda e: e.tensor_scalar(
                        lamn44[:, ci, :], lam44[:, ci, :], -1.0, None, AX.mult),
                        reads=[klam], writes=[klamn])
                    p.op("dve", lambda e: e.tensor_scalar_min(
                        S["sq4"][:], S["sq4"][:], 0.0), reads=[ksq], writes=[ksq])
                    p.op("act", lambda e: e.activation(
                        S["Dt4"][:], S["sq4"][:], AF.Exp), reads=[ksq], writes=[kDt])
                A(u_dec)

                def u_l2a():
                    p.op("act", lambda e: e.activation(
                        S["sq4"][:], qkc[:, :, cs:cs + 128], AF.Square),
                        reads=["qkc"], writes=[ksq])
                    p.op("pe", lambda e: e.matmul(
                        P[3][0:1, 0:512], ones_cf[:],
                        S["sq4"][:].rearrange("p a c -> p (a c)"), start=True, stop=True),
                        reads=[ksq, "masks"], writes=["PS3"])
                    p.op("act", lambda e: e.activation(
                        S["rn"][0:1, :], P[3][0:1, 0:512], AF.Sqrt, bias=epsA[0:1, 0:1]),
                        reads=["PS3", "masks"], writes=[krn])
                A(u_l2a)

                def u_l2b():
                    p.op("dve", lambda e: e.reciprocal(
                        S["rn"][0:1, :], S["rn"][0:1, :]), reads=[krn], writes=[krn])
                    p.op("dve", lambda e: e.tensor_scalar(
                        S["rn"][0:1, 0:256], S["rn"][0:1, 0:256], 0.08838834764831845,
                        None, AX.mult), reads=[krn], writes=[krn])
                A(u_l2b)

                def u_l2c():
                    p.op("pe", lambda e: e.matmul(
                        P[2][:, 0:512], ones_rf[:], S["rn"][0:1, :], start=True, stop=True),
                        reads=[krn, "masks"], writes=["PS2"])
                    p.op("dve", lambda e: e.tensor_tensor(
                        qh4[:, ci], qkc[:, 0:2, cs:cs + 128], PV2[:, 0:2, :], AX.mult),
                        reads=["PS2", "qkc"], writes=[kqh])
                    p.op("dve", lambda e: e.tensor_tensor(
                        kh4[:, ci], qkc[:, 2:4, cs:cs + 128], PV2[:, 2:4, :], AX.mult),
                        reads=["PS2", "qkc"], writes=[kkh])
                A(u_l2c)

                def u_gramG():
                    def fgrG(e):
                        r = None
                        for kh in range(2):
                            r = e.matmul(P[3][:, kh * 128:(kh + 1) * 128], kh4[:, ci, kh, :],
                                         kh4[:, ci, kh, :], start=True, stop=True)
                        return r
                    p.op("pe", fgrG, reads=[kkh], writes=["PS3"])
                    p.op("dve", lambda e: e.tensor_tensor(
                        S["scr4"][:], S["Dt4"][:], G2, AX.mult),
                        reads=["PS3", kDt], writes=[kscr])
                A(u_gramG)

                def u_gramQ():
                    def fgrQ(e):
                        r = None
                        for kh in range(2):
                            r = e.matmul(P[4][:, kh * 128:(kh + 1) * 128], kh4[:, ci, kh, :],
                                         qh4[:, ci, kh, :], start=True, stop=True)
                        return r
                    p.op("pe", fgrQ, reads=[kkh, kqh], writes=["PS4"])
                    p.op("dve", lambda e: e.tensor_tensor(
                        S["sq4"][:], S["Dt4"][:], QK2, AX.mult),
                        reads=["PS4", kDt], writes=[ksq])
                A(u_gramQ)

                def u_mat():
                    p.op("pool", lambda e: e.tensor_tensor(
                        S["scr4"][:], S["scr4"][:], mid0(maskUs[:], 4), AX.mult),
                        reads=[kscr, "masks"], writes=[kscr])
                    p.op("pool", lambda e: e.tensor_tensor(
                        S["Mt4"][:], S["scr4"][:], s0(bbuf[pt][:, ci, 0:4], 128), AX.mult),
                        reads=[kscr, kB], writes=[kMt])
                    p.op("pool", lambda e: e.tensor_tensor(
                        At44[:, ci], S["sq4"][:], mid0(maskUi[:], 4), AX.mult),
                        reads=[ksq, "masks"], writes=[f"At{ci}"])
                A(u_mat)

                def u_kT():
                    def fkT(e):
                        r = None
                        for kh in range(2):
                            r = e.transpose(P6[:, kh * 128:(kh + 1) * 128], kh4[:, ci, kh, :],
                                            identb[:])
                        return r
                    p.op("pe", fkT, reads=[kkh, "masks"], writes=["PSBa"])
                    p.op("dve", lambda e: e.tensor_tensor(
                        Kt44[:, ci], P6a2, s0(dec44[:, ci, :], 128), AX.mult),
                        reads=["PSBa", kdec], writes=[f"Kt{ci}"])
                A(u_kT)

                def u_vT():
                    def fvT(e):
                        r = None
                        for vh in range(4):
                            r = e.transpose(P6[:, vh * 128:(vh + 1) * 128],
                                            vc_b[:, vh, cs:cs + 128], identb[:])
                        return r
                    p.op("pe", fvT, reads=["vcb", "masks"], writes=["PSBa"])
                    p.op("dve", lambda e: e.tensor_copy(V44[:, ci], P6a4),
                         reads=["PSBa"], writes=[f"V4{ci}"])
                A(u_vT)

                def u_zT():
                    def fzT(e):
                        r = None
                        for vh in range(4):
                            r = e.transpose(P6[:, vh * 128:(vh + 1) * 128],
                                            z_b[:, vh, cs:cs + 128], identb[:])
                        return r
                    p.op("pe", fzT, reads=["zb", "masks"], writes=["PSBa"])
                    p.op("dve", lambda e: e.tensor_copy(z44[:, ci], P6a4),
                         reads=["PSBa"], writes=[f"z4{ci}"])
                A(u_zT)

                def u_V0():
                    p.op("dve", lambda e: e.tensor_tensor(
                        S["Vp"][0][:], mid0(identb[:], 4), S["Mt4"][:], AX.subtract),
                        reads=["masks", kMt], writes=[f"Vp{sx}0"])
                    def fMT(e):
                        r = None
                        for vh in range(4):
                            r = e.transpose(P6[:, vh * 128:(vh + 1) * 128], S["Mt4"][:, vh, :],
                                            identb[:])
                        return r
                    p.op("pe", fMT, reads=[kMt, "masks"], writes=["PSBa"])
                    p.op("act", lambda e: e.copy(S["Nb"][0][:], P6a4),
                         reads=["PSBa"], writes=[f"Nb{sx}0"])
                A(u_V0)

                for j in range(1, 7):
                    tb_p, tkp = ((S["Mt4"], kMt) if j == 1
                                 else (S["Tb"][(j - 1) % 2], f"Tb{sx}{(j - 1) % 2}"))
                    nb_p, nkp = S["Nb"][(j - 1) % 2], f"Nb{sx}{(j - 1) % 2}"
                    vp_p, vkp = S["Vp"][(j - 1) % 2], f"Vp{sx}{(j - 1) % 2}"
                    ji = j % 2
                    if j < 6:
                        def u_sqT(nb_p=nb_p, tb_p=tb_p, nkp=nkp, tkp=tkp, ji=ji):
                            def fsqT(e):
                                r = None
                                for vh in range(4):
                                    r = e.matmul(PV2[:, vh, :], nb_p[:, vh, :], tb_p[:, vh, :],
                                                 start=True, stop=True)
                                return r
                            p.op("pe", fsqT, reads=[nkp, tkp], writes=["PS2"])
                            p.op("dve", lambda e: e.tensor_copy(S["Tb"][ji][:], PV2[:]),
                                 reads=["PS2"], writes=[f"Tb{sx}{ji}"])
                        A(u_sqT)

                    def u_sqN(nb_p=nb_p, tb_p=tb_p, nkp=nkp, tkp=tkp, ji=ji):
                        def fsqN(e):
                            r = None
                            for vh in range(4):
                                r = e.matmul(PV3[:, vh, :], tb_p[:, vh, :], nb_p[:, vh, :],
                                             start=True, stop=True)
                            return r
                        p.op("pe", fsqN, reads=[nkp, tkp], writes=["PS3"])
                        p.op("act", lambda e: e.copy(S["Nb"][ji][:], PV3[:]),
                             reads=["PS3"], writes=[f"Nb{sx}{ji}"])
                    A(u_sqN)

                    def u_Vm(vp_p=vp_p, vkp=vkp, ji=ji, j=j):
                        def fVm(e):
                            r = None
                            for vh in range(4):
                                r = e.matmul(PV4[:, vh, :], S["Nb"][ji][:, vh, :],
                                             vp_p[:, vh, :], start=True, stop=True)
                            return r
                        p.op("pe", fVm, reads=[f"Nb{sx}{ji}", vkp], writes=["PS4"])
                        if j < 6:
                            p.op("dve", lambda e: e.tensor_tensor(
                                S["Vp"][ji][:], vp_p[:], PV4[:], AX.add),
                                reads=["PS4", vkp], writes=[f"Vp{sx}{ji}"])
                        else:
                            p.op("dve", lambda e: e.tensor_tensor(
                                W4[:, ci], vp_p[:], PV4[:], AX.add),
                                reads=["PS4", vkp], writes=[f"W4{ci}"])
                    A(u_Vm)
                return st

            PV0, PV1 = PV[0], PV[1]

            def chain_emit(b, tt, ci):
                pt = tt % 2
                ch = tt * CPT + ci
                op = ch % 2
                first = (ch == 0)
                if not first:
                    def fks(e):
                        r = None
                        for vh in range(4):
                            r = e.matmul(PV0[:, vh, :], kh4[:, ci, vh // 2, :], S_b[:, vh, :],
                                         start=True, stop=True)
                        return r
                    p.op("pe", fks, reads=[f"kh{ci}", "Sb"], writes=["PS0"])
                    def fqs(e):
                        r = None
                        for vh in range(4):
                            r = e.matmul(PV1[:, vh, :], qh4[:, ci, vh // 2, :], S_b[:, vh, :],
                                         start=True, stop=True)
                        return r
                    p.op("pe", fqs, reads=[f"qh{ci}", "Sb"], writes=["PS1"])
                    p.op("act", lambda e: e.copy(QS4[:, ci], PV1[:]),
                         reads=["PS1"], writes=[f"QS{ci}"])
                    p.op("dve", lambda e: e.tensor_tensor(
                        Y0f[:], PV0[:], s0(lamn44[:, ci, :], 128), AX.mult),
                        reads=["PS0", f"lamn{ci}"], writes=["Y0f"])
                    p.op("dve", lambda e: e.tensor_tensor(
                        Y0b[:], Y0f[:], V44[:, ci], AX.add),
                        reads=["Y0f", f"V4{ci}"], writes=["Y0b"])
                else:
                    cop("dve", Y0b[:], V44[:, ci], [f"V4{ci}"], ["Y0b"])
                def fyW(e):
                    r = None
                    for vh in range(4):
                        r = e.matmul(PV0[:, vh, :], W4[:, ci, vh, :], Y0b[:, vh, :],
                                     start=True, stop=True)
                    return r
                p.op("pe", fyW, reads=[f"W4{ci}", "Y0b"], writes=["PS0"])
                p.op("dve", lambda e: e.tensor_tensor(
                    U4[:, ci], PV0[:], s0(bbuf[pt][:, ci, 0:4], 128), AX.mult),
                    reads=["PS0", f"bbuf{pt}{ci}"], writes=[f"U4{ci}"])
                def fsd(e):
                    r = None
                    for vh in range(4):
                        r = e.matmul(PV0[:, vh, :], Kt44[:, ci, vh, :], U4[:, ci, vh, :],
                                     start=True, stop=True)
                    return r
                p.op("pe", fsd, reads=[f"Kt{ci}", f"U4{ci}"], writes=["PS0"])
                if first:
                    p.op("dve", lambda e: e.tensor_copy(S_f[:], PV0[:]),
                         reads=["PS0"], writes=["Sf"])
                else:
                    p.op("dve", lambda e: e.tensor_tensor(
                        S_f[:], S_f[:], s0(LamB44[:, ci, :], 128), AX.mult),
                        reads=["Sf", f"LamB{ci}"], writes=["Sf"])
                    p.op("dve", lambda e: e.tensor_tensor(S_f[:], S_f[:], PV0[:], AX.add),
                         reads=["Sf", "PS0"], writes=["Sf"])
                cop("act", S_b[:], S_f[:], ["Sf"], ["Sb"])
                def fo(e):
                    r = None
                    for vh in range(4):
                        r = e.matmul(PV1[:, vh, :], At44[:, ci, vh, :], U4[:, ci, vh, :],
                                     start=True, stop=True)
                    return r
                p.op("pe", fo, reads=[f"At{ci}", f"U4{ci}", f"QS{ci}"], writes=["PS1"])
                ko = f"o4{op}"
                if first:
                    p.op("dve", lambda e: e.tensor_copy(o4[op][:], PV1[:]),
                         reads=["PS1"], writes=[ko])
                else:
                    p.op("dve", lambda e: e.tensor_tensor(
                        o4[op][:], QS4[:, ci], s0(lam44[:, ci, :], 128), AX.mult),
                        reads=[f"QS{ci}", f"lam{ci}"], writes=[ko])
                    p.op("dve", lambda e: e.tensor_tensor(o4[op][:], o4[op][:], PV1[:], AX.add),
                         reads=[ko, "PS1"], writes=[ko])
                p.op("dve", lambda e: e.tensor_tensor(o4[op][:], o4[op][:], z44[:, ci], AX.mult),
                     reads=[ko, f"z4{ci}"], writes=[ko])
                p.op("act", lambda e: e.activation(scro[op][:], o4[op][:], AF.Square),
                     reads=[ko], writes=[f"scro{op}"])
                r6 = dataclasses.replace(ssum4[op][:], ap=list(ssum4[op][:].ap) + [[1, 1]])
                p.op("dve", lambda e: e.reduce_sum(r6, scro[op][:], AL.X),
                     reads=[f"scro{op}"], writes=[f"ssum{op}"])
                p.op("dve", lambda e: e.tensor_scalar(rstd4[op][:], ssum4[op][:], 1.0 / 128,
                                                      1e-6, AX.mult, AX.add),
                     reads=[f"ssum{op}"], writes=[f"rstd{op}"])
                p.op("act", lambda e: e.activation(rstd4[op][:], rstd4[op][:], AF.Sqrt),
                     reads=[f"rstd{op}"], writes=[f"rstd{op}"])
                p.op("dve", lambda e: e.reciprocal(rstd4[op][:], rstd4[op][:]),
                     reads=[f"rstd{op}"], writes=[f"rstd{op}"])
                p.op("dve", lambda e: e.tensor_tensor(
                    o4[op][:], o4[op][:], s0(rstd4[op][:], 128), AX.mult),
                    reads=[ko, f"rstd{op}"], writes=[ko])
                p.op("dve", lambda e: e.tensor_tensor(
                    ogb[op][:], o4[op][:], mid0(nw_bc[:], 4), AX.mult),
                    reads=[ko, "nwbc"], writes=[f"ogb{op}"])
                def ftro(e):
                    r = None
                    for vh in range(4):
                        r = e.transpose(P6[:, 512 + vh * 128:512 + (vh + 1) * 128],
                                        ogb[op][:, vh, :], identb[:])
                    return r
                p.op("pe", ftro, reads=[f"ogb{op}", "masks"], writes=["PSBb"])
                p.op("dve", lambda e: e.tensor_copy(
                    ogt[op][:], P6[:, 512:1024].rearrange("p (a c) -> p a c", a=4)),
                    reads=["PSBb"], writes=[f"ogt{op}"])
                dst = og_my[b].ap().rearrange("(v d) t -> d v t", v=4)[:, :, ch * 128:(ch + 1) * 128]
                p.op("pool", lambda e, dst=dst: e.dma_start(out=dst, in_=ogt[op][:]),
                     reads=[f"ogt{op}"], writes=[f"ogd{b}_{ch}"], dma=f"d_og{b}_{op}")

            for b in range(B):
                for tt in range(NTPB):
                    proj_ttile(b, tt)
                    streams = [prep_steps(b, tt, ci) for ci in range(CPT)]

                    def drain_pair(pair):
                        ix = {c2: 0 for c2 in pair}
                        alive = True
                        while alive:
                            alive = False
                            for c2 in pair:
                                if ix[c2] < len(streams[c2]):
                                    streams[c2][ix[c2]]()
                                    ix[c2] += 1
                                    alive = True
                    drain_pair((0, 1))
                    chain_emit(b, tt, 0)
                    chain_emit(b, tt, 1)
                    drain_cast(5)
                    drain_pair((2, 3))
                    chain_emit(b, tt, 2)
                    chain_emit(b, tt, 3)
                    drain_cast(5)
                def fag(e, b=b):
                    return e.collective_compute(
                        "AllGather", AX.bypass, replica_groups=[list(range(NC8))],
                        ins=[og_my[b].ap().opt()], outs=[og_ag[b].ap().opt()])
                p.op("pool", fag, reads=[f"ogd{b}_{ch}" for ch in range(NCH)],
                     writes=[f"ogag{b}"], cc=True)
            drain_cast(len(cast_q) + 1)

        p.barrier()
        with ExitStack() as p2:
            sb2 = lambda nm, shape, dty=F32: p2.enter_context(nc.sbuf_tensor(nm, shape, dty))
            x1f = sb2("x1f", [128, KPB, 2048], BF16)
            yf = sb2("yf", [128, DPB, 512], F32)
            vb2 = sb2("vb2", [128, 512], BF16)
            wstb = sb2("wstb", [128, 32, 128], BF16)
            cmb_t = sb2("cmb_t", [128, 8], F32)
            cmb_b = sb2("cmb_b", [128, 8], BF16)
            cmbfm2 = sb2("cmbfm2", [8, 2048], BF16)
            sel8 = sb2("sel8", [8, 8 * 128], BF16)
            m8 = sb2("m8", [128, 2], F32)

            def fsel(e):
                r = None
                for ei in range(8):
                    r = e.tensor_scalar(sel8[:, ei * 128:(ei + 1) * 128],
                                        s0(pramp[0:8, 0:1], 128), float(ei), None, AX.is_equal)
                return r
            p.op("dve", fsel, reads=["pramp"], writes=["sel8"])

            def ln_fm(vget, vkeys, emit_out):
                for cpb in range(DPB):
                    cop("dve", vb2[:], vget(cpb), vkeys, ["vb2"])
                    p.op("pe", lambda e, cpb=cpb: e.matmul(
                        P[5][0:1, 0:512], ones_cb[:], vb2[:],
                        start=(cpb == 0), stop=(cpb == DPB - 1)), reads=["vb2"], writes=["PS5"])
                    p.op("act", lambda e: e.activation(vb2[:], vb2[:], AF.Square),
                         reads=["vb2"], writes=["vb2"])
                    p.op("pe", lambda e, cpb=cpb: e.matmul(
                        P[4][0:1, 0:512], ones_cb[:], vb2[:],
                        start=(cpb == 0), stop=(cpb == DPB - 1)), reads=["vb2"], writes=["PS4"])
                p.op("dve", lambda e: e.tensor_scalar(stat_m[0:1, :], P[5][0:1, 0:512],
                                                      1.0 / 2048, None, AX.mult),
                     reads=["PS5"], writes=["statm"])
                p.op("dve", lambda e: e.tensor_scalar(stat_q[0:1, :], P[4][0:1, 0:512],
                                                      1.0 / 2048, None, AX.mult),
                     reads=["PS4"], writes=["statq"])
                p.op("dve", lambda e: e.tensor_tensor(stat_v[0:1, :], stat_m[0:1, :],
                                                      stat_m[0:1, :], AX.mult),
                     reads=["statm"], writes=["statv"])
                p.op("dve", lambda e: e.tensor_tensor(stat_v[0:1, :], stat_q[0:1, :],
                                                      stat_v[0:1, :], AX.subtract),
                     reads=["statq", "statv"], writes=["statv"])
                p.op("act", lambda e: e.activation(stat_v[0:1, :], stat_v[0:1, :],
                                                   AF.Sqrt, bias=epsB[0:1, 0:1]),
                     reads=["statv", "masks"], writes=["statv"])
                p.op("dve", lambda e: e.reciprocal(stat_v[0:1, :], stat_v[0:1, :]),
                     reads=["statv"], writes=["statv"])
                p.op("pe", lambda e: e.matmul(P[5][:, 0:512], ones_rf[:], stat_m[:],
                                              start=True, stop=True),
                     reads=["statm"], writes=["PS5"])
                p.op("pe", lambda e: e.matmul(P[4][:, 0:512], ones_rf[:], stat_v[:],
                                              start=True, stop=True),
                     reads=["statv"], writes=["PS4"])
                cop("dve", r_s[:], P[4][:, 0:512], ["PS4"], ["rs"])
                for cpb in range(DPB):
                    p.op("dve", lambda e, cpb=cpb: e.tensor_tensor(
                        scr2[:], vget(cpb), P[5][:, 0:512], AX.subtract),
                        reads=list(vkeys) + ["PS5"], writes=["scr2"])
                    emit_out(cpb)

            p2a = p2.enter_context(ExitStack())
            ogsb = p2a.enter_context(nc.sbuf_tensor("ogsb", [128, 32, 512], BF16))
            for nb in range(NB):
                def fldog(e, sem, inc, nb=nb):
                    for arm in e.switch_core_id(n=128):
                        qv = int(arm) % 8
                        bq, soff = qv // 2, (qv % 2) * 2048
                        col = soff + nb * 512
                        e.dma_start(out=ogsb[:], in_=og_ag[bq].ap().rearrange(
                            "(k d) t -> d k t", d=128)[:, :, col:col + 512]).then_inc(sem, inc)
                p.op("sp", fldog, reads=[f"ogag{bq}" for bq in range(B)],
                     writes=["ogsb"], dma="d_ogsb", selfinc=True)
                def fldxr(e, sem, inc, nb=nb):
                    for arm in e.switch_core_id(n=128):
                        qv = int(arm) % 8
                        xcol = (qv // 2) * T + (qv % 2) * 2048 + nb * 512
                        e.dma_start(out=yf[:], in_=xT.rearrange("(k d) t -> d k t", d=128)
                                    [:, :, xcol:xcol + 512]).then_inc(sem, inc)
                p.op("sp", fldxr, reads=["yfx"], writes=["yfx"], dma="d_yfres", selfinc=True)
                for cpb in range(DPB):
                    p.op("sp", lambda e, cpb=cpb: e.dma_start(
                        out=wstb[:], in_=wo_bf[cpb].rearrange("k d c -> d k c")),
                        reads=[f"wobf{cpb}"], writes=["wstb"], dma="d_wstb")
                    ps, pk = P[cpb % 2], PK[cpb % 2]
                    def fwo(e, ps=ps):
                        r = None
                        for k in range(32):
                            r = e.matmul(ps[:], wstb[:, k, :], ogsb[:, k, :],
                                         start=(k == 0), stop=(k == 31))
                        return r
                    p.op("pe", fwo, reads=["wstb", "ogsb"], writes=[pk])
                    p.op("dve", lambda e, cpb=cpb, ps=ps: e.tensor_tensor(
                        yf[:, cpb, :], yf[:, cpb, :], ps[:], AX.add),
                        reads=[pk, "yfx"], writes=["yfx"])

                def emit_x1(cpb, nb=nb):
                    p.op("dve", lambda e, cpb=cpb: e.tensor_tensor(
                        x1f[:, cpb, nb * 512:(nb + 1) * 512], scr2[:], r_s[:], AX.mult),
                        reads=["scr2", "rs"], writes=["x1f"])
                ln_fm(lambda cpb: yf[:, cpb, :], ["yfx"], emit_x1)

            for ttk in range(16):
                def flg(e, ttk=ttk):
                    r = None
                    for k in range(KPB):
                        r = e.matmul(P[5][:, 0:8], x1f[:, k, ttk * 128:(ttk + 1) * 128],
                                     gatew_b[:, k, :], start=(k == 0), stop=(k == KPB - 1))
                    return r
                p.op("pe", flg, reads=["x1f", "gatewb"], writes=["PS5"])
                p.op("dve", lambda e: e.reduce_max(m8[:, 0:1], P[5][:, 0:8], AL.X),
                     reads=["PS5"], writes=["m8"])
                p.op("dve", lambda e: e.tensor_scalar(m8[:, 0:1], m8[:, 0:1], -1.0, None, AX.mult),
                     reads=["m8"], writes=["m8"])
                p.op("act", lambda e: e.activation(cmb_t[:], P[5][:, 0:8], AF.Exp,
                                                   bias=m8[:, 0:1]),
                     reads=["PS5", "m8"], writes=["cmbt"])
                p.op("dve", lambda e: e.reduce_sum(m8[:, 1:2], cmb_t[:], AL.X),
                     reads=["cmbt"], writes=["m8b"])
                p.op("dve", lambda e: e.reciprocal(m8[:, 1:2], m8[:, 1:2]),
                     reads=["m8b"], writes=["m8b"])
                p.op("dve", lambda e: e.tensor_scalar(cmb_t[:], cmb_t[:], m8[:, 1:2], None, AX.mult),
                     reads=["cmbt", "m8b"], writes=["cmbt"])
                p.op("dve", lambda e: e.reduce_max(m8[:, 0:1], cmb_t[:], AL.X),
                     reads=["cmbt"], writes=["m8"])
                p.op("dve", lambda e: e.tensor_scalar(scr2[:, 0:8], cmb_t[:], m8[:, 0:1],
                                                      None, AX.is_ge),
                     reads=["cmbt", "m8"], writes=["scr2"])
                p.op("dve", lambda e: e.scalar_tensor_tensor(scr2[:, 0:8], scr2[:, 0:8], -1e30,
                                                             cmb_t[:], AX.mult, AX.add),
                     reads=["scr2", "cmbt"], writes=["scr2"])
                p.op("dve", lambda e: e.reduce_max(m8[:, 1:2], scr2[:, 0:8], AL.X),
                     reads=["scr2"], writes=["m8b"])
                p.op("dve", lambda e: e.tensor_scalar(scr2[:, 0:8], cmb_t[:], m8[:, 1:2],
                                                      None, AX.is_ge),
                     reads=["cmbt", "m8b"], writes=["scr2"])
                p.op("dve", lambda e: e.tensor_tensor(cmb_b[:], cmb_t[:], scr2[:, 0:8], AX.mult),
                     reads=["cmbt", "scr2"], writes=["cmbb"])
                p.op("pe", lambda e, ttk=ttk: e.transpose(P6[0:8, 0:128],
                                                          cmb_b[:], identb[:]),
                     reads=["cmbb", "masks"], writes=["PS6"])
                cop("dve", cmbfm2[:, ttk * 128:(ttk + 1) * 128], P6[0:8, 0:128],
                    ["PS6"], ["cmbfm"])
            p2a.close()

            p2b = p2.enter_context(ExitStack())
            sb2b = lambda nm, shape, dty=F32: p2b.enter_context(nc.sbuf_tensor(nm, shape, dty))
            cb_e = sb2b("cb_e", [128, 2048], BF16)
            sgb = sb2b("sgb", [128, NB, 512], BF16)
            ggb = sb2b("ggb", [128, NB, 512], BF16)
            wsg = [sb2b(f"wsg{i}", [128, 16, 128], F32) for i in range(2)]
            wsb = [sb2b(f"wsb{i}", [128, 16, 128], BF16) for i in range(2)]

            ucnt = [0]
            for gi in range(NGI):
                is_sh = gi >= E * HPB_R
                e_i = gi // HPB_R if not is_sh else None
                hpb = gi % HPB_R if not is_sh else gi - E * HPB_R
                if (not is_sh) and hpb == 0:
                    for nb2 in range(NB):
                        p.op("pe", lambda e, e_i=e_i, nb2=nb2: e.matmul(
                            P[4][:, 0:512], sel8[:, e_i * 128:(e_i + 1) * 128],
                            cmbfm2[:, nb2 * 512:(nb2 + 1) * 512], start=True, stop=True),
                            reads=["cmbfm", "sel8", "cbe"], writes=["PS4"])
                        cop("dve", cb_e[:, nb2 * 512:(nb2 + 1) * 512], P[4][:, 0:512],
                            ["PS4"], ["cbe"] + (["ogsb", "wstb"] if gi == 0 else []))
                wsrc_g = wg[e_i, hpb] if not is_sh else swg[hpb]
                wsrc_u = wu[e_i, hpb] if not is_sh else swu[hpb]
                for phase, wsrc in (("g", wsrc_g), ("u", wsrc_u)):
                    par = ucnt[0] % 2
                    guard = ["ogsb", "wstb"] if ucnt[0] < 2 else []
                    ucnt[0] += 1
                    wsgX, wsgK = wsg[par], f"wsg{par}"
                    wsbX, wsbK = wsb[par], f"wsb{par}"
                    p.op("sp", lambda e, wsrc=wsrc, wsgX=wsgX: e.dma_start(
                        out=wsgX[:], in_=wsrc.rearrange("k d c -> d k c")),
                        writes=[wsgK] + guard, dma=f"d_wstg{par}")
                    cop(ev(), wsbX[:], wsgX[:], [wsgK], [wsbK] + guard)
                    def fgu(e, wsbX=wsbX):
                        r = None
                        for k in range(KPB):
                            for nb2 in range(NB):
                                r = e.matmul(P[nb2][:],
                                             wsbX[:, k, :], x1f[:, k, nb2 * 512:(nb2 + 1) * 512],
                                             start=(k == 0), stop=(k == KPB - 1))
                        return r
                    p.op("pe", fgu, reads=[wsbK, "x1f"],
                         writes=["PS0", "PS1", "PS2", "PS3"])
                    if phase == "g":
                        for nb2 in range(NB):
                            p.op("act", lambda e, nb2=nb2: e.activation(
                                sgb[:, nb2, :], P[nb2][:], AF.Silu),
                                reads=[PK[nb2]], writes=["sgb"] + guard)
                    else:
                        for nb2 in range(NB):
                            p.op("dve", lambda e, nb2=nb2: e.tensor_tensor(
                                ggb[:, nb2, :], P[nb2][:], sgb[:, nb2, :], AX.mult),
                                reads=[PK[nb2], "sgb"], writes=["ggb"] + guard)
                        if not is_sh:
                            p.op("pool", lambda e: e.tensor_tensor(
                                ggb[:].rearrange("p a c -> p (a c)"),
                                ggb[:].rearrange("p a c -> p (a c)"), cb_e[:], AX.mult),
                                reads=["ggb", "cbe"], writes=["ggb"])
                p.op("pool", lambda e, gi=gi: e.dma_start(
                    out=g_d[gi], in_=ggb[:].rearrange("p a c -> p (a c)")),
                    reads=["ggb"], writes=[f"gd{gi}"], dma="d_gd")

            p2b.close()
            p2c = p2.enter_context(ExitStack())
            sb2c = lambda nm, shape, dty=F32: p2c.enter_context(nc.sbuf_tensor(nm, shape, dty))
            g_eb = [sb2c(f"g_eb{i}", [128, 11, 512], BF16) for i in range(2)]
            wd_eb = [sb2c(f"wd_eb{i}", [128, 88, 128], BF16) for i in range(2)]
            gA = ["wsg0", "wsg1", "wsb0", "wsb1", "cbe", "sgb", "ggb"]
            for nb in range(NB):
                for eg in range(NEG):
                    gpar = (nb * NEG + eg) % 2
                    gX, gK = g_eb[gpar], f"geb{gpar}"
                    guard = gA if (nb, eg) in ((0, 0), (0, 1)) else []
                    p.op("sp", lambda e, eg=eg, nb=nb, gX=gX: e.dma_start(
                        out=gX[:], in_=g_d[eg * 11:(eg + 1) * 11, :, nb * 512:(nb + 1) * 512]
                        .rearrange("a d c -> d a c")),
                        reads=[f"gd{gg}" for gg in range(eg * 11, (eg + 1) * 11)],
                        writes=[gK] + guard, dma=f"d_geb{gpar}")
                    for hf in range(2):
                        wpar = (nb * NEG * 2 + eg * 2 + hf) % 2
                        wX, wK = wd_eb[wpar], f"wdeb{wpar}"
                        r0 = (eg * 2 + hf) * 88
                        wdkeys = [f"wdbf{(eg * 2 + hf) * 88 + gl * 8}" for gl in range(11)]
                        p.op("sp", lambda e, r0=r0, wX=wX: e.dma_start(
                            out=wX[:], in_=wd_bf[r0:r0 + 88].rearrange("r d c -> d r c")),
                            reads=wdkeys, writes=[wK] + guard, dma=f"d_wdeb{wpar}")
                        for dj in range(8):
                            dpb = hf * 8 + dj
                            ps, pk = P[4 + dpb % 2], PK[4 + dpb % 2]
                            def fdn(e, dj=dj, ps=ps, wX=wX, gX=gX):
                                r = None
                                for gl in range(11):
                                    r = e.matmul(ps[:], wX[:, gl * 8 + dj, :], gX[:, gl, :],
                                                 start=(gl == 0), stop=(gl == 10))
                                return r
                            p.op("pe", fdn, reads=[wK, gK], writes=[pk])
                            if eg == 0:
                                p.op("dve", lambda e, dpb=dpb, ps=ps: e.tensor_copy(
                                    yf[:, dpb, :], ps[:]), reads=[pk], writes=["yfx"])
                            else:
                                p.op("dve", lambda e, dpb=dpb, ps=ps: e.tensor_tensor(
                                    yf[:, dpb, :], yf[:, dpb, :], ps[:], AX.add),
                                    reads=[pk, "yfx"], writes=["yfx"])
                for dpb in range(DPB):
                    p.op("dve", lambda e, dpb=dpb, nb=nb: e.tensor_tensor(
                        yf[:, dpb, :], yf[:, dpb, :],
                        x1f[:, dpb, nb * 512:(nb + 1) * 512], AX.add),
                        reads=["yfx", "x1f"], writes=["yfx"])

                def emit_out(cpb, nb=nb):
                    p.op("dve", lambda e, cpb=cpb: e.tensor_tensor(
                        yf[:, cpb, :], scr2[:], r_s[:], AX.mult),
                        reads=["scr2", "rs"], writes=["yfx"])
                    p.op("pool", lambda e, cpb=cpb, nb=nb: e.dma_start(
                        out=out_fm[cpb * 128:(cpb + 1) * 128, nb * 512:(nb + 1) * 512],
                        in_=yf[:, cpb, :]), reads=["yfx"], writes=[f"outd{nb}_{cpb}"],
                        dma="d_out")
                ln_fm(lambda cpb: yf[:, cpb, :], ["yfx"], emit_out)

            p.barrier()

        p.emit(blk)
    return nc


_NC_CACHE = {}


def _get_nc():
    if "nc" not in _NC_CACHE:
        _NC_CACHE["nc"] = build_nc()
    return _NC_CACHE["nc"]


def _prep_host(inputs):
    f32 = np.float32
    x = np.asarray(inputs["L0_attn_hidden_states"], f32)
    xT = np.ascontiguousarray(x.reshape(B * T, D).T)
    Wq = np.asarray(inputs["L0_attn_W_qkvz"], f32)
    Wba = np.asarray(inputs["L0_attn_W_ba"], f32)
    cw = np.asarray(inputs["L0_attn_conv_w"], f32)
    dtb = np.asarray(inputs["L0_attn_dt_bias"], f32)
    alog = np.asarray(inputs["L0_attn_A_log"], f32)
    nw = np.asarray(inputs["L0_attn_norm_weight"], f32)
    Wo = np.asarray(inputs["L0_attn_W_out"], f32)
    gw = np.asarray(inputs["L0_ffn_gate_w"], f32)
    wg_ = np.asarray(inputs["L0_ffn_w_gate"], f32)
    wu_ = np.asarray(inputs["L0_ffn_w_up"], f32)
    wd_ = np.asarray(inputs["L0_ffn_w_down"], f32)
    swg_ = np.asarray(inputs["L0_ffn_sw_gate"], f32)
    swu_ = np.asarray(inputs["L0_ffn_sw_up"], f32)
    swd_ = np.asarray(inputs["L0_ffn_sw_down"], f32)

    ident_c = np.eye(128, dtype=ml_dtypes.bfloat16)
    rr_, cc_ = np.arange(128)[:, None], np.arange(128)[None, :]
    masku_c = (cc_ >= rr_).astype(f32)
    maskus_c = (cc_ > rr_).astype(f32)
    pramp_c = np.arange(128, dtype=f32)
    wout_t = np.ascontiguousarray(
        Wo.reshape(32, 128, DPB, 128).transpose(2, 0, 1, 3))
    gatew_t = np.ascontiguousarray(gw.reshape(KPB, 128, 8))
    wg_t = np.ascontiguousarray(
        wg_.reshape(E, KPB, 128, HPB_R, 128).transpose(0, 3, 1, 2, 4))
    wu_t = np.ascontiguousarray(
        wu_.reshape(E, KPB, 128, HPB_R, 128).transpose(0, 3, 1, 2, 4))
    wd_t = np.ascontiguousarray(
        wd_.reshape(E, HPB_R, 128, DPB, 128).transpose(0, 1, 3, 2, 4)
        .reshape(E * HPB_R, DPB, 128, 128))
    swg_t = np.ascontiguousarray(swg_.reshape(KPB, 128, HPB_S, 128).transpose(2, 0, 1, 3))
    swu_t = np.ascontiguousarray(swu_.reshape(KPB, 128, HPB_S, 128).transpose(2, 0, 1, 3))
    swd_t = np.ascontiguousarray(swd_.reshape(HPB_S, 128, DPB, 128).transpose(0, 2, 1, 3))

    in_maps = []
    for c in range(NC8):
        khs = [2 * c, 2 * c + 1]
        vhs = [4 * c + i for i in range(4)]
        cols = []
        for kh in khs:
            cols.append(np.arange(kh * 768, kh * 768 + 128))
        for kh in khs:
            cols.append(np.arange(kh * 768 + 128, kh * 768 + 256))
        for vh in vhs:
            cols.append(np.arange((vh // 2) * 768 + 256 + (vh % 2) * 128,
                                  (vh // 2) * 768 + 256 + (vh % 2) * 128 + 128))
        for vh in vhs:
            cols.append(np.arange((vh // 2) * 768 + 512 + (vh % 2) * 128,
                                  (vh // 2) * 768 + 512 + (vh % 2) * 128 + 128))
        wq_c = np.ascontiguousarray(
            np.stack([Wq[:, ci] for ci in cols], 0).reshape(12, KPB, 128, 128))
        bacols = [4 * (vh // 2) + (vh % 2) for vh in vhs] + \
                 [4 * (vh // 2) + 2 + (vh % 2) for vh in vhs]
        wba_c = np.ascontiguousarray(Wba[:, bacols].reshape(KPB, 128, 8))
        convrows = np.concatenate(
            [np.arange(kh * 128, (kh + 1) * 128) for kh in khs] +
            [np.arange(2048 + kh * 128, 2048 + (kh + 1) * 128) for kh in khs] +
            [np.arange(4096 + vh * 128, 4096 + (vh + 1) * 128) for vh in vhs])
        convw_c = np.ascontiguousarray(cw[convrows].reshape(8, 128, 4))
        dtb8 = np.concatenate([np.zeros(4, f32), dtb[vhs]])
        alog8 = np.concatenate([np.zeros(4, f32), alog[vhs]])
        in_maps.append({
            "xT": xT, "wqkvz": wq_c, "wba": wba_c, "convw": convw_c,
            "dtb8": dtb8, "alog8": alog8, "nwin": nw, "wout": wout_t,
            "gatew": gatew_t, "wg": wg_t, "wu": wu_t, "wdn": wd_t,
            "swg": swg_t, "swu": swu_t, "swd": swd_t,
            "ident_in": ident_c, "masku_in": masku_c, "maskus_in": maskus_c,
            "pramp_in": pramp_c,
        })
    return in_maps


def kernel(**inputs):
    nc = _get_nc()
    in_maps = _prep_host(inputs)
    res = run_bass_kernel_spmd(nc, in_maps, core_ids=list(range(NC8)))
    out = np.zeros((B * T, D), np.float32)
    for c in range(NC8):
        q = int(round(float(res.results[c]["myq"][0, 0])))
        out[q * 2048:(q + 1) * 2048, :] = res.results[c]["out_fm"].T
    return out.reshape(B, T, D)


# version 17
# speedup vs baseline: 1.0374x; 1.0196x over previous
import dataclasses
from contextlib import ExitStack
import numpy as np
import ml_dtypes
import concourse.bass as bass
import concourse.mybir as mybir
from concourse.bass_utils import run_bass_kernel_spmd

F32 = mybir.dt.float32
BF16 = mybir.dt.bfloat16
AX = mybir.AluOpType
AF = mybir.ActivationFunctionType
AL = mybir.AxisListType

B, T, D = 4, 4096, 2048
NTPB, TT = 8, 512
C, NCH, CPT = 128, 32, 4
NC8 = 8
E, HPB_R, HPB_S = 8, 11, 22
DPB, KPB, NB = 16, 16, 4
NGI = E * HPB_R + HPB_S
NEG = 10


def s0(ap, n):
    return dataclasses.replace(ap, ap=list(ap.ap) + [[0, n]])


def mid0(ap, n):
    a = list(ap.ap)
    return dataclasses.replace(ap, ap=[a[0], [0, n]] + a[1:])


def dup2(ap):
    a = list(ap.ap)
    return dataclasses.replace(ap, ap=[a[0], a[1], [0, 2], a[2]])


class Prog:
    ENGS = ("pe", "dve", "act", "pool", "sp")

    def __init__(self, nc, ctx):
        self.strict = True
        self.nc = nc
        self.q = {e: [] for e in self.ENGS}
        self.ctx = ctx
        self.sem, self.cnt = {}, {}
        for s in self.ENGS + ("cc",):
            self.sem[s] = ctx.enter_context(nc.semaphore("sem_" + s))
            self.cnt[s] = 0
        self.seen = {e: {} for e in self.ENGS}
        self.bufw, self.bufr = {}, {}

    def _need(self, eng, dep, waits, seen):
        sem, val = dep
        if sem == eng and (eng == "pe" or (eng != "act" and not self.strict)):
            return
        if seen.get(sem, 0) >= val:
            return
        waits.append((sem, val))
        seen[sem] = val

    def op(self, eng, fn, reads=(), writes=(), dma=None, cc=False, selfinc=False):
        waits, seen = [], self.seen[eng]
        for b in reads:
            w = self.bufw.get(b)
            if w:
                self._need(eng, w, waits, seen)
        for b in writes:
            w = self.bufw.get(b)
            if w:
                self._need(eng, w, waits, seen)
            for r in self.bufr.get(b, ()):
                self._need(eng, r, waits, seen)
        if dma:
            sname, step = dma, 16
            if sname not in self.sem:
                self.sem[sname] = self.ctx.enter_context(
                    self.nc.semaphore("sem_" + sname))
                self.cnt[sname] = 0
        elif cc:
            sname, step = "cc", 1
        else:
            sname, step = eng, 1
        self.cnt[sname] += step
        val = self.cnt[sname]
        wl = [(self.sem[s], v) for s, v in waits]
        sem_obj = self.sem[sname]

        inc = 16 if dma else step
        if selfinc:
            def run(engine, fn=fn, wl=wl, sem_obj=sem_obj, inc=inc):
                for s, v in wl:
                    engine.wait_ge(s, v)
                fn(engine, sem_obj, inc)
        else:
            def run(engine, fn=fn, wl=wl, sem_obj=sem_obj, inc=inc):
                for s, v in wl:
                    engine.wait_ge(s, v)
                inst = fn(engine)
                if inst is not None:
                    inst.then_inc(sem_obj, inc)

        self.q[eng].append(run)
        for b in writes:
            self.bufw[b] = (sname, val)
            self.bufr[b] = []
        for b in reads:
            self.bufr.setdefault(b, []).append((sname, val))

    def barrier(self):
        snap = dict(self.cnt)
        for e in self.ENGS:
            todo = []
            for s, v in snap.items():
                if (s == e and not self.strict) or v == 0 or \
                        self.seen[e].get(s, 0) >= v:
                    continue
                todo.append((self.sem[s], v))
                self.seen[e][s] = v

            def run(engine, todo=todo):
                for s, v in todo:
                    engine.wait_ge(s, v)
            self.q[e].append(run)
        self.bufw, self.bufr = {}, {}

    def emit(self, blk):
        qq = self.q
        blk.tensor(lambda e: [f(e) for f in qq["pe"]] and None)
        blk.vector(lambda e: [f(e) for f in qq["dve"]] and None)
        blk.scalar(lambda e: [f(e) for f in qq["act"]] and None)
        blk.gpsimd(lambda e: [f(e) for f in qq["pool"]] and None)
        blk.sync(lambda e: [f(e) for f in qq["sp"]] and None)


def build_nc():
    nc = bass.Bass("TRN2", target_bir_lowering=False)
    dt = nc.dram_tensor
    xT = dt("xT", [D, B * T], F32, kind="ExternalInput").ap()
    wqkvz = dt("wqkvz", [12, KPB, 128, 128], F32, kind="ExternalInput").ap()
    wba = dt("wba", [KPB, 128, 8], F32, kind="ExternalInput").ap()
    convw = dt("convw", [8, 128, 4], F32, kind="ExternalInput").ap()
    dtb8 = dt("dtb8", [8], F32, kind="ExternalInput").ap()
    alog8 = dt("alog8", [8], F32, kind="ExternalInput").ap()
    nwin = dt("nwin", [128], F32, kind="ExternalInput").ap()
    wout = dt("wout", [DPB, 32, 128, 128], F32, kind="ExternalInput").ap()
    gatew = dt("gatew", [KPB, 128, 8], F32, kind="ExternalInput").ap()
    wg = dt("wg", [E, HPB_R, KPB, 128, 128], F32, kind="ExternalInput").ap()
    wu = dt("wu", [E, HPB_R, KPB, 128, 128], F32, kind="ExternalInput").ap()
    wdn = dt("wdn", [E * HPB_R, DPB, 128, 128], F32, kind="ExternalInput").ap()
    swg = dt("swg", [HPB_S, KPB, 128, 128], F32, kind="ExternalInput").ap()
    swu = dt("swu", [HPB_S, KPB, 128, 128], F32, kind="ExternalInput").ap()
    swd = dt("swd", [HPB_S, DPB, 128, 128], F32, kind="ExternalInput").ap()
    out_fm = dt("out_fm", [D, 2048], F32, kind="ExternalOutput").ap()
    myq = dt("myq", [1, 1], F32, kind="ExternalOutput").ap()
    ident_in = dt("ident_in", [128, 128], BF16, kind="ExternalInput").ap()
    masku_in = dt("masku_in", [128, 128], F32, kind="ExternalInput").ap()
    maskus_in = dt("maskus_in", [128, 128], F32, kind="ExternalInput").ap()
    pramp_in = dt("pramp_in", [128], F32, kind="ExternalInput").ap()
    og_my = [dt(f"og_my_{b}", [512, T], BF16) for b in range(B)]
    og_ag = [dt(f"og_ag_{b}", [NC8 * 512, T], BF16) for b in range(B)]
    wd_bf = dt("wd_bf", [NGI * DPB, 128, 128], BF16).ap()
    wo_bf = dt("wo_bf", [DPB, 32, 128, 128], BF16).ap()
    g_d = dt("g_d", [NGI, 128, 2048], BF16).ap()

    with ExitStack() as ctx:
        p = Prog(nc, ctx)
        sb = lambda nm, shape, dty=F32: ctx.enter_context(nc.sbuf_tensor(nm, shape, dty))
        psum = lambda nm, shape, dty=F32: ctx.enter_context(nc.psum_tensor(nm, shape, dty))

        identb = sb("identb", [128, 128], BF16)
        maskUi = sb("maskUi", [128, 128], F32)
        maskUs = sb("maskUs", [128, 128], F32)
        ones_cf = sb("ones_cf", [128, 1], F32)
        ones_cb = sb("ones_cb", [128, 1], BF16)
        ones_rf = sb("ones_rf", [1, 128], F32)
        pramp = sb("pramp", [128, 1], F32)
        nw_bc = sb("nw_bc", [128, 128], F32)
        row128 = sb("row128", [1, 128], F32)
        dtb_bc = sb("dtb_bc", [128, 8], F32)
        nea_bc = sb("nea_bc", [128, 8], F32)
        vec8 = sb("vec8", [1, 8], F32)
        gatew_b = sb("gatew_b", [128, KPB, 8], BF16)
        tiny = sb("tiny", [1, 1], F32)
        stat_m = sb("stat_m", [1, 512], F32)
        stat_q = sb("stat_q", [1, 512], F32)
        stat_v = sb("stat_v", [1, 512], F32)
        epsA = sb("epsA", [1, 1], F32)
        epsB = sb("epsB", [1, 1], F32)
        r_s = sb("r_s", [128, 512], F32)
        scr2 = sb("scr2", [128, 512], F32)

        P = [psum(f"PS{i}", [128, 512], F32) for i in range(7)]
        P6 = psum("PSB", [128, 1024], BF16)
        PV = [P[i][:].rearrange("p (a c) -> p a c", a=4) for i in range(7)]
        PK = ["PS0", "PS1", "PS2", "PS3", "PS4", "PS5", "PSX"]

        blk = ctx.enter_context(nc.Block())
        evt = [0]

        def ev():
            evt[0] ^= 1
            return "dve" if evt[0] else "act"

        def cop(eng, dst, src, rd, wr):
            if eng == "act":
                p.op("act", lambda e, d=dst, s=src: e.copy(d, s), reads=rd, writes=wr)
            else:
                p.op(eng, lambda e, d=dst, s=src: e.tensor_copy(d, s), reads=rd, writes=wr)

        p.op("sp", lambda e: e.dma_start(out=identb[:], in_=ident_in),
             writes=["masks"], dma="d_ini0")
        p.op("sp", lambda e: e.dma_start(out=maskUi[:], in_=masku_in),
             writes=["masks"], dma="d_ini1")
        p.op("sp", lambda e: e.dma_start(out=maskUs[:], in_=maskus_in),
             writes=["masks"], dma="d_ini2")
        p.op("sp", lambda e: e.dma_start(
            out=pramp[:, 0], in_=pramp_in),
            writes=["pramp"], dma="d_ini3")
        def f_ones(e):
            e.memset(ones_cf[:], 1.0)
            e.memset(ones_cb[:], 1.0)
            e.memset(epsA[:], 1e-6)
            e.memset(epsB[:], 1e-5)
            return e.memset(ones_rf[:], 1.0)
        p.op("dve", f_ones, writes=["masks"])

        p.op("sp", lambda e: e.dma_start(out=row128[0:1, :], in_=nwin.rearrange("(a b) -> a b", a=1)), writes=["row128"], dma="d_row")
        p.op("pe", lambda e: e.matmul(P[4][:, 0:128], ones_rf[:], row128[:], start=True, stop=True),
             reads=["masks", "row128"], writes=["PS4"])
        cop("dve", nw_bc[:], P[4][:, 0:128], ["PS4"], ["nwbc"])
        p.op("sp", lambda e: e.dma_start(out=vec8[0:1, :], in_=dtb8.rearrange("(a b) -> a b", a=1)), writes=["vec8"], dma="d_vec8")
        p.op("pe", lambda e: e.matmul(P[4][:, 0:8], ones_rf[:], vec8[:], start=True, stop=True),
             reads=["masks", "vec8", "nwbc"], writes=["PS4"])
        cop("dve", dtb_bc[:], P[4][:, 0:8], ["PS4"], ["dtbbc"])
        p.op("sp", lambda e: e.dma_start(out=vec8[0:1, :], in_=alog8.rearrange("(a b) -> a b", a=1)),
             reads=["dtbbc", "vec8"], writes=["vec8"], dma="d_vec8")
        p.op("act", lambda e: e.activation(vec8[0:1, :], vec8[0:1, :], AF.Exp),
             reads=["vec8"], writes=["vec8"])
        p.op("dve", lambda e: e.tensor_scalar(vec8[0:1, :], vec8[0:1, :], -1.0, None, AX.mult),
             reads=["vec8"], writes=["vec8"])
        p.op("pe", lambda e: e.matmul(P[4][:, 0:8], ones_rf[:], vec8[:], start=True, stop=True),
             reads=["vec8", "dtbbc"], writes=["PS4"])
        cop("dve", nea_bc[:], P[4][:, 0:8], ["PS4"], ["neabc"])

        def f_q(e, sem, inc):
            for arm in e.switch_core_id(n=128):
                e.memset(tiny[0:1, 0:1], float(int(arm) % 8)).then_inc(sem, inc)
        p.op("dve", f_q, writes=["tiny"], selfinc=True)
        p.op("sp", lambda e: e.dma_start(out=myq, in_=tiny[0:1, 0:1]),
             reads=["tiny"], writes=["myqd"], dma="d_out")

        with ExitStack() as p1:
            sb1 = lambda nm, shape, dty=F32: p1.enter_context(nc.sbuf_tensor(nm, shape, dty))
            wq_b = sb1("wq_b", [128, 12, KPB, 128], BF16)
            wba_b = sb1("wba_b", [128, KPB, 8], BF16)
            convw_s = sb1("convw_s", [128, 8, 4], F32)
            diag_b = sb1("diag_b", [128, 8, 4, 128], BF16)
            winit = sb1("winit", [128, KPB, 128], F32)
            xstg = sb1("xstg", [128, 2, TT], F32)
            x_bb = sb1("x_bb", [128, KPB, TT], BF16)
            raw = sb1("raw", [128, 8, TT + 4], BF16)
            qkc = sb1("qkc", [128, 4, TT], BF16)
            vc_b = sb1("vc_b", [128, 4, TT], BF16)
            z_b = sb1("z_b", [128, 4, TT], BF16)
            gbuf = sb1("gbuf", [128, CPT, 8], F32)
            bbuf = [sb1(f"bbuf{i}", [128, CPT, 8], F32) for i in range(2)]
            sp_t = sb1("sp_t", [128, 96], F32)
            W4 = sb1("W4", [128, CPT, 4, 128], BF16)
            At44 = sb1("At44", [128, CPT, 4, 128], BF16)
            Kt44 = sb1("Kt44", [128, CPT, 4, 128], BF16)
            V44 = sb1("V44", [128, CPT, 4, 128], BF16)
            z44 = sb1("z44", [128, CPT, 4, 128], BF16)
            kh4 = sb1("kh4", [128, CPT, 2, 128], BF16)
            qh4 = sb1("qh4", [128, CPT, 2, 128], BF16)
            lam44 = sb1("lam44", [128, CPT, 4], F32)
            lamn44 = sb1("lamn44", [128, CPT, 4], F32)
            dec44 = sb1("dec44", [128, CPT, 4], F32)
            LamB44 = sb1("LamB44", [128, CPT, 4], F32)
            bcol4 = sb1("bcol4", [128, CPT, 4], F32)
            brow4 = sb1("brow4", [1, CPT, 512], F32)
            TS = []
            for s_ in range(2):
                TS.append(dict(
                    Dt4=sb1(f"Dt4_{s_}", [128, 4, 128], BF16),
                    scr4=sb1(f"scr4_{s_}", [128, 4, 128], BF16),
                    sq4=sb1(f"sq4_{s_}", [128, 4, 128], F32),
                    Mt4=sb1(f"Mt4_{s_}", [128, 4, 128], BF16),
                    Tb=[sb1(f"Tb{s_}{j}", [128, 4, 128], BF16) for j in range(2)],
                    Nb=[sb1(f"Nb{s_}{j}", [128, 4, 128], BF16) for j in range(2)],
                    Vp=[sb1(f"Vp{s_}{j}", [128, 4, 128], BF16) for j in range(2)],
                    rn=sb1(f"rn_{s_}", [1, 512], F32),
                    bfm=sb1(f"bfm_{s_}", [4, 128], F32),
                ))
            Y0f = sb1("Y0f", [128, 4, 128], F32)
            Y0b = sb1("Y0b", [128, 4, 128], BF16)
            U4 = sb1("U4", [128, CPT, 4, 128], BF16)
            QS4 = sb1("QS4", [128, CPT, 4, 128], BF16)
            S_f = sb1("S_f", [128, 4, 128], F32)
            S_b = sb1("S_b", [128, 4, 128], BF16)
            o4 = [sb1(f"o4_{i}", [128, 4, 128], F32) for i in range(2)]
            scro = [sb1(f"scro_{i}", [128, 4, 128], BF16) for i in range(2)]
            ssum4 = [sb1(f"ssum4_{i}", [128, 4], F32) for i in range(2)]
            rstd4 = [sb1(f"rstd4_{i}", [128, 4], F32) for i in range(2)]
            ogb = [sb1(f"ogb_{i}", [128, 4, 128], BF16) for i in range(2)]
            ogt = [sb1(f"ogt_{i}", [128, 4, 128], BF16) for i in range(2)]

            p.op("sp", lambda e: e.dma_start(out=convw_s[:], in_=convw.rearrange("p d j -> d p j")),
                 writes=["convws"], dma="d_convw")
            for pb in range(8):
                for j in range(4):
                    p.op("dve", lambda e, pb=pb, j=j: e.tensor_scalar(
                        diag_b[:, pb, j, :], identb[:], convw_s[:, pb, j:j + 1], None, AX.mult),
                        reads=["masks", "convws"], writes=["diag"])
            for pb in range(12):
                p.op("sp", lambda e, pb=pb: e.dma_start(
                    out=winit[:], in_=wqkvz[pb].rearrange("k d c -> d k c")),
                    writes=["winit"], dma="d_winit")
                cop(ev(), wq_b[:, pb], winit[:], ["winit"], ["wqb"])
            p.op("sp", lambda e: e.dma_start(
                out=winit[:, :, 0:8], in_=wba.rearrange("k d c -> d k c")),
                writes=["winit"], dma="d_winit")
            cop("dve", wba_b[:], winit[:, :, 0:8], ["winit"], ["wbab"])
            p.op("sp", lambda e: e.dma_start(
                out=winit[:, :, 0:8], in_=gatew.rearrange("k d c -> d k c")),
                writes=["winit"], dma="d_winit")
            cop("dve", gatew_b[:], winit[:, :, 0:8], ["winit"], ["gatewb"])
            p.op("dve", lambda e: e.memset(raw[:, :, 0:4], 0.0),
                 writes=[f"raw{pb}" for pb in range(8)])

            wstbC = sb1("wstbC", [128, 16, 128], BF16)
            cast_q = []
            for cpb in range(DPB):
                for hf in range(2):
                    cast_q.append(lambda cpb=cpb, hf=hf: p.op(
                        "sp", lambda e: e.dma_start(
                            out=winit[:],
                            in_=wout[cpb, hf * 16:(hf + 1) * 16].rearrange("k d c -> d k c")),
                        writes=["winit"], dma="d_winit"))
                    cast_q.append(lambda: cop(ev(), wstbC[:], winit[:],
                                              ["winit"], ["wstbC"]))
                    cast_q.append(lambda cpb=cpb, hf=hf: p.op(
                        "pool", lambda e: e.dma_start(
                            out=wo_bf[cpb, hf * 16:(hf + 1) * 16].rearrange("k d c -> d k c"),
                            in_=wstbC[:]),
                        reads=["wstbC"], writes=[f"wobf{cpb}"], dma="d_wbf"))
            for gi in range(NGI):
                srcap = wdn[gi] if gi < E * HPB_R else swd[gi - E * HPB_R]
                eg, gl = gi // 11, gi % 11
                cast_q.append(lambda srcap=srcap: p.op(
                    "sp", lambda e: e.dma_start(
                        out=winit[:], in_=srcap.rearrange("k d c -> d k c")),
                    writes=["winit"], dma="d_winit"))
                cast_q.append(lambda: cop(ev(), wstbC[:], winit[:],
                                          ["winit"], ["wstbC"]))
                for hf in range(2):
                    r0 = (eg * 2 + hf) * 88 + gl * 8
                    cast_q.append(lambda r0=r0, hf=hf: p.op(
                        "pool", lambda e: e.dma_start(
                            out=wd_bf[r0:r0 + 8].rearrange("k d c -> d k c"),
                            in_=wstbC[:, hf * 8:(hf + 1) * 8, :]),
                        reads=["wstbC"], writes=[f"wdbf{r0}"], dma="d_wbf"))

            def drain_cast(n):
                for _ in range(min(n, len(cast_q))):
                    cast_q.pop(0)()

            def proj_ttile(b, tt):
                t0 = b * T + tt * TT
                pt = tt % 2
                pj = [0]

                def bank():
                    pj[0] ^= 1
                    return (P[5], "PS5") if pj[0] else (P[6], "PSX")
                for kk in range(KPB):
                    sl = kk % 2
                    p.op("sp", lambda e, kk=kk, sl=sl: e.dma_start(
                        out=xstg[:, sl, :], in_=xT[kk * 128:(kk + 1) * 128, t0:t0 + TT]),
                        writes=[f"xstg{sl}"], dma=f"d_x{sl}")
                    cop(ev(), x_bb[:, kk, :], xstg[:, sl, :], [f"xstg{sl}"], ["xbb"])
                for pb in range(12):
                    ps, pk = bank()
                    def fmm(e, pb=pb, ps=ps):
                        r = None
                        for k in range(KPB):
                            r = e.matmul(ps[:], wq_b[:, pb, k, :], x_bb[:, k, :],
                                         start=(k == 0), stop=(k == KPB - 1))
                        return r
                    p.op("pe", fmm, reads=["wqb", "xbb"], writes=[pk])
                    if pb < 8:
                        cop(ev(), raw[:, pb, 4:TT + 4], ps[:], [pk], [f"raw{pb}"])
                    else:
                        p.op("act", lambda e, pb=pb, ps=ps: e.activation(
                            z_b[:, pb - 8, :], ps[:], AF.Silu), reads=[pk], writes=["zb"])
                ps, pk = bank()
                def fba(e, ps=ps):
                    r = None
                    for ci in range(CPT):
                        for k in range(KPB):
                            r = e.matmul(ps[:, ci * 8:(ci + 1) * 8],
                                         x_bb[:, k, ci * 128:(ci + 1) * 128],
                                         wba_b[:, k, :], start=(k == 0), stop=(k == KPB - 1))
                    return r
                p.op("pe", fba, reads=["wbab", "xbb"], writes=[pk])
                ps32 = ps[:, 0:32].rearrange("p (a c) -> p a c", a=4)
                sp32 = sp_t[:, 0:32].rearrange("p (a c) -> p a c", a=4)
                p.op("act", lambda e, ps=ps: e.activation(
                    bbuf[pt][:], ps32, AF.Sigmoid),
                    reads=[pk], writes=[f"bbuf{pt}{ci2}" for ci2 in range(CPT)])
                p.op("dve", lambda e, ps=ps: e.tensor_tensor(
                    sp32, ps32, mid0(dtb_bc[:], 4), AX.add),
                    reads=[pk, "dtbbc"], writes=["sp0"])
                p.op("act", lambda e: e.activation(sp_t[:, 32:64], sp_t[:, 0:32], AF.Abs),
                     reads=["sp0"], writes=["sp1"])
                p.op("act", lambda e: e.activation(sp_t[:, 64:96], sp_t[:, 32:64], AF.Exp, scale=-1.0),
                     reads=["sp1"], writes=["sp2"])
                p.op("act", lambda e: e.activation(sp_t[:, 32:64], sp_t[:, 64:96], AF.Ln, bias=1.0),
                     reads=["sp2"], writes=["sp1"])
                p.op("dve", lambda e: e.tensor_scalar_max(sp_t[:, 64:96], sp_t[:, 0:32], 0.0),
                     reads=["sp0"], writes=["sp2"])
                p.op("dve", lambda e: e.tensor_tensor(sp_t[:, 0:32], sp_t[:, 32:64],
                                                      sp_t[:, 64:96], AX.add),
                     reads=["sp1", "sp2"], writes=["sp0"])
                p.op("dve", lambda e: e.tensor_tensor(
                    gbuf[:], sp32, mid0(nea_bc[:], 4), AX.mult),
                    reads=["sp0", "neabc"], writes=[f"gbuf{ci2}" for ci2 in range(CPT)])
                for pb in range(8):
                    rk = f"raw{pb}"
                    ps, pk = bank()
                    def fcv(e, pb=pb, ps=ps):
                        r = None
                        for j in range(4):
                            r = e.matmul(ps[:], diag_b[:, pb, j, :], raw[:, pb, 1 + j:1 + j + TT],
                                         start=(j == 0), stop=(j == 3))
                        return r
                    p.op("pe", fcv, reads=["diag", rk], writes=[pk])
                    if pb < 4:
                        p.op("act", lambda e, pb=pb, ps=ps: e.activation(
                            qkc[:, pb, :], ps[:], AF.Silu), reads=[pk], writes=["qkc"])
                    else:
                        p.op("act", lambda e, pb=pb, ps=ps: e.activation(
                            vc_b[:, pb - 4, :], ps[:], AF.Silu), reads=[pk], writes=["vcb"])
                    if tt < NTPB - 1:
                        p.op("dve", lambda e, pb=pb: e.tensor_copy(
                            raw[:, pb, 0:4], raw[:, pb, TT:TT + 4]), reads=[rk], writes=[rk])
                    else:
                        p.op("dve", lambda e, pb=pb: e.memset(raw[:, pb, 0:4], 0.0),
                             reads=[rk], writes=[rk])

            PV2, PV3, PV4 = PV[2], PV[3], PV[4]

            def prep_steps(b, tt, ci):
                st = []
                A = st.append
                sx = ci % 2
                S = TS[sx]
                pt = tt % 2
                cs = ci * 128
                kG, kB = f"gbuf{ci}", f"bbuf{pt}{ci}"
                kbc, kdec, klam, klamn, kLB = (f"bcol{ci}", f"dec{ci}", f"lam{ci}",
                                               f"lamn{ci}", f"LamB{ci}")
                kkh, kqh = f"kh{ci}", f"qh{ci}"
                kDt, kscr, ksq, kMt = f"Dt{sx}", f"scr{sx}", f"sq4{sx}", f"Mt{sx}"
                krn, kbfm = f"rn{sx}", f"bfm{sx}"
                G2 = dup2(P[3][:, 0:256].rearrange("p (a c) -> p a c", a=2))
                QK2 = dup2(P[4][:, 0:256].rearrange("p (a c) -> p a c", a=2))
                P6a2 = dup2(P6[:, 0:256].rearrange("p (a c) -> p a c", a=2))
                P6a4 = P6[:, 0:512].rearrange("p (a c) -> p a c", a=4)

                def u_bcol():
                    p.op("pe", lambda e: e.matmul(
                        P[2][:, 0:4], maskUi[:], gbuf[:, ci, 4:8], start=True, stop=True),
                        reads=[kG, "masks"], writes=["PS2"])
                    p.op("dve", lambda e: e.tensor_copy(bcol4[:, ci, :], P[2][:, 0:4]),
                         reads=["PS2"], writes=[kbc])
                A(u_bcol)

                def u_bfm():
                    p.op("pe", lambda e: e.matmul(
                        P[3][0:4, 0:128], gbuf[:, ci, 4:8], maskUi[:], start=True, stop=True),
                        reads=[kG, "masks"], writes=["PS3"])
                    p.op("dve", lambda e: e.tensor_copy(S["bfm"][:], P[3][0:4, 0:128]),
                         reads=["PS3"], writes=[kbfm])
                A(u_bfm)
                brow_dst = brow4[0:1, ci, :]
                brow_v = dataclasses.replace(
                    brow_dst, ap=[brow_dst.ap[0], [128, 4], [1, 128]])
                A(lambda: p.op("sp", lambda e: e.dma_start(out=brow_v, in_=S["bfm"][:]),
                               reads=[kbfm], writes=[f"brow{ci}"], dma=f"d_brow{sx}"))

                def u_fbb():
                    def fbb(e):
                        r = None
                        for vh in range(4):
                            r = e.matmul(PV2[:, vh, :], ones_rf[:],
                                         brow4[0:1, ci, vh * 128:(vh + 1) * 128],
                                         start=True, stop=True)
                        return r
                    p.op("pe", fbb, reads=[f"brow{ci}", "masks"], writes=["PS2"])
                    p.op("dve", lambda e: e.tensor_tensor(
                        dec44[:, ci, :], PV2[:, :, 127], bcol4[:, ci, :], AX.subtract),
                        reads=["PS2", kbc], writes=[kdec])
                    p.op("act", lambda e: e.activation(
                        LamB44[:, ci, :], PV2[:, :, 127], AF.Exp), reads=["PS2"], writes=[kLB])
                    p.op("dve", lambda e: e.tensor_tensor(
                        S["sq4"][:], PV2[:], s0(bcol4[:, ci, :], 128), AX.subtract),
                        reads=["PS2", kbc], writes=[ksq])
                A(u_fbb)

                def u_dec():
                    p.op("act", lambda e: e.activation(
                        dec44[:, ci, :], dec44[:, ci, :], AF.Exp), reads=[kdec], writes=[kdec])
                    p.op("act", lambda e: e.activation(
                        lam44[:, ci, :], bcol4[:, ci, :], AF.Exp), reads=[kbc], writes=[klam])
                    p.op("dve", lambda e: e.tensor_scalar(
                        lamn44[:, ci, :], lam44[:, ci, :], -1.0, None, AX.mult),
                        reads=[klam], writes=[klamn])
                    p.op("dve", lambda e: e.tensor_scalar_min(
                        S["sq4"][:], S["sq4"][:], 0.0), reads=[ksq], writes=[ksq])
                    p.op("act", lambda e: e.activation(
                        S["Dt4"][:], S["sq4"][:], AF.Exp), reads=[ksq], writes=[kDt])
                A(u_dec)

                def u_l2a():
                    p.op("act", lambda e: e.activation(
                        S["sq4"][:], qkc[:, :, cs:cs + 128], AF.Square),
                        reads=["qkc"], writes=[ksq])
                    p.op("pe", lambda e: e.matmul(
                        P[3][0:1, 0:512], ones_cf[:],
                        S["sq4"][:].rearrange("p a c -> p (a c)"), start=True, stop=True),
                        reads=[ksq, "masks"], writes=["PS3"])
                    p.op("act", lambda e: e.activation(
                        S["rn"][0:1, :], P[3][0:1, 0:512], AF.Sqrt, bias=epsA[0:1, 0:1]),
                        reads=["PS3", "masks"], writes=[krn])
                A(u_l2a)

                def u_l2b():
                    p.op("dve", lambda e: e.reciprocal(
                        S["rn"][0:1, :], S["rn"][0:1, :]), reads=[krn], writes=[krn])
                    p.op("dve", lambda e: e.tensor_scalar(
                        S["rn"][0:1, 0:256], S["rn"][0:1, 0:256], 0.08838834764831845,
                        None, AX.mult), reads=[krn], writes=[krn])
                A(u_l2b)

                def u_l2c():
                    p.op("pe", lambda e: e.matmul(
                        P[2][:, 0:512], ones_rf[:], S["rn"][0:1, :], start=True, stop=True),
                        reads=[krn, "masks"], writes=["PS2"])
                    p.op("dve", lambda e: e.tensor_tensor(
                        qh4[:, ci], qkc[:, 0:2, cs:cs + 128], PV2[:, 0:2, :], AX.mult),
                        reads=["PS2", "qkc"], writes=[kqh])
                    p.op("dve", lambda e: e.tensor_tensor(
                        kh4[:, ci], qkc[:, 2:4, cs:cs + 128], PV2[:, 2:4, :], AX.mult),
                        reads=["PS2", "qkc"], writes=[kkh])
                A(u_l2c)

                def u_gramG():
                    def fgrG(e):
                        r = None
                        for kh in range(2):
                            r = e.matmul(P[3][:, kh * 128:(kh + 1) * 128], kh4[:, ci, kh, :],
                                         kh4[:, ci, kh, :], start=True, stop=True)
                        return r
                    p.op("pe", fgrG, reads=[kkh], writes=["PS3"])
                    p.op("dve", lambda e: e.tensor_tensor(
                        S["scr4"][:], S["Dt4"][:], G2, AX.mult),
                        reads=["PS3", kDt], writes=[kscr])
                A(u_gramG)

                def u_gramQ():
                    def fgrQ(e):
                        r = None
                        for kh in range(2):
                            r = e.matmul(P[4][:, kh * 128:(kh + 1) * 128], kh4[:, ci, kh, :],
                                         qh4[:, ci, kh, :], start=True, stop=True)
                        return r
                    p.op("pe", fgrQ, reads=[kkh, kqh], writes=["PS4"])
                    p.op("dve", lambda e: e.tensor_tensor(
                        S["sq4"][:], S["Dt4"][:], QK2, AX.mult),
                        reads=["PS4", kDt], writes=[ksq])
                A(u_gramQ)

                def u_mat():
                    p.op("pool", lambda e: e.tensor_tensor(
                        S["scr4"][:], S["scr4"][:], mid0(maskUs[:], 4), AX.mult),
                        reads=[kscr, "masks"], writes=[kscr])
                    p.op("pool", lambda e: e.tensor_tensor(
                        S["Mt4"][:], S["scr4"][:], s0(bbuf[pt][:, ci, 0:4], 128), AX.mult),
                        reads=[kscr, kB], writes=[kMt])
                    p.op("pool", lambda e: e.tensor_tensor(
                        At44[:, ci], S["sq4"][:], mid0(maskUi[:], 4), AX.mult),
                        reads=[ksq, "masks"], writes=[f"At{ci}"])
                A(u_mat)

                def u_kT():
                    def fkT(e):
                        r = None
                        for kh in range(2):
                            r = e.transpose(P6[:, kh * 128:(kh + 1) * 128], kh4[:, ci, kh, :],
                                            identb[:])
                        return r
                    p.op("pe", fkT, reads=[kkh, "masks"], writes=["PSBa"])
                    p.op("dve", lambda e: e.tensor_tensor(
                        Kt44[:, ci], P6a2, s0(dec44[:, ci, :], 128), AX.mult),
                        reads=["PSBa", kdec], writes=[f"Kt{ci}"])
                A(u_kT)

                def u_vT():
                    def fvT(e):
                        r = None
                        for vh in range(4):
                            r = e.transpose(P6[:, vh * 128:(vh + 1) * 128],
                                            vc_b[:, vh, cs:cs + 128], identb[:])
                        return r
                    p.op("pe", fvT, reads=["vcb", "masks"], writes=["PSBa"])
                    p.op("dve", lambda e: e.tensor_copy(V44[:, ci], P6a4),
                         reads=["PSBa"], writes=[f"V4{ci}"])
                A(u_vT)

                def u_zT():
                    def fzT(e):
                        r = None
                        for vh in range(4):
                            r = e.transpose(P6[:, vh * 128:(vh + 1) * 128],
                                            z_b[:, vh, cs:cs + 128], identb[:])
                        return r
                    p.op("pe", fzT, reads=["zb", "masks"], writes=["PSBa"])
                    p.op("dve", lambda e: e.tensor_copy(z44[:, ci], P6a4),
                         reads=["PSBa"], writes=[f"z4{ci}"])
                A(u_zT)

                def u_V0():
                    p.op("dve", lambda e: e.tensor_tensor(
                        S["Vp"][0][:], mid0(identb[:], 4), S["Mt4"][:], AX.subtract),
                        reads=["masks", kMt], writes=[f"Vp{sx}0"])
                    def fMT(e):
                        r = None
                        for vh in range(4):
                            r = e.transpose(P6[:, vh * 128:(vh + 1) * 128], S["Mt4"][:, vh, :],
                                            identb[:])
                        return r
                    p.op("pe", fMT, reads=[kMt, "masks"], writes=["PSBa"])
                    p.op("act", lambda e: e.copy(S["Nb"][0][:], P6a4),
                         reads=["PSBa"], writes=[f"Nb{sx}0"])
                A(u_V0)

                for j in range(1, 7):
                    tb_p, tkp = ((S["Mt4"], kMt) if j == 1
                                 else (S["Tb"][(j - 1) % 2], f"Tb{sx}{(j - 1) % 2}"))
                    nb_p, nkp = S["Nb"][(j - 1) % 2], f"Nb{sx}{(j - 1) % 2}"
                    vp_p, vkp = S["Vp"][(j - 1) % 2], f"Vp{sx}{(j - 1) % 2}"
                    ji = j % 2
                    if j < 6:
                        def u_sqT(nb_p=nb_p, tb_p=tb_p, nkp=nkp, tkp=tkp, ji=ji):
                            def fsqT(e):
                                r = None
                                for vh in range(4):
                                    r = e.matmul(PV2[:, vh, :], nb_p[:, vh, :], tb_p[:, vh, :],
                                                 start=True, stop=True)
                                return r
                            p.op("pe", fsqT, reads=[nkp, tkp], writes=["PS2"])
                            p.op("dve", lambda e: e.tensor_copy(S["Tb"][ji][:], PV2[:]),
                                 reads=["PS2"], writes=[f"Tb{sx}{ji}"])
                        A(u_sqT)

                    def u_sqN(nb_p=nb_p, tb_p=tb_p, nkp=nkp, tkp=tkp, ji=ji):
                        def fsqN(e):
                            r = None
                            for vh in range(4):
                                r = e.matmul(PV3[:, vh, :], tb_p[:, vh, :], nb_p[:, vh, :],
                                             start=True, stop=True)
                            return r
                        p.op("pe", fsqN, reads=[nkp, tkp], writes=["PS3"])
                        p.op("act", lambda e: e.copy(S["Nb"][ji][:], PV3[:]),
                             reads=["PS3"], writes=[f"Nb{sx}{ji}"])
                    A(u_sqN)

                    def u_Vm(vp_p=vp_p, vkp=vkp, ji=ji, j=j):
                        def fVm(e):
                            r = None
                            for vh in range(4):
                                r = e.matmul(PV4[:, vh, :], S["Nb"][ji][:, vh, :],
                                             vp_p[:, vh, :], start=True, stop=True)
                            return r
                        p.op("pe", fVm, reads=[f"Nb{sx}{ji}", vkp], writes=["PS4"])
                        if j < 6:
                            p.op("dve", lambda e: e.tensor_tensor(
                                S["Vp"][ji][:], vp_p[:], PV4[:], AX.add),
                                reads=["PS4", vkp], writes=[f"Vp{sx}{ji}"])
                        else:
                            p.op("dve", lambda e: e.tensor_tensor(
                                W4[:, ci], vp_p[:], PV4[:], AX.add),
                                reads=["PS4", vkp], writes=[f"W4{ci}"])
                    A(u_Vm)
                return st

            PV0, PV1 = PV[0], PV[1]

            def chain_emit(b, tt, ci):
                pt = tt % 2
                ch = tt * CPT + ci
                op = ch % 2
                first = (ch == 0)
                if not first:
                    def fks(e):
                        r = None
                        for vh in range(4):
                            r = e.matmul(PV0[:, vh, :], kh4[:, ci, vh // 2, :], S_b[:, vh, :],
                                         start=True, stop=True)
                        return r
                    p.op("pe", fks, reads=[f"kh{ci}", "Sb"], writes=["PS0"])
                    def fqs(e):
                        r = None
                        for vh in range(4):
                            r = e.matmul(PV1[:, vh, :], qh4[:, ci, vh // 2, :], S_b[:, vh, :],
                                         start=True, stop=True)
                        return r
                    p.op("pe", fqs, reads=[f"qh{ci}", "Sb"], writes=["PS1"])
                    p.op("act", lambda e: e.copy(QS4[:, ci], PV1[:]),
                         reads=["PS1"], writes=[f"QS{ci}"])
                    p.op("dve", lambda e: e.tensor_tensor(
                        Y0f[:], PV0[:], s0(lamn44[:, ci, :], 128), AX.mult),
                        reads=["PS0", f"lamn{ci}"], writes=["Y0f"])
                    p.op("dve", lambda e: e.tensor_tensor(
                        Y0b[:], Y0f[:], V44[:, ci], AX.add),
                        reads=["Y0f", f"V4{ci}"], writes=["Y0b"])
                else:
                    cop("dve", Y0b[:], V44[:, ci], [f"V4{ci}"], ["Y0b"])
                def fyW(e):
                    r = None
                    for vh in range(4):
                        r = e.matmul(PV0[:, vh, :], W4[:, ci, vh, :], Y0b[:, vh, :],
                                     start=True, stop=True)
                    return r
                p.op("pe", fyW, reads=[f"W4{ci}", "Y0b"], writes=["PS0"])
                p.op("dve", lambda e: e.tensor_tensor(
                    U4[:, ci], PV0[:], s0(bbuf[pt][:, ci, 0:4], 128), AX.mult),
                    reads=["PS0", f"bbuf{pt}{ci}"], writes=[f"U4{ci}"])
                def fsd(e):
                    r = None
                    for vh in range(4):
                        r = e.matmul(PV0[:, vh, :], Kt44[:, ci, vh, :], U4[:, ci, vh, :],
                                     start=True, stop=True)
                    return r
                p.op("pe", fsd, reads=[f"Kt{ci}", f"U4{ci}"], writes=["PS0"])
                if first:
                    p.op("dve", lambda e: e.tensor_copy(S_f[:], PV0[:]),
                         reads=["PS0"], writes=["Sf"])
                else:
                    p.op("dve", lambda e: e.tensor_tensor(
                        S_f[:], S_f[:], s0(LamB44[:, ci, :], 128), AX.mult),
                        reads=["Sf", f"LamB{ci}"], writes=["Sf"])
                    p.op("dve", lambda e: e.tensor_tensor(S_f[:], S_f[:], PV0[:], AX.add),
                         reads=["Sf", "PS0"], writes=["Sf"])
                cop("act", S_b[:], S_f[:], ["Sf"], ["Sb"])
                def fo(e):
                    r = None
                    for vh in range(4):
                        r = e.matmul(PV1[:, vh, :], At44[:, ci, vh, :], U4[:, ci, vh, :],
                                     start=True, stop=True)
                    return r
                p.op("pe", fo, reads=[f"At{ci}", f"U4{ci}", f"QS{ci}"], writes=["PS1"])
                ko = f"o4{op}"
                if first:
                    p.op("dve", lambda e: e.tensor_copy(o4[op][:], PV1[:]),
                         reads=["PS1"], writes=[ko])
                else:
                    p.op("dve", lambda e: e.tensor_tensor(
                        o4[op][:], QS4[:, ci], s0(lam44[:, ci, :], 128), AX.mult),
                        reads=[f"QS{ci}", f"lam{ci}"], writes=[ko])
                    p.op("dve", lambda e: e.tensor_tensor(o4[op][:], o4[op][:], PV1[:], AX.add),
                         reads=[ko, "PS1"], writes=[ko])
                p.op("dve", lambda e: e.tensor_tensor(o4[op][:], o4[op][:], z44[:, ci], AX.mult),
                     reads=[ko, f"z4{ci}"], writes=[ko])
                p.op("act", lambda e: e.activation(scro[op][:], o4[op][:], AF.Square),
                     reads=[ko], writes=[f"scro{op}"])
                r6 = dataclasses.replace(ssum4[op][:], ap=list(ssum4[op][:].ap) + [[1, 1]])
                p.op("dve", lambda e: e.reduce_sum(r6, scro[op][:], AL.X),
                     reads=[f"scro{op}"], writes=[f"ssum{op}"])
                p.op("dve", lambda e: e.tensor_scalar(rstd4[op][:], ssum4[op][:], 1.0 / 128,
                                                      1e-6, AX.mult, AX.add),
                     reads=[f"ssum{op}"], writes=[f"rstd{op}"])
                p.op("act", lambda e: e.activation(rstd4[op][:], rstd4[op][:], AF.Sqrt),
                     reads=[f"rstd{op}"], writes=[f"rstd{op}"])
                p.op("dve", lambda e: e.reciprocal(rstd4[op][:], rstd4[op][:]),
                     reads=[f"rstd{op}"], writes=[f"rstd{op}"])
                p.op("dve", lambda e: e.tensor_tensor(
                    o4[op][:], o4[op][:], s0(rstd4[op][:], 128), AX.mult),
                    reads=[ko, f"rstd{op}"], writes=[ko])
                p.op("dve", lambda e: e.tensor_tensor(
                    ogb[op][:], o4[op][:], mid0(nw_bc[:], 4), AX.mult),
                    reads=[ko, "nwbc"], writes=[f"ogb{op}"])
                def ftro(e):
                    r = None
                    for vh in range(4):
                        r = e.transpose(P6[:, 512 + vh * 128:512 + (vh + 1) * 128],
                                        ogb[op][:, vh, :], identb[:])
                    return r
                p.op("pe", ftro, reads=[f"ogb{op}", "masks"], writes=["PSBb"])
                p.op("dve", lambda e: e.tensor_copy(
                    ogt[op][:], P6[:, 512:1024].rearrange("p (a c) -> p a c", a=4)),
                    reads=["PSBb"], writes=[f"ogt{op}"])
                dst = og_my[b].ap().rearrange("(v d) t -> d v t", v=4)[:, :, ch * 128:(ch + 1) * 128]
                p.op("pool", lambda e, dst=dst: e.dma_start(out=dst, in_=ogt[op][:]),
                     reads=[f"ogt{op}"], writes=[f"ogd{b}_{ch}"], dma=f"d_og{b}_{op}")

            for b in range(B):
                for tt in range(NTPB):
                    proj_ttile(b, tt)
                    streams = [prep_steps(b, tt, ci) for ci in range(CPT)]
                    for pair in ((0, 1), (2, 3)):
                        ix = {c2: 0 for c2 in pair}
                        alive = True
                        while alive:
                            alive = False
                            for c2 in pair:
                                if ix[c2] < len(streams[c2]):
                                    streams[c2][ix[c2]]()
                                    ix[c2] += 1
                                    alive = True
                    for ci in range(CPT):
                        chain_emit(b, tt, ci)
                        drain_cast(5)
                def fag(e, b=b):
                    return e.collective_compute(
                        "AllGather", AX.bypass, replica_groups=[list(range(NC8))],
                        ins=[og_my[b].ap().opt()], outs=[og_ag[b].ap().opt()])
                p.op("pool", fag, reads=[f"ogd{b}_{ch}" for ch in range(NCH)],
                     writes=[f"ogag{b}"], cc=True)
            drain_cast(len(cast_q) + 1)

        p.barrier()
        with ExitStack() as p2:
            sb2 = lambda nm, shape, dty=F32: p2.enter_context(nc.sbuf_tensor(nm, shape, dty))
            x1f = sb2("x1f", [128, KPB, 2048], BF16)
            yf = sb2("yf", [128, DPB, 512], F32)
            vb2 = sb2("vb2", [128, 512], BF16)
            wstb = sb2("wstb", [128, 32, 128], BF16)
            cmb_t = sb2("cmb_t", [128, 8], F32)
            cmb_b = sb2("cmb_b", [128, 8], BF16)
            cmbfm2 = sb2("cmbfm2", [8, 2048], BF16)
            sel8 = sb2("sel8", [8, 8 * 128], BF16)
            m8 = sb2("m8", [128, 2], F32)

            def fsel(e):
                r = None
                for ei in range(8):
                    r = e.tensor_scalar(sel8[:, ei * 128:(ei + 1) * 128],
                                        s0(pramp[0:8, 0:1], 128), float(ei), None, AX.is_equal)
                return r
            p.op("dve", fsel, reads=["pramp"], writes=["sel8"])

            def ln_fm(vget, vkeys, emit_out):
                for cpb in range(DPB):
                    cop("dve", vb2[:], vget(cpb), vkeys, ["vb2"])
                    p.op("pe", lambda e, cpb=cpb: e.matmul(
                        P[5][0:1, 0:512], ones_cb[:], vb2[:],
                        start=(cpb == 0), stop=(cpb == DPB - 1)), reads=["vb2"], writes=["PS5"])
                    p.op("act", lambda e: e.activation(vb2[:], vb2[:], AF.Square),
                         reads=["vb2"], writes=["vb2"])
                    p.op("pe", lambda e, cpb=cpb: e.matmul(
                        P[4][0:1, 0:512], ones_cb[:], vb2[:],
                        start=(cpb == 0), stop=(cpb == DPB - 1)), reads=["vb2"], writes=["PS4"])
                p.op("dve", lambda e: e.tensor_scalar(stat_m[0:1, :], P[5][0:1, 0:512],
                                                      1.0 / 2048, None, AX.mult),
                     reads=["PS5"], writes=["statm"])
                p.op("dve", lambda e: e.tensor_scalar(stat_q[0:1, :], P[4][0:1, 0:512],
                                                      1.0 / 2048, None, AX.mult),
                     reads=["PS4"], writes=["statq"])
                p.op("dve", lambda e: e.tensor_tensor(stat_v[0:1, :], stat_m[0:1, :],
                                                      stat_m[0:1, :], AX.mult),
                     reads=["statm"], writes=["statv"])
                p.op("dve", lambda e: e.tensor_tensor(stat_v[0:1, :], stat_q[0:1, :],
                                                      stat_v[0:1, :], AX.subtract),
                     reads=["statq", "statv"], writes=["statv"])
                p.op("act", lambda e: e.activation(stat_v[0:1, :], stat_v[0:1, :],
                                                   AF.Sqrt, bias=epsB[0:1, 0:1]),
                     reads=["statv", "masks"], writes=["statv"])
                p.op("dve", lambda e: e.reciprocal(stat_v[0:1, :], stat_v[0:1, :]),
                     reads=["statv"], writes=["statv"])
                p.op("pe", lambda e: e.matmul(P[5][:, 0:512], ones_rf[:], stat_m[:],
                                              start=True, stop=True),
                     reads=["statm"], writes=["PS5"])
                p.op("pe", lambda e: e.matmul(P[4][:, 0:512], ones_rf[:], stat_v[:],
                                              start=True, stop=True),
                     reads=["statv"], writes=["PS4"])
                cop("dve", r_s[:], P[4][:, 0:512], ["PS4"], ["rs"])
                for cpb in range(DPB):
                    p.op("dve", lambda e, cpb=cpb: e.tensor_tensor(
                        scr2[:], vget(cpb), P[5][:, 0:512], AX.subtract),
                        reads=list(vkeys) + ["PS5"], writes=["scr2"])
                    emit_out(cpb)

            p2a = p2.enter_context(ExitStack())
            ogsb = p2a.enter_context(nc.sbuf_tensor("ogsb", [128, 32, 512], BF16))
            for nb in range(NB):
                def fldog(e, sem, inc, nb=nb):
                    for arm in e.switch_core_id(n=128):
                        qv = int(arm) % 8
                        bq, soff = qv // 2, (qv % 2) * 2048
                        col = soff + nb * 512
                        e.dma_start(out=ogsb[:], in_=og_ag[bq].ap().rearrange(
                            "(k d) t -> d k t", d=128)[:, :, col:col + 512]).then_inc(sem, inc)
                p.op("sp", fldog, reads=[f"ogag{bq}" for bq in range(B)],
                     writes=["ogsb"], dma="d_ogsb", selfinc=True)
                def fldxr(e, sem, inc, nb=nb):
                    for arm in e.switch_core_id(n=128):
                        qv = int(arm) % 8
                        xcol = (qv // 2) * T + (qv % 2) * 2048 + nb * 512
                        e.dma_start(out=yf[:], in_=xT.rearrange("(k d) t -> d k t", d=128)
                                    [:, :, xcol:xcol + 512]).then_inc(sem, inc)
                p.op("sp", fldxr, reads=["yfx"], writes=["yfx"], dma="d_yfres", selfinc=True)
                for cpb in range(DPB):
                    p.op("sp", lambda e, cpb=cpb: e.dma_start(
                        out=wstb[:], in_=wo_bf[cpb].rearrange("k d c -> d k c")),
                        reads=[f"wobf{cpb}"], writes=["wstb"], dma="d_wstb")
                    ps, pk = P[cpb % 2], PK[cpb % 2]
                    def fwo(e, ps=ps):
                        r = None
                        for k in range(32):
                            r = e.matmul(ps[:], wstb[:, k, :], ogsb[:, k, :],
                                         start=(k == 0), stop=(k == 31))
                        return r
                    p.op("pe", fwo, reads=["wstb", "ogsb"], writes=[pk])
                    p.op("dve", lambda e, cpb=cpb, ps=ps: e.tensor_tensor(
                        yf[:, cpb, :], yf[:, cpb, :], ps[:], AX.add),
                        reads=[pk, "yfx"], writes=["yfx"])

                def emit_x1(cpb, nb=nb):
                    p.op("dve", lambda e, cpb=cpb: e.tensor_tensor(
                        x1f[:, cpb, nb * 512:(nb + 1) * 512], scr2[:], r_s[:], AX.mult),
                        reads=["scr2", "rs"], writes=["x1f"])
                ln_fm(lambda cpb: yf[:, cpb, :], ["yfx"], emit_x1)

            for ttk in range(16):
                def flg(e, ttk=ttk):
                    r = None
                    for k in range(KPB):
                        r = e.matmul(P[5][:, 0:8], x1f[:, k, ttk * 128:(ttk + 1) * 128],
                                     gatew_b[:, k, :], start=(k == 0), stop=(k == KPB - 1))
                    return r
                p.op("pe", flg, reads=["x1f", "gatewb"], writes=["PS5"])
                p.op("dve", lambda e: e.reduce_max(m8[:, 0:1], P[5][:, 0:8], AL.X),
                     reads=["PS5"], writes=["m8"])
                p.op("dve", lambda e: e.tensor_scalar(m8[:, 0:1], m8[:, 0:1], -1.0, None, AX.mult),
                     reads=["m8"], writes=["m8"])
                p.op("act", lambda e: e.activation(cmb_t[:], P[5][:, 0:8], AF.Exp,
                                                   bias=m8[:, 0:1]),
                     reads=["PS5", "m8"], writes=["cmbt"])
                p.op("dve", lambda e: e.reduce_sum(m8[:, 1:2], cmb_t[:], AL.X),
                     reads=["cmbt"], writes=["m8b"])
                p.op("dve", lambda e: e.reciprocal(m8[:, 1:2], m8[:, 1:2]),
                     reads=["m8b"], writes=["m8b"])
                p.op("dve", lambda e: e.tensor_scalar(cmb_t[:], cmb_t[:], m8[:, 1:2], None, AX.mult),
                     reads=["cmbt", "m8b"], writes=["cmbt"])
                p.op("dve", lambda e: e.reduce_max(m8[:, 0:1], cmb_t[:], AL.X),
                     reads=["cmbt"], writes=["m8"])
                p.op("dve", lambda e: e.tensor_scalar(scr2[:, 0:8], cmb_t[:], m8[:, 0:1],
                                                      None, AX.is_ge),
                     reads=["cmbt", "m8"], writes=["scr2"])
                p.op("dve", lambda e: e.scalar_tensor_tensor(scr2[:, 0:8], scr2[:, 0:8], -1e30,
                                                             cmb_t[:], AX.mult, AX.add),
                     reads=["scr2", "cmbt"], writes=["scr2"])
                p.op("dve", lambda e: e.reduce_max(m8[:, 1:2], scr2[:, 0:8], AL.X),
                     reads=["scr2"], writes=["m8b"])
                p.op("dve", lambda e: e.tensor_scalar(scr2[:, 0:8], cmb_t[:], m8[:, 1:2],
                                                      None, AX.is_ge),
                     reads=["cmbt", "m8b"], writes=["scr2"])
                p.op("dve", lambda e: e.tensor_tensor(cmb_b[:], cmb_t[:], scr2[:, 0:8], AX.mult),
                     reads=["cmbt", "scr2"], writes=["cmbb"])
                p.op("pe", lambda e, ttk=ttk: e.transpose(P6[0:8, 0:128],
                                                          cmb_b[:], identb[:]),
                     reads=["cmbb", "masks"], writes=["PS6"])
                cop("dve", cmbfm2[:, ttk * 128:(ttk + 1) * 128], P6[0:8, 0:128],
                    ["PS6"], ["cmbfm"])
            p2a.close()

            p2b = p2.enter_context(ExitStack())
            sb2b = lambda nm, shape, dty=F32: p2b.enter_context(nc.sbuf_tensor(nm, shape, dty))
            cb_e = sb2b("cb_e", [128, 2048], BF16)
            sgb = sb2b("sgb", [128, NB, 512], BF16)
            ggb = sb2b("ggb", [128, NB, 512], BF16)
            wsg = [sb2b(f"wsg{i}", [128, 16, 128], F32) for i in range(2)]
            wsb = [sb2b(f"wsb{i}", [128, 16, 128], BF16) for i in range(2)]

            ucnt = [0]
            for gi in range(NGI):
                is_sh = gi >= E * HPB_R
                e_i = gi // HPB_R if not is_sh else None
                hpb = gi % HPB_R if not is_sh else gi - E * HPB_R
                if (not is_sh) and hpb == 0:
                    for nb2 in range(NB):
                        p.op("pe", lambda e, e_i=e_i, nb2=nb2: e.matmul(
                            P[4][:, 0:512], sel8[:, e_i * 128:(e_i + 1) * 128],
                            cmbfm2[:, nb2 * 512:(nb2 + 1) * 512], start=True, stop=True),
                            reads=["cmbfm", "sel8", "cbe"], writes=["PS4"])
                        cop("dve", cb_e[:, nb2 * 512:(nb2 + 1) * 512], P[4][:, 0:512],
                            ["PS4"], ["cbe"] + (["ogsb", "wstb"] if gi == 0 else []))
                wsrc_g = wg[e_i, hpb] if not is_sh else swg[hpb]
                wsrc_u = wu[e_i, hpb] if not is_sh else swu[hpb]
                for phase, wsrc in (("g", wsrc_g), ("u", wsrc_u)):
                    par = ucnt[0] % 2
                    guard = ["ogsb", "wstb"] if ucnt[0] < 2 else []
                    ucnt[0] += 1
                    wsgX, wsgK = wsg[par], f"wsg{par}"
                    wsbX, wsbK = wsb[par], f"wsb{par}"
                    p.op("sp", lambda e, wsrc=wsrc, wsgX=wsgX: e.dma_start(
                        out=wsgX[:], in_=wsrc.rearrange("k d c -> d k c")),
                        writes=[wsgK] + guard, dma=f"d_wstg{par}")
                    cop(ev(), wsbX[:], wsgX[:], [wsgK], [wsbK] + guard)
                    def fgu(e, wsbX=wsbX):
                        r = None
                        for k in range(KPB):
                            for nb2 in range(NB):
                                r = e.matmul(P[nb2][:],
                                             wsbX[:, k, :], x1f[:, k, nb2 * 512:(nb2 + 1) * 512],
                                             start=(k == 0), stop=(k == KPB - 1))
                        return r
                    p.op("pe", fgu, reads=[wsbK, "x1f"],
                         writes=["PS0", "PS1", "PS2", "PS3"])
                    if phase == "g":
                        for nb2 in range(NB):
                            p.op("act", lambda e, nb2=nb2: e.activation(
                                sgb[:, nb2, :], P[nb2][:], AF.Silu),
                                reads=[PK[nb2]], writes=["sgb"] + guard)
                    else:
                        for nb2 in range(NB):
                            p.op("dve", lambda e, nb2=nb2: e.tensor_tensor(
                                ggb[:, nb2, :], P[nb2][:], sgb[:, nb2, :], AX.mult),
                                reads=[PK[nb2], "sgb"], writes=["ggb"] + guard)
                        if not is_sh:
                            p.op("pool", lambda e: e.tensor_tensor(
                                ggb[:].rearrange("p a c -> p (a c)"),
                                ggb[:].rearrange("p a c -> p (a c)"), cb_e[:], AX.mult),
                                reads=["ggb", "cbe"], writes=["ggb"])
                p.op("pool", lambda e, gi=gi: e.dma_start(
                    out=g_d[gi], in_=ggb[:].rearrange("p a c -> p (a c)")),
                    reads=["ggb"], writes=[f"gd{gi}"], dma="d_gd")

            p2b.close()
            p2c = p2.enter_context(ExitStack())
            sb2c = lambda nm, shape, dty=F32: p2c.enter_context(nc.sbuf_tensor(nm, shape, dty))
            g_eb = [sb2c(f"g_eb{i}", [128, 11, 512], BF16) for i in range(2)]
            wd_eb = [sb2c(f"wd_eb{i}", [128, 88, 128], BF16) for i in range(2)]
            gA = ["wsg0", "wsg1", "wsb0", "wsb1", "cbe", "sgb", "ggb"]
            for nb in range(NB):
                for eg in range(NEG):
                    gpar = (nb * NEG + eg) % 2
                    gX, gK = g_eb[gpar], f"geb{gpar}"
                    guard = gA if (nb, eg) in ((0, 0), (0, 1)) else []
                    p.op("sp", lambda e, eg=eg, nb=nb, gX=gX: e.dma_start(
                        out=gX[:], in_=g_d[eg * 11:(eg + 1) * 11, :, nb * 512:(nb + 1) * 512]
                        .rearrange("a d c -> d a c")),
                        reads=[f"gd{gg}" for gg in range(eg * 11, (eg + 1) * 11)],
                        writes=[gK] + guard, dma=f"d_geb{gpar}")
                    for hf in range(2):
                        wpar = (nb * NEG * 2 + eg * 2 + hf) % 2
                        wX, wK = wd_eb[wpar], f"wdeb{wpar}"
                        r0 = (eg * 2 + hf) * 88
                        wdkeys = [f"wdbf{(eg * 2 + hf) * 88 + gl * 8}" for gl in range(11)]
                        p.op("sp", lambda e, r0=r0, wX=wX: e.dma_start(
                            out=wX[:], in_=wd_bf[r0:r0 + 88].rearrange("r d c -> d r c")),
                            reads=wdkeys, writes=[wK] + guard, dma=f"d_wdeb{wpar}")
                        for dj in range(8):
                            dpb = hf * 8 + dj
                            ps, pk = P[4 + dpb % 2], PK[4 + dpb % 2]
                            def fdn(e, dj=dj, ps=ps, wX=wX, gX=gX):
                                r = None
                                for gl in range(11):
                                    r = e.matmul(ps[:], wX[:, gl * 8 + dj, :], gX[:, gl, :],
                                                 start=(gl == 0), stop=(gl == 10))
                                return r
                            p.op("pe", fdn, reads=[wK, gK], writes=[pk])
                            if eg == 0:
                                p.op("dve", lambda e, dpb=dpb, ps=ps: e.tensor_copy(
                                    yf[:, dpb, :], ps[:]), reads=[pk], writes=["yfx"])
                            else:
                                p.op("dve", lambda e, dpb=dpb, ps=ps: e.tensor_tensor(
                                    yf[:, dpb, :], yf[:, dpb, :], ps[:], AX.add),
                                    reads=[pk, "yfx"], writes=["yfx"])
                for dpb in range(DPB):
                    p.op("dve", lambda e, dpb=dpb, nb=nb: e.tensor_tensor(
                        yf[:, dpb, :], yf[:, dpb, :],
                        x1f[:, dpb, nb * 512:(nb + 1) * 512], AX.add),
                        reads=["yfx", "x1f"], writes=["yfx"])

                def emit_out(cpb, nb=nb):
                    p.op("dve", lambda e, cpb=cpb: e.tensor_tensor(
                        yf[:, cpb, :], scr2[:], r_s[:], AX.mult),
                        reads=["scr2", "rs"], writes=["yfx"])
                    p.op("pool", lambda e, cpb=cpb, nb=nb: e.dma_start(
                        out=out_fm[cpb * 128:(cpb + 1) * 128, nb * 512:(nb + 1) * 512],
                        in_=yf[:, cpb, :]), reads=["yfx"], writes=[f"outd{nb}_{cpb}"],
                        dma="d_out")
                ln_fm(lambda cpb: yf[:, cpb, :], ["yfx"], emit_out)

            p.barrier()

        p.emit(blk)
    return nc


_NC_CACHE = {}


def _get_nc():
    if "nc" not in _NC_CACHE:
        _NC_CACHE["nc"] = build_nc()
    return _NC_CACHE["nc"]


def _prep_host(inputs):
    f32 = np.float32
    x = np.asarray(inputs["L0_attn_hidden_states"], f32)
    xT = np.ascontiguousarray(x.reshape(B * T, D).T)
    Wq = np.asarray(inputs["L0_attn_W_qkvz"], f32)
    Wba = np.asarray(inputs["L0_attn_W_ba"], f32)
    cw = np.asarray(inputs["L0_attn_conv_w"], f32)
    dtb = np.asarray(inputs["L0_attn_dt_bias"], f32)
    alog = np.asarray(inputs["L0_attn_A_log"], f32)
    nw = np.asarray(inputs["L0_attn_norm_weight"], f32)
    Wo = np.asarray(inputs["L0_attn_W_out"], f32)
    gw = np.asarray(inputs["L0_ffn_gate_w"], f32)
    wg_ = np.asarray(inputs["L0_ffn_w_gate"], f32)
    wu_ = np.asarray(inputs["L0_ffn_w_up"], f32)
    wd_ = np.asarray(inputs["L0_ffn_w_down"], f32)
    swg_ = np.asarray(inputs["L0_ffn_sw_gate"], f32)
    swu_ = np.asarray(inputs["L0_ffn_sw_up"], f32)
    swd_ = np.asarray(inputs["L0_ffn_sw_down"], f32)

    ident_c = np.eye(128, dtype=ml_dtypes.bfloat16)
    rr_, cc_ = np.arange(128)[:, None], np.arange(128)[None, :]
    masku_c = (cc_ >= rr_).astype(f32)
    maskus_c = (cc_ > rr_).astype(f32)
    pramp_c = np.arange(128, dtype=f32)
    wout_t = np.ascontiguousarray(
        Wo.reshape(32, 128, DPB, 128).transpose(2, 0, 1, 3))
    gatew_t = np.ascontiguousarray(gw.reshape(KPB, 128, 8))
    wg_t = np.ascontiguousarray(
        wg_.reshape(E, KPB, 128, HPB_R, 128).transpose(0, 3, 1, 2, 4))
    wu_t = np.ascontiguousarray(
        wu_.reshape(E, KPB, 128, HPB_R, 128).transpose(0, 3, 1, 2, 4))
    wd_t = np.ascontiguousarray(
        wd_.reshape(E, HPB_R, 128, DPB, 128).transpose(0, 1, 3, 2, 4)
        .reshape(E * HPB_R, DPB, 128, 128))
    swg_t = np.ascontiguousarray(swg_.reshape(KPB, 128, HPB_S, 128).transpose(2, 0, 1, 3))
    swu_t = np.ascontiguousarray(swu_.reshape(KPB, 128, HPB_S, 128).transpose(2, 0, 1, 3))
    swd_t = np.ascontiguousarray(swd_.reshape(HPB_S, 128, DPB, 128).transpose(0, 2, 1, 3))

    in_maps = []
    for c in range(NC8):
        khs = [2 * c, 2 * c + 1]
        vhs = [4 * c + i for i in range(4)]
        cols = []
        for kh in khs:
            cols.append(np.arange(kh * 768, kh * 768 + 128))
        for kh in khs:
            cols.append(np.arange(kh * 768 + 128, kh * 768 + 256))
        for vh in vhs:
            cols.append(np.arange((vh // 2) * 768 + 256 + (vh % 2) * 128,
                                  (vh // 2) * 768 + 256 + (vh % 2) * 128 + 128))
        for vh in vhs:
            cols.append(np.arange((vh // 2) * 768 + 512 + (vh % 2) * 128,
                                  (vh // 2) * 768 + 512 + (vh % 2) * 128 + 128))
        wq_c = np.ascontiguousarray(
            np.stack([Wq[:, ci] for ci in cols], 0).reshape(12, KPB, 128, 128))
        bacols = [4 * (vh // 2) + (vh % 2) for vh in vhs] + \
                 [4 * (vh // 2) + 2 + (vh % 2) for vh in vhs]
        wba_c = np.ascontiguousarray(Wba[:, bacols].reshape(KPB, 128, 8))
        convrows = np.concatenate(
            [np.arange(kh * 128, (kh + 1) * 128) for kh in khs] +
            [np.arange(2048 + kh * 128, 2048 + (kh + 1) * 128) for kh in khs] +
            [np.arange(4096 + vh * 128, 4096 + (vh + 1) * 128) for vh in vhs])
        convw_c = np.ascontiguousarray(cw[convrows].reshape(8, 128, 4))
        dtb8 = np.concatenate([np.zeros(4, f32), dtb[vhs]])
        alog8 = np.concatenate([np.zeros(4, f32), alog[vhs]])
        in_maps.append({
            "xT": xT, "wqkvz": wq_c, "wba": wba_c, "convw": convw_c,
            "dtb8": dtb8, "alog8": alog8, "nwin": nw, "wout": wout_t,
            "gatew": gatew_t, "wg": wg_t, "wu": wu_t, "wdn": wd_t,
            "swg": swg_t, "swu": swu_t, "swd": swd_t,
            "ident_in": ident_c, "masku_in": masku_c, "maskus_in": maskus_c,
            "pramp_in": pramp_c,
        })
    return in_maps


def kernel(**inputs):
    nc = _get_nc()
    in_maps = _prep_host(inputs)
    res = run_bass_kernel_spmd(nc, in_maps, core_ids=list(range(NC8)))
    out = np.zeros((B * T, D), np.float32)
    for c in range(NC8):
        q = int(round(float(res.results[c]["myq"][0, 0])))
        out[q * 2048:(q + 1) * 2048, :] = res.results[c]["out_fm"].T
    return out.reshape(B, T, D)
